# version 38
# baseline (speedup 1.0000x reference)
import numpy as np
import ml_dtypes
from contextlib import ExitStack

import concourse.bass as bass
import concourse.mybir as mybir
from concourse.bass_utils import run_bass_kernel_spmd

F32 = mybir.dt.float32
BF16 = mybir.dt.bfloat16
AF = mybir.ActivationFunctionType
ALU = mybir.AluOpType

DEBUG = False
SELF_EDGES = True
D = 1024
DC = 8
NST = 16
HID = 4096
T = 2048
KPRE = 24
TW = T + KPRE
CH = 16
NCHAIN = T // CH
STEPS = KPRE + CH
EPS = 1e-5


class Sem:
    def __init__(self, stack, nc, name):
        self.h = stack.enter_context(nc.semaphore(name))
        self.n = 0


class Plan:

    def __init__(self, own=None):
        self.items = []
        self.own = own or {}

    def op(self, eng, fn, wait=(), inc=None):
        own = self.own.get(eng)
        selfwait = None
        if own is not None and (inc is None or inc[0] is own):
            if SELF_EDGES:
                selfwait = (own, own.n)
            inc = (own, 1)
        self.items.append((eng, fn, list(wait), inc, selfwait))
        if inc is not None:
            sem, k = inc
            sem.n += k
            return sem.n
        return None

    def run(self, engine, name):
        for eng, fn, waits, inc, selfwait in self.items:
            if eng != name:
                continue
            if selfwait is not None and selfwait[1] > 0:
                engine.wait_ge(selfwait[0].h, selfwait[1])
            for sem, val in waits:
                engine.wait_ge(sem.h, val)
            inst = fn(engine)
            if inc is not None:
                sem, k = inc
                inst.then_inc(sem.h, k)

    def flush(self, nc):
        with nc.Block() as blk:
            blk.tensor(lambda e: self.run(e, "T"))
            blk.vector(lambda e: self.run(e, "V"))
            blk.scalar(lambda e: self.run(e, "S"))
            blk.gpsimd(lambda e: self.run(e, "G"))
            blk.sync(lambda e: self.run(e, "Y"))
        self.items = []


def sv(t, pn, col, step, count):
    return bass.AP(t, col, [[t.shape[-1], pn], [step, count]])


def build_program():
    nc = bass.Bass("TRN2", target_bir_lowering=False)

    xhd = nc.dram_tensor("xh", [128, DC * TW], F32, kind="ExternalInput").ap()
    maskh = nc.dram_tensor("maskh", [NST, KPRE], F32, kind="ExternalInput").ap()
    wsTd = nc.dram_tensor("wsT", [D, NST], F32, kind="ExternalInput").ap()
    wiTd = nc.dram_tensor("wiT", [D, NST], F32, kind="ExternalInput").ap()
    a_td = nc.dram_tensor("a_t", [NST, NST], F32, kind="ExternalInput").ap()
    i16d = nc.dram_tensor("i16", [NST, NST], F32, kind="ExternalInput").ap()
    woTd = nc.dram_tensor("woT", [NST, D], F32, kind="ExternalInput").ap()
    wgTd = nc.dram_tensor("wgT", [128, DC * D], F32, kind="ExternalInput").ap()
    w1Td = nc.dram_tensor("w1T", [128, 16 * 2048], F32, kind="ExternalInput").ap()
    w2Td = nc.dram_tensor("w2T", [128, 32 * 1024], F32, kind="ExternalInput").ap()
    lnw8d = nc.dram_tensor("lnw8", [128, DC], F32, kind="ExternalInput").ap()
    lnb8d = nc.dram_tensor("lnb8", [128, DC], F32, kind="ExternalInput").ap()
    bscd = nc.dram_tensor("bs_c", [NST, 1], F32, kind="ExternalInput").ap()
    bicd = nc.dram_tensor("bi_c", [NST, 1], F32, kind="ExternalInput").ap()
    bo8d = nc.dram_tensor("bo8", [128, DC], F32, kind="ExternalInput").ap()
    bg8d = nc.dram_tensor("bg8", [128, DC], F32, kind="ExternalInput").ap()
    b18d = nc.dram_tensor("b18", [128, 32], F32, kind="ExternalInput").ap()
    b28d = nc.dram_tensor("b28", [128, DC], F32, kind="ExternalInput").ap()
    outTd = nc.dram_tensor("outT", [128, 4 * 4096], F32, kind="ExternalOutput").ap()
    if DEBUG:
        dbg = {k: nc.dram_tensor(k, shp, dt, kind="ExternalOutput").ap()
               for k, shp, dt in [
                   ("dbg_xn", [D, TW], BF16), ("dbg_u", [NST, TW], BF16),
                   ("dbg_h", [NST, TW], BF16), ("dbg_gate", [D, T], BF16),
                   ("dbg_y1b", [D, T], BF16), ("dbg_y2acc", [D, T], F32),
                   ("dbg_al1", [128, 32], F32), ("dbg_al2", [128, DC], F32),
                   ("dbg_bgp", [128, DC], F32), ("dbg_bu", [NST, 1], F32),
               ]}

    wsT3 = wsTd.rearrange("(c p) n -> p c n", p=128)
    wiT3 = wiTd.rearrange("(c p) n -> p c n", p=128)
    XOF = lambda g: 8 * KPRE + (g - 1) * 2048 if g > 0 else 0

    with ExitStack() as S0:
        e0 = S0.enter_context
        dC = Sem(S0, nc, "dC")
        dXp = [Sem(S0, nc, f"dX{i}") for i in range(3)]
        dWp = [Sem(S0, nc, "dW0"), Sem(S0, nc, "dW1")]
        dWb = Sem(S0, nc, "dWb")
        dO = Sem(S0, nc, "dO")
        dO1 = Sem(S0, nc, "dO1")
        dDbg = Sem(S0, nc, "dDbg")
        sT = Sem(S0, nc, "sT")
        sV = Sem(S0, nc, "sV")
        sS = Sem(S0, nc, "sS")
        sG = Sem(S0, nc, "sG")
        sY = Sem(S0, nc, "sY")
        ALLSEMS = [("T", sT), ("V", sV), ("S", sS), ("G", sG), ("Y", sY)]
        P = Plan(own={"V": sV, "S": sS, "G": sG})


        lnw_sb = e0(nc.sbuf_tensor("lnw_sb", [128, DC], F32))
        lnb_sb = e0(nc.sbuf_tensor("lnb_sb", [128, DC], F32))
        bs_sb = e0(nc.sbuf_tensor("bs_sb", [NST, 1], F32))
        bi_sb = e0(nc.sbuf_tensor("bi_sb", [NST, 1], F32))
        bu_sb = e0(nc.sbuf_tensor("bu_sb", [NST, 1], F32))
        bo_sb = e0(nc.sbuf_tensor("bo_sb", [128, DC], F32))
        bgr_sb = e0(nc.sbuf_tensor("bgr_sb", [128, DC], F32))
        bgp_sb = e0(nc.sbuf_tensor("bgp_sb", [128, DC], F32))
        b1_sb = e0(nc.sbuf_tensor("b1_sb", [128, 32], F32))
        b2_sb = e0(nc.sbuf_tensor("b2_sb", [128, DC], F32))
        al1_sb = e0(nc.sbuf_tensor("al1_sb", [128, 32], F32))
        al2_sb = e0(nc.sbuf_tensor("al2_sb", [128, DC], F32))
        al2acc = e0(nc.sbuf_tensor("al2acc", [128, DC], F32))

        cst = e0(nc.sbuf_tensor("cst", [128, 5], F32))
        b2a_sb = e0(nc.sbuf_tensor("b2a_sb", [128, DC], F32))
        ones_b = e0(nc.sbuf_tensor("ones_b", [128, 1], BF16))
        ones_f = None
        y1b = e0(nc.sbuf_tensor("y1b", [128, DC * T], BF16))
        slab = e0(nc.sbuf_tensor("slab", [128, DC * 256], F32))
        slabB = e0(nc.sbuf_tensor("slabB", [128, 2048], F32))
        sgn1A = e0(nc.sbuf_tensor("sgn1A", [128, DC * 1024], BF16))
        sgn2A = e0(nc.sbuf_tensor("sgn2A", [128, DC * 1024], BF16))
        absq = [e0(nc.sbuf_tensor(f"absq{i}", [128, 2048], BF16)) for i in range(2)]
        ps = [e0(nc.psum_tensor(f"ps{i}", [128, 512], F32)) for i in range(8)]

        def ln_stage1(src, gw, off, glen, cbuf, cw, sqbuf, ps_sx, ps_sq,
                      src_wait, war_cast, war_sq, war_mm=(), sq_col=0,
                      cast_eng="V", sq0_wait=None, sq1_wait=()):
            merged = ps_sq is ps_sx
            cast_sem = {"V": sV, "G": sG}[cast_eng]
            if sq0_wait is None:
                sq0_wait = src_wait
            m = {}
            P.op(cast_eng, lambda v: v.tensor_copy(
                bass.AP(cbuf, 0, [[cw * DC, 128], [cw, DC], [1, glen]]),
                bass.AP(src, off, [[gw * DC, 128], [gw, DC], [1, glen]])),
                wait=src_wait + war_cast,
                inc=(cast_sem, 1) if cast_eng != "V" else None)
            m["cast"] = cast_sem.n
            m["cast_sem"] = cast_sem
            hw = DC // 2
            P.op("S", lambda s: s.activation(
                bass.AP(sqbuf, 0, [[cw * hw, 128], [cw, hw], [1, glen]]),
                bass.AP(src, off, [[gw * DC, 128], [gw, hw], [1, glen]]),
                AF.Square), wait=list(sq0_wait) + war_sq)
            m["sq0"] = sS.n
            for c in range(DC):
                P.op("T", lambda t, c=c: t.matmul(
                    ps_sx[0:1, 0:glen], ones_b[:, :], cbuf[:, c * cw:c * cw + glen],
                    start=(c == 0), stop=(c == DC - 1 and not merged)),
                    wait=([(m["cast_sem"], m["cast"])] + list(war_mm)) if c == 0 else [])
            for c in range(hw):
                P.op("T", lambda t, c=c: t.matmul(
                    ps_sq[0:1, sq_col:sq_col + glen], ones_b[:, :],
                    sqbuf[:, c * cw:c * cw + glen],
                    start=(c == 0 and not merged), stop=False),
                    wait=[(sS, m["sq0"])] if c == 0 else [],
                    inc=(sT, 1) if c == hw - 1 else None)
            t_half = sT.n
            P.op("S", lambda s: s.activation(
                bass.AP(sqbuf, 0, [[cw * hw, 128], [cw, hw], [1, glen]]),
                bass.AP(src, off + hw * gw, [[gw * DC, 128], [gw, hw], [1, glen]]),
                AF.Square), wait=[(sT, t_half)] + list(sq1_wait))
            m["sq1"] = sS.n
            for c in range(hw):
                P.op("T", lambda t, c=c: t.matmul(
                    ps_sq[0:1, sq_col:sq_col + glen], ones_b[:, :],
                    sqbuf[:, c * cw:c * cw + glen],
                    start=False, stop=(c == hw - 1)),
                    wait=[(sS, m["sq1"])] if c == 0 else [],
                    inc=(sT, 1) if c == hw - 1 else None)
            m["mm"] = sT.n
            return m

        class PsumRows:

            def __init__(self, bank, rw):
                self.bank = bank
                self.rw = rw

            def __getitem__(self, key):
                _, colsl = key
                a = colsl.start
                slot = a // self.rw
                off = a - slot * self.rw
                n = colsl.stop - a
                p = 32 * (slot + 1)
                return bass.AP(self.bank, p * 512 + off, [[512, 1], [1, n]])

        def ln_rows_bf(m, glen, ps_sx, ps_sq, war_rows=(), sq_col=0):
            mu = bass.AP(rows1f, 0, [[512, 1], [1, glen]])
            mu2 = bass.AP(rows1f, 256, [[512, 1], [1, glen]])
            slotA = bass.AP(ps_sx, 32 * 512, [[512, 1], [1, glen]])
            slotB = bass.AP(ps_sx, 64 * 512, [[512, 1], [1, glen]])
            P.op("V", lambda v: v.tensor_scalar(
                mu, ps_sx[0:1, 0:glen], 1.0 / D, None, ALU.mult),
                wait=[(sT, m["mm"])] + list(war_rows))
            P.op("V", lambda v: v.tensor_scalar(
                slotA, ps_sq[0:1, sq_col:sq_col + glen], 1.0 / D, None, ALU.mult))
            P.op("V", lambda v: v.tensor_mul(mu2, mu, mu))
            P.op("V", lambda v: v.tensor_sub(slotA, slotA, mu2))
            P.op("V", lambda v: v.tensor_copy(rows1b[0:1, 0:glen], mu))
            m["rows"] = sV.n
            P.op("S", lambda s: s.activation(
                slotB, slotA, AF.Ln, bias=cst[0:1, 0:1]),
                wait=[(sV, m["rows"])])
            P.op("S", lambda s: s.activation(
                rows1b[0:1, 256:256 + glen], slotB,
                AF.Exp, scale=cst[0:1, 1:2]))
            m["rstd"] = sS.n
            return m

        def ln_bc_bf(m, glen, ps_bcmu, ps_bcr, war_bc):
            P.op("T", lambda t: t.matmul(
                ps_bcmu[0:128, 0:glen], ones_1b[:, :], rows1b[0:1, 0:glen],
                start=True, stop=True), wait=[(sS, m["rstd"])] + list(war_bc))
            P.op("T", lambda t: t.matmul(
                ps_bcr[0:128, 0:glen], ones_1b[:, :], rows1b[0:1, 256:256 + glen],
                start=True, stop=True), inc=(sT, 1))
            m["bc"] = sT.n
            m["bcmu"] = ps_bcmu
            m["bcr"] = ps_bcr
            return m

        def ln_stage2a(m, glen, rowsT, rw, ps_sx, ps_sq, ps_bcmu, ps_bcr, war_bc,
                       war_rows=(), bf_bc=None, sq_col=0):
            if bf_bc is not None:
                ln_rows_bf(m, glen, ps_sx, ps_sq, war_rows, sq_col=sq_col)
                ln_bc_bf(m, glen, ps_bcmu, ps_bcr, war_bc)
                m["bcmu"] = ps_bcmu
                m["bcr"] = ps_bcr
                return m
            else:
                R = [i * rw for i in range(4)]
                P.op("V", lambda v: v.tensor_scalar(
                    rowsT[:, R[0]:R[0] + glen], ps_sx[0:1, 0:glen], 1.0 / D, None, ALU.mult),
                    wait=[(sT, m["mm"])] + list(war_rows))
                P.op("V", lambda v: v.tensor_scalar(
                    rowsT[:, R[1]:R[1] + glen], ps_sq[0:1, 0:glen], 1.0 / D, None, ALU.mult))
                P.op("V", lambda v: v.tensor_mul(
                    rowsT[:, R[2]:R[2] + glen], rowsT[:, R[0]:R[0] + glen],
                    rowsT[:, R[0]:R[0] + glen]))
                P.op("V", lambda v: v.tensor_sub(
                    rowsT[:, R[1]:R[1] + glen], rowsT[:, R[1]:R[1] + glen],
                    rowsT[:, R[2]:R[2] + glen]))
                m["rows"] = sV.n
                P.op("S", lambda s: s.activation(
                    rowsT[:, R[2]:R[2] + glen], rowsT[:, R[1]:R[1] + glen],
                    AF.Ln, bias=cst[0:1, 0:1]), wait=[(sV, m["rows"])])
                P.op("S", lambda s: s.activation(
                    rowsT[:, R[3]:R[3] + glen], rowsT[:, R[2]:R[2] + glen],
                    AF.Exp, scale=cst[0:1, 1:2]))
                m["rstd"] = sS.n
                P.op("T", lambda t: t.matmul(
                    ps_bcmu[0:128, 0:glen], ones_f[:, :], rowsT[0:1, R[0]:R[0] + glen],
                    start=True, stop=True), wait=[(sS, m["rstd"])] + war_bc)
                P.op("T", lambda t: t.matmul(
                    ps_bcr[0:128, 0:glen], ones_f[:, :], rowsT[0:1, R[3]:R[3] + glen],
                    start=True, stop=True), inc=(sT, 1))
            m["bc"] = sT.n
            m["bcmu"] = ps_bcmu
            m["bcr"] = ps_bcr
            return m

        def ln_stage2b(m, src, gw, off, glen, out_ap, affine=None):
            sview = bass.AP(src, off, [[gw * DC, 128], [gw, DC], [1, glen]])
            P.op("V", lambda v: v.tensor_sub(
                sview, sview,
                bass.AP(m["bcmu"], 0, [[512, 128], [0, DC], [1, glen]])),
                wait=[(sT, m["bc"])])
            P.op("V", lambda v: v.tensor_mul(
                out_ap, sview,
                bass.AP(m["bcr"], 0, [[512, 128], [0, DC], [1, glen]])))
            if affine is not None:
                dst, col = affine
                for c in range(DC):
                    P.op("V", lambda v, c=c: v.tensor_scalar(
                        bass.AP(dst, c * TW + col, [[TW * DC, 128], [1, glen]]),
                        cb[:, c * 256:c * 256 + glen],
                        lnw_sb[:, c:c + 1], lnb_sb[:, c:c + 1],
                        ALU.mult, ALU.add))
            m["apply"] = sV.n
            return m

        def ln_stage2(m, src, gw, off, glen, rowsT, rw, ps_sx, ps_sq,
                      ps_bcmu, ps_bcr, out_ap, war_bc):
            ln_stage2a(m, glen, rowsT, rw, ps_sx, ps_sq, ps_bcmu, ps_bcr, war_bc)
            return ln_stage2b(m, src, gw, off, glen, out_ap)

        prep_state = {"A": [], "B": [], "C": [], "prev_alpha": 0, "absq_i": 0}

        def emit_prep_subload(q, wnum, s_idx, sg1, sg2, al_ps, first_al,
                              last_al, slabT, wkey, dsem, split=False):
            ms = {}
            sb = (q * 4 + s_idx)
            if wnum == 1:
                srcd = w1Td[:, sb * 2048:(sb + 1) * 2048]
                sdst = bass.AP(sg1, s_idx * 256,
                               [[1024 * DC, 128], [1024, DC], [1, 256]])
            else:
                srcd = w2Td[:, (q * 8 + 2 * s_idx) * 1024:(q * 8 + 2 * s_idx + 2) * 1024]
                sdst = bass.AP(sg2, 2 * s_idx * 1024, [[1024 * DC, 128], [1, 2048]])
            ms["dma"] = P.op("Y", lambda y: y.dma_start(slabT[:, 0:2048], srcd),
                             wait=prep_state[wkey], inc=(dsem, 16))

            def sig_thunk():
                P.op("S", lambda s: s.activation(sdst, slabT[:, 0:2048], AF.Sign),
                     wait=[(dsem, ms["dma"])], inc=(sS, 1))
                ms["sign"] = sS.n
                ab = absq[prep_state["absq_i"] % 2]
                prep_state["absq_i"] += 1
                ms["ab"] = ab
                P.op("G", lambda g: g.tensor_mul(ab[:, 0:2048], slabT[:, 0:2048], sdst),
                     wait=[(sS, ms["sign"]), (sT, prep_state["prev_alpha"])],
                     inc=(sG, 1))
                ms["abs"] = sG.n
                prep_state[wkey] = [(sG, ms["abs"])]

            def alpha_thunk():
                ab = ms["ab"]
                if wnum == 1:
                    for hs in range(2):
                        hcm = 8 + s_idx * 2 + hs
                        for c in range(DC):
                            st_flag = first_al and hs == 0 and c == 0 and s_idx == 0
                            P.op("T", lambda t, c=c, hs=hs, hcm=hcm, st_flag=st_flag: t.matmul(
                                al_ps[0:128, hcm:hcm + 1],
                                ab[:, c * 256 + hs * 128:c * 256 + hs * 128 + 128],
                                ones_b[:, :], start=st_flag, stop=False),
                                wait=[(sG, ms["abs"])] if (c == 0 and hs == 0) else [],
                                inc=(sT, 1) if (c == DC - 1 and hs == 1) else None)
                else:
                    for hs in range(2):
                        for dsx in range(DC):
                            st_flag = first_al and hs == 0 and dsx == 0 and s_idx == 0
                            sp_flag = last_al and hs == 1 and dsx == DC - 1 and s_idx == 3
                            P.op("T", lambda t, hs=hs, dsx=dsx, st_flag=st_flag, sp_flag=sp_flag: t.matmul(
                                al_ps[0:128, dsx:dsx + 1],
                                ab[:, hs * 1024 + dsx * 128:hs * 1024 + dsx * 128 + 128],
                                ones_b[:, :], start=st_flag, stop=sp_flag),
                                wait=[(sG, ms["abs"])] if (dsx == 0 and hs == 0) else [],
                                inc=(sT, 1) if (dsx == DC - 1 and hs == 1) else None)
                ms["alpha"] = sT.n
                prep_state["prev_alpha"] = ms["alpha"]
            ms["sig_thunk"] = sig_thunk
            ms["alpha_thunk"] = alpha_thunk
            if not split:
                sig_thunk()
            return ms

        def emit_alpha_evacs(q, al_ps, al_wait, first_acc):
            P.op("S", lambda s, q=q: s.activation(
                al1_sb[:, q * 8:(q + 1) * 8], al_ps[0:128, 8:16],
                AF.Identity, scale=cst[:, 2:3]), wait=al_wait, inc=(sS, 1))
            ev1 = sS.n
            if first_acc:
                P.op("V", lambda v: v.tensor_copy(al2acc[:, :], al_ps[0:128, 0:8]),
                     wait=al_wait, inc=(sV, 1))
            else:
                P.op("V", lambda v: v.tensor_add(
                    al2acc[:, :], al2acc[:, :], al_ps[0:128, 0:8]),
                    wait=al_wait, inc=(sV, 1))
            return ev1, sV.n

        with ExitStack() as S1:
            e1 = S1.enter_context
            ws_f = slabB[:, 0:DC * NST]
            wi_f = slabB[:, DC * NST:2 * DC * NST]
            wsum_f = ws_f
            wsum_b = e1(nc.sbuf_tensor("wsum_b", [128, DC * NST], BF16))
            wo_b = e1(nc.sbuf_tensor("wo_b", [NST, D], BF16))
            xst = [e1(nc.sbuf_tensor(f"xst{i}", [128, DC * 256], F32)) for i in range(3)]
            xn = e1(nc.sbuf_tensor("xn", [128, DC * TW], BF16))
            gate = e1(nc.sbuf_tensor("gate", [128, DC * T], BF16))
            cb = e1(nc.sbuf_tensor("cb", [128, DC * 256], BF16))
            sq = e1(nc.sbuf_tensor("sq", [128, DC * 128], BF16))
            tmps = [e1(nc.sbuf_tensor(f"tmps{i}", [128, 256], F32)) for i in range(3)]
            tmpu = tmps[0]
            rows1f = e1(nc.sbuf_tensor("rows1f", [1, 2 * 256], F32))
            a_sb = bass.AP(slabB, 2 * DC * NST, [[2048, NST], [1, NST]])
            i_sb = bass.AP(slabB, 2 * DC * NST + NST, [[2048, NST], [1, NST]])
            mask_sb = bass.AP(tmps[1], 208, [[256, NST], [1, KPRE]])
            hbuf = e1(nc.sbuf_tensor("hbuf", [NST, TW], BF16))
            ubuf = e1(nc.sbuf_tensor("ubuf", [NST, TW], BF16))
            a_bf = e1(nc.sbuf_tensor("a_bf", [NST, NST], BF16))
            i_bf = e1(nc.sbuf_tensor("i_bf", [NST, NST], BF16))
            rows1b = e1(nc.sbuf_tensor("rows1b", [1, 2 * 256], BF16))
            ones_1b = e1(nc.sbuf_tensor("ones_1b", [1, 128], BF16))

            for dst, src in [
                (a_sb, a_td), (i_sb, i16d),
                (lnw_sb[:, :], lnw8d), (lnb_sb[:, :], lnb8d),
                (bs_sb[:, :], bscd), (bi_sb[:, :], bicd),
                (bo_sb[:, :], bo8d), (bgr_sb[:, :], bg8d),
                (b1_sb[:, :], b18d), (b2_sb[:, :], b28d),
                (mask_sb, maskh),
                (slab[0:NST, 0:D], woTd),
            ]:
                P.op("G", lambda g, d=dst, s=src: g.dma_start(d, s), inc=(dC, 16))
            P.op("G", lambda g: g.dma_start(
                ws_f.rearrange("p (c n) -> p c n", c=DC), wsT3), inc=(dC, 16))
            P.op("G", lambda g: g.dma_start(
                wi_f.rearrange("p (c n) -> p c n", c=DC), wiT3), inc=(dC, 16))
            d_const = dC.n

            P.op("V", lambda v: v.memset(ones_b[:, :], 1.0))
            P.op("V", lambda v: v.memset(cst[:, 0:1], EPS))
            P.op("V", lambda v: v.memset(cst[:, 1:2], -0.5))
            P.op("V", lambda v: v.memset(cst[:, 2:3], 1.0 / D))
            P.op("V", lambda v: v.memset(cst[:, 3:4], 1.0 / HID))
            P.op("V", lambda v: v.memset(cst[:, 4:5], -1.0))
            P.op("V", lambda v: v.tensor_add(wsum_f, ws_f, wi_f),
                 wait=[(dC, d_const)])
            P.op("V", lambda v: v.tensor_copy(wsum_b[:, :], slabB[:, 0:DC * NST]))
            P.op("V", lambda v: v.tensor_copy(a_bf[:, :], a_sb))
            P.op("V", lambda v: v.tensor_copy(i_bf[:, :], i_sb))
            P.op("V", lambda v: v.memset(ones_1b[:, :], 1.0))
            P.op("V", lambda v: v.tensor_copy(wo_b[:, :], slab[0:NST, 0:D]),
                 inc=(sV, 1))
            v_prep = sV.n
            P.op("V", lambda v: v.tensor_add(bu_sb[:, :], bs_sb[:, :], bi_sb[:, :]),
                 inc=(sV, 1))
            v_bu = sV.n
            t_bu = sT.n

            d_x = [None] * 9
            d_x[0] = P.op("S", lambda g: g.dma_start(
                bass.AP(xst[0], 0, [[2048, 128], [256, DC], [1, KPRE]]),
                xhd[:, 0:8 * KPRE].rearrange("p (c t) -> p c t", c=DC)),
                inc=(dXp[0], 16))
            d_x[1] = P.op("S", lambda g: g.dma_start(
                xst[1][:, :], xhd[:, XOF(1):XOF(1) + 2048]), inc=(dXp[1], 16))
            d_x[2] = P.op("S", lambda g: g.dma_start(
                xst[2][:, :], xhd[:, XOF(2):XOF(2) + 2048]), inc=(dXp[2], 16))

            st1m = {}
            st2m = {}
            q0_subs = []
            prep_state["B"] = [(sT, t_bu)]

            def ln1_s1(gi):
                glen = KPRE if gi == 0 else 256
                war_cast = [(sT, st1m[gi - 1]["mm"])] if gi >= 1 else []
                war_mm = ([(sV, st1m[gi - 1]["rows"]), (sS, st1m[gi - 1]["rstd"])]
                          if gi >= 1 else [])
                st1m[gi] = ln_stage1(
                    xst[gi % 3], 256, 0, glen, cb, 256, sq,
                    ps[0], ps[0],
                    [(dXp[gi % 3], d_x[gi])], war_cast, war_cast, war_mm,
                    sq_col=256)

            d_wg = [None] * 8
            cast_done = [None] * 8
            chain_t = {}
            chain_bank = {}
            bank_last_sig = {}
            gate_mode = ["ln1"]
            ln1_rot = [0]
            scan_rot = [0]
            wg_all = y1b

            def emit_wg_dma(em, wait):
                d_wg[em] = P.op("Y", lambda y, em=em: y.dma_start(
                    bass.AP(slab, (em % 2) * 128, [[256 * DC, 128], [256, DC], [1, 128]]),
                    wgTd[:, em * 1024:(em + 1) * 1024].rearrange(
                        "p (c e) -> p c e", c=DC)),
                    wait=wait, inc=(dWp[em % 2], 16))

            emit_wg_dma(0, [(sV, v_prep)])
            emit_wg_dma(1, [(sV, v_prep)])

            def th_cast(em, c):
                wt = [(dWp[em % 2], d_wg[em])] if c == 0 else []
                P.op("G", lambda g, c=c, em=em: g.tensor_copy(
                    wg_all[:, em * 1024 + c * 128:em * 1024 + (c + 1) * 128],
                    slab[:, c * 256 + (em % 2) * 128:c * 256 + (em % 2) * 128 + 128]),
                    wait=wt, inc=(sG, 1) if c == DC - 1 else None)
                if c == DC - 1:
                    cast_done[em] = sG.n

            def th_chain(em, tile, c):
                ci = em * 4 + tile
                if c == 0:
                    if gate_mode[0] == "ln1":
                        bank = (6, 1)[ln1_rot[0] % 2]
                        ln1_rot[0] += 1
                    else:
                        bank = scan_rot[0] % 4
                        scan_rot[0] += 1
                    chain_bank[ci] = bank
                    wt = [(sG, cast_done[em]),
                          (sV, st2m[2 * tile + 2]["apply"])]
                    if bank in bank_last_sig:
                        wt.append((sS, bank_last_sig[bank]))
                else:
                    bank = chain_bank[ci]
                    wt = []
                P.op("T", lambda t, c=c, em=em, tile=tile, bank=bank: t.matmul(
                    ps[bank][0:128, 0:512],
                    wg_all[:, em * 1024 + c * 128:em * 1024 + (c + 1) * 128],
                    bass.AP(xn, c * TW + KPRE + tile * 512, [[TW * DC, 128], [1, 512]]),
                    start=(c == 0), stop=(c == DC - 1)),
                    wait=wt, inc=(sT, 1) if c == DC - 1 else None)
                if c == DC - 1:
                    chain_t[ci] = sT.n

            def th_sig(em, tile):
                ci = em * 4 + tile
                bank = chain_bank[ci]
                P.op("S", lambda s, em=em, tile=tile, bank=bank: s.activation(
                    bass.AP(gate, em * T + tile * 512, [[T * DC, 128], [1, 512]]),
                    ps[bank][0:128, 0:512], AF.Sigmoid,
                    bias=bgr_sb[:, em:em + 1]),
                    wait=[(sT, chain_t[ci])], inc=(sS, 1))
                bank_last_sig[bank] = sS.n

            def gate_gen():
                sched = []
                for em in range(8):
                    ctag = max(1, (em + 1) // 2)
                    for c in range(DC):
                        sched.append((ctag, em * 20 + c, "G",
                                      lambda em=em, c=c: th_cast(em, c)))
                    if em + 2 < 8:
                        sched.append((ctag, em * 20 + 15, "Y",
                                      lambda em=em: emit_wg_dma(
                                          em + 2, [(sG, cast_done[em])])))
                for tile in range(4):
                    for em in range(8):
                        tag = max(2 * tile + 2, 1, (em + 1) // 2)
                        base = 1000 + (tile * 8 + em) * 20
                        for c in range(DC):
                            sched.append((tag, base + c, "T",
                                          lambda em=em, tile=tile, c=c:
                                          th_chain(em, tile, c)))
                        sched.append((tag, base + 10, "S",
                                      lambda em=em, tile=tile: th_sig(em, tile)))
                sched.sort(key=lambda it: (it[0], it[1]))
                for tag, _, eng, thunk in sched:
                    yield (eng, tag, thunk)

            gen = gate_gen()
            gen_done = False
            peeked = [None]

            def pull_gate(budget, cur_g=8):
                nonlocal gen_done
                while not gen_done:
                    if peeked[0] is None:
                        try:
                            peeked[0] = next(gen)
                        except StopIteration:
                            gen_done = True
                            break
                    eng, need, thunk = peeked[0]
                    if need > cur_g:
                        break
                    if eng == "T" and budget <= 0:
                        break
                    peeked[0] = None
                    thunk()
                    if eng == "T":
                        budget -= 1

            ln1_s1(0)
            for gi in range(9):
                glen = KPRE if gi == 0 else 256
                bcp = [ps[2], ps[3]] if gi % 2 == 0 else [ps[4], ps[5]]
                war_bc = [(sV, st2m[gi - 2]["apply"])] if gi - 2 in st2m else [(sV, v_bu)]
                war_rows = [(sT, st1m[gi - 1]["bc"])] if gi >= 1 else []
                ln_stage2a(st1m[gi], glen, None, 256,
                           ps[0], ps[0], bcp[0], bcp[1], war_bc,
                           war_rows, bf_bc=rows1b, sq_col=256)
                if gi >= 1:
                    q0_subs[gi - 1]["sig_thunk"]()
                if gi >= 2:
                    q0_subs[gi - 2]["alpha_thunk"]()
                if gi < 8:
                    q0_subs.append(emit_prep_subload(
                        0, 1 if gi < 4 else 2, gi % 4, sgn1A, sgn2A, ps[7],
                        first_al=(gi == 0), last_al=(gi == 7),
                        slabT=slabB, wkey="B", dsem=dWb, split=True))
                if gi >= 1:
                    g2 = gi - 1
                    gl2 = KPRE if g2 == 0 else 256
                    xncol = 0 if g2 == 0 else KPRE + 256 * (g2 - 1)
                    out_ap = bass.AP(cb, 0, [[256 * DC, 128], [256, DC], [1, gl2]])
                    st2m[g2] = ln_stage2b(st1m[g2], xst[g2 % 3], 256, 0, gl2, out_ap,
                                          affine=(xn, xncol))
                    if g2 + 3 <= 8:
                        h2 = g2 + 2
                        d_x[g2 + 3] = P.op("S", lambda g, h2=h2, g2=g2: g.dma_start(
                            xst[g2 % 3][:, :], xhd[:, XOF(h2 + 1):XOF(h2 + 1) + 2048]),
                            wait=[(sV, st2m[g2]["apply"])], inc=(dXp[g2 % 3], 16))
                if gi + 1 <= 8:
                    ln1_s1(gi + 1)
                if gi >= 3:
                    pull_gate(32, gi - 1)
            out_ap = bass.AP(cb, 0, [[256 * DC, 128], [256, DC], [1, 256]])
            st2m[8] = ln_stage2b(st1m[8], xst[8 % 3], 256, 0, 256, out_ap,
                                 affine=(xn, KPRE + 256 * 7))
            v_ln1 = st2m[8]["apply"]
            q0_subs[7]["alpha_thunk"]()

            gate_mode[0] = "scan"
            ugroups = [(0, KPRE)] + [(KPRE + 512 * i, 512) for i in range(4)]
            s_uev = sS.n
            v_umask = sV.n
            for ui, (g0, glen) in enumerate(ugroups):
                for c in range(DC):
                    wt = []
                    if c == 0:
                        wt = [(sS, s_uev)]
                        if ui == 0:
                            wt.append((sV, v_ln1))
                    P.op("T", lambda t, c=c, g0=g0, glen=glen: t.matmul(
                        ps[4][0:NST, 0:glen], wsum_b[:, c * NST:(c + 1) * NST],
                        bass.AP(xn, c * TW + g0, [[TW * DC, 128], [1, glen]]),
                        start=(c == 0), stop=(c == DC - 1)),
                        wait=wt, inc=(sT, 1) if c == DC - 1 else None)
                t_u = sT.n
                if ui == 0:
                    P.op("S", lambda s: s.activation(
                        tmpu[0:NST, 0:KPRE], ps[4][0:NST, 0:KPRE], AF.Identity,
                        bias=bu_sb[0:NST, 0:1]),
                        wait=[(sT, t_u), (sV, v_bu), (sV, v_ln1)], inc=(sS, 1))
                    s_uev = sS.n
                    P.op("V", lambda v: v.tensor_mul(
                        ubuf[0:NST, 0:KPRE], tmpu[0:NST, 0:KPRE], mask_sb),
                        wait=[(sS, s_uev)], inc=(sV, 1))
                    v_umask = sV.n
                else:
                    P.op("S", lambda s, g0=g0, glen=glen: s.activation(
                        ubuf[0:NST, g0:g0 + glen], ps[4][0:NST, 0:glen], AF.Identity,
                        bias=bu_sb[0:NST, 0:1]),
                        wait=[(sT, t_u)], inc=(sS, 1))
                    s_uev = sS.n
                if ui >= 1:
                    pull_gate(16, 8)
            s_u_done = sS.n
            while not gen_done:
                pull_gate(8)
            ev1_q0, v_al2_q0 = emit_alpha_evacs(
                0, ps[7], [(sT, q0_subs[7]["alpha"])], first_acc=True)
            al1_ev0 = ev1_q0

            w_ps6 = [(sS, bank_last_sig[6])] if 6 in bank_last_sig else []
            NH = NCHAIN // 2
            hoff = NH * CH
            s_tanh_h = [None, None]
            for r in range(STEPS):
                for h, (bank, off) in enumerate(((6, 0), (5, hoff))):
                    if r == 0:
                        wt = [(sS, s_u_done), (sV, v_umask), (sV, v_bu)]
                        if h == 0:
                            wt += w_ps6
                        P.op("T", lambda t, bank=bank, off=off: t.matmul(
                            ps[bank][0:NST, 0:NH], i_bf[:, :],
                            sv(ubuf, NST, off, CH, NH),
                            start=True, stop=True),
                            wait=wt, inc=(sT, 1))
                    else:
                        P.op("T", lambda t, r=r, bank=bank, off=off, h=h: t.matmul(
                            ps[bank][0:NST, 0:NH], a_bf[:, :],
                            sv(hbuf, NST, off + r - 1, CH, NH),
                            start=True, stop=False),
                            wait=[(sS, s_tanh_h[h])])
                        P.op("T", lambda t, r=r, bank=bank, off=off: t.matmul(
                            ps[bank][0:NST, 0:NH], i_bf[:, :],
                            sv(ubuf, NST, off + r, CH, NH),
                            start=False, stop=True), inc=(sT, 1))
                    t_mm = sT.n
                    P.op("S", lambda s, r=r, bank=bank, off=off: s.activation(
                        sv(hbuf, NST, off + r, CH, NH), ps[bank][0:NST, 0:NH],
                        AF.Tanh),
                        wait=[(sT, t_mm)], inc=(sS, 1))
                    s_tanh_h[h] = sS.n
            s_tanh = s_tanh_h[1]
            t_scan_end = sT.n
            s_gate_done = sS.n
            P.op("V", lambda v: v.nop(), wait=[(sS, s_tanh)], inc=(sV, 1))
            v_hb = sV.n

            if DEBUG:
                P.op("Y", lambda y: y.dma_start(dbg["dbg_u"], ubuf[:, :]),
                     wait=[(sT, t_scan_end), (sS, sS.n), (sV, sV.n)], inc=(dDbg, 16))

            d_xb = [None] * 8
            stripe_apply = [None] * 8
            prevm = None
            vy_hist = []
            evac_hist = []
            stt_part = {}
            st1b = {}
            st2b = {}
            for st in range(3):
                d_xb[st] = P.op("S", lambda g, st=st: g.dma_start(
                    xst[st % 3][:, :], xhd[:, XOF(st + 1):XOF(st + 1) + 2048]),
                    wait=[(sV, v_ln1)], inc=(dXp[st % 3], 16))
            def b_rows(st):
                war_rows = [(sT, st1b[st - 1]["bc"])] if st >= 1 else \
                    [(sG, q0_subs[7]["abs"]), (sS, q0_subs[7]["sign"])]
                ln_rows_bf(st1b[st], 256, ps[4], ps[4], war_rows, sq_col=256)

            def b_bc(st):
                bcp = [ps[2], ps[3]] if st % 2 == 0 else [ps[6], ps[7]]
                war_bc = [(sV, st2b[st - 2]["apply"])] if st >= 2 else \
                    [(sV, v_ln1), (sS, ev1_q0), (sV, v_al2_q0)]
                ln_bc_bf(st1b[st], 256, bcp[0], bcp[1], war_bc)

            def b_apply(st):
                out_ap = bass.AP(y1b, 256 * st, [[T * DC, 128], [T, DC], [1, 256]])
                st2b[st] = ln_stage2b(st1b[st], xst[st % 3], 256, 0, 256, out_ap)
                stripe_apply[st] = st2b[st]["apply"]
                if st + 3 < 8:
                    d_xb[st + 3] = P.op("S", lambda g, st=st: g.dma_start(
                        xst[(st + 3) % 3][:, :],
                        xhd[:, XOF(st + 4):XOF(st + 4) + 2048]),
                        wait=[(sV, st2b[st]["apply"])],
                        inc=(dXp[(st + 3) % 3], 16))

            for st in range(8):
                for c in range(DC):
                    b3 = (0, 1, 5)[(st * DC + c) % 3]
                    wt = []
                    if c == 0 and st == 0:
                        wt = [(sV, v_hb), (sV, v_ln1), (sS, s_gate_done)]
                    if len(evac_hist) >= 3:
                        wt.append((sS, evac_hist[-3]))
                    P.op("T", lambda t, c=c, st=st, b3=b3: t.matmul(
                        ps[b3][0:128, 0:256], wo_b[0:NST, c * 128:(c + 1) * 128],
                        hbuf[0:NST, KPRE + 256 * st:KPRE + 256 * (st + 1)],
                        start=True, stop=True),
                        wait=wt, inc=(sT, 1))
                    t_m3 = sT.n
                    ew = [(sT, t_m3)]
                    if len(vy_hist) >= 3:
                        ew.append(vy_hist[-3])
                    P.op("S", lambda s, c=c, b3=b3: s.activation(
                        tmps[c % 3][:, 0:256], ps[b3][0:128, 0:256],
                        AF.Identity, bias=bo_sb[:, c:c + 1]),
                        wait=ew, inc=(sS, 1))
                    evac_hist.append(sS.n)
                    P.op("G", lambda g, c=c, st=st: g.tensor_mul(
                        tmps[c % 3][:, 0:256], tmps[c % 3][:, 0:256],
                        bass.AP(gate, c * T + 256 * st, [[T * DC, 128], [1, 256]])),
                        wait=[(sS, evac_hist[-1])], inc=(sG, 1))
                    g_mul = sG.n
                    P.op("V", lambda v, c=c, st=st: v.scalar_tensor_tensor(
                        xst[st % 3][:, c * 256:(c + 1) * 256],
                        xst[st % 3][:, c * 256:(c + 1) * 256], 2.0,
                        tmps[c % 3][:, 0:256], ALU.mult, ALU.add),
                        wait=[(sG, g_mul)] +
                             ([(dXp[st % 3], d_xb[st])] if c == 0 else []),
                        inc=(sV, 1))
                    vy_hist.append((sV, sV.n))
                    if c in (3, 7):
                        stt_part[(st, c)] = (sV, sV.n)
                v_y0 = vy_hist[-1]
                if st >= 1:
                    b_rows(st - 1)
                war_cast = [(sT, st1b[st - 1]["mm"])] if st >= 1 else [(sT, st1m[8]["mm"])]
                war_mm = ([(sV, st1b[st - 1]["rows"]), (sS, st1b[st - 1]["rstd"])]
                          if st >= 1 else [])
                st1b[st] = ln_stage1(
                    xst[st % 3], 256, 0, 256, cb, 256, sq, ps[4], ps[4],
                    [v_y0], war_cast, war_cast, war_mm, sq_col=256,
                    sq0_wait=[stt_part[(st, 3)]], sq1_wait=[stt_part[(st, 7)]])
                if st >= 1:
                    b_bc(st - 1)
                if st >= 2:
                    b_apply(st - 2)
            b_rows(7)
            b_bc(7)
            b_apply(6)
            b_apply(7)
            v_y1_done = st2b[7]["apply"]
            if DEBUG:
                allw = [(sT, sT.n), (sV, sV.n), (sS, sS.n), (sG, sG.n)]
                for name, t, cview in [
                    ("dbg_xn", xn, ("(c p) t -> p c t", TW)),
                    ("dbg_h", hbuf, None),
                    ("dbg_gate", gate, ("(c p) t -> p c t", T)),
                    ("dbg_y1b", y1b, ("(c p) t -> p c t", T)),
                    ("dbg_bu", bu_sb, None),
                ]:
                    dst = dbg[name]
                    if cview is not None:
                        dst = dst.rearrange("(c p) t -> p c t", p=128)
                        srcv = t[:, :].rearrange("p (c t) -> p c t", c=DC)
                    else:
                        srcv = t[:, :]
                    P.op("Y", lambda y, d=dst, s=srcv: y.dma_start(d, s),
                         wait=allw, inc=(dDbg, 16))
            joinv = {}
            for eng, sem in ALLSEMS:
                P.op(eng, lambda e: e.nop(),
                     wait=[(sem, sem.n)] if eng in ("T", "Y") else [],
                     inc=(sem, 1))
                joinv[eng] = sem.n
            P.flush(nc)

        with ExitStack() as S2a:
            e2 = S2a.enter_context
            y2acc = e2(nc.sbuf_tensor("y2acc", [128, DC * T], F32))
            with ExitStack() as S2b:
                e2b = S2b.enter_context
                sgn1B = e2b(nc.sbuf_tensor("sgn1B", [128, DC * 1024], BF16))
                sgn2B = e2b(nc.sbuf_tensor("sgn2B", [128, DC * 1024], BF16))
                gbuf = [e2b(nc.sbuf_tensor(f"gbuf{i}", [128, DC * 512], BF16))
                        for i in range(2)]
                slabC = slabB

                for eng, _ in ALLSEMS:
                    P.op(eng, lambda e: e.nop(),
                         wait=[(osem, joinv[oeng]) for oeng, osem in ALLSEMS
                               if oeng != eng] +
                              ([(dDbg, dDbg.n)] if DEBUG else []))

                v_al2_fin = [0]
                s_b2a = [0]
                sgn1 = [sgn1A, sgn1B]
                sgn2 = [sgn2A, sgn2B]
                q_sign_done = [q0_subs[7]["sign"], None, None, None]
                q_alpha_done = [q0_subs[7]["alpha"], None, None, None]
                al1_ev = [al1_ev0, None, None, None]
                gelu_hist = []
                ev5_hist = []
                for q in range(4):
                    s1q = sgn1[q % 2]
                    s2q = sgn2[q % 2]
                    guard = [(sT, sT.n), (sG, sG.n)]
                    prep_state["A"] = prep_state["A"] + guard + \
                        ([(sV, v_y1_done)] if q == 0 else [])
                    prep_state["B"] = prep_state["B"] + guard + [(sS, sS.n)] + \
                        ([(sV, v_y1_done)] if q == 0 else [])
                    subs = []
                    if q < 3:
                        for s_idx in range(4):
                            subs.append((1, s_idx))
                            subs.append((2, s_idx))
                    sub_i = 0
                    last_sign = None
                    pending_alpha = []
                    prepped = []
                    gelu_tile_done = [None] * 4

                    def emit_m4(tile, s1q=s1q, q=q):
                        for hc in range(8):
                            ghc = q * 8 + hc
                            for c in range(DC):
                                wt = []
                                if c == 0:
                                    if len(gelu_hist) >= 2:
                                        wt.append((sS, gelu_hist[-2]))
                                    if hc == 0 and tile == 0:
                                        wt.append((sS, q_sign_done[q]))
                                        wt.append((sV, v_y1_done))
                                P.op("T", lambda t, c=c, hc=hc, tile=tile: t.matmul(
                                    ps[hc % 2][0:128, 0:512],
                                    s1q[:, c * 1024 + hc * 128:c * 1024 + hc * 128 + 128],
                                    bass.AP(y1b, c * T + tile * 512, [[T * DC, 128], [1, 512]]),
                                    start=(c == 0), stop=(c == DC - 1)),
                                    wait=wt, inc=(sT, 1) if c == DC - 1 else None)
                            t_m4 = sT.n
                            gwt = [(sT, t_m4)]
                            if hc == 0 and tile == 0:
                                gwt.append((sS, al1_ev[q]))
                            ghc = q * 8 + hc
                            P.op("S", lambda s, hc=hc, ghc=ghc, tile=tile: s.activation(
                                gbuf[tile % 2][:, hc * 512:(hc + 1) * 512],
                                ps[hc % 2][0:128, 0:512],
                                AF.Gelu, bias=b1_sb[:, ghc:ghc + 1],
                                scale=al1_sb[:, ghc:ghc + 1]),
                                wait=gwt, inc=(sS, 1))
                            gelu_hist.append(sS.n)
                        gelu_tile_done[tile] = sS.n

                    def emit_m5(tile, s2q=s2q, q=q):
                        for c in range(DC):
                            for hc in range(8):
                                wt = []
                                if hc == 0:
                                    wt = [(sS, gelu_tile_done[tile])]
                                    if len(ev5_hist) >= 2:
                                        wt.append((sV, ev5_hist[-2]))
                                P.op("T", lambda t, c=c, hc=hc, tile=tile: t.matmul(
                                    ps[2 + c % 2][0:128, 0:512],
                                    s2q[:, hc * 1024 + c * 128:hc * 1024 + c * 128 + 128],
                                    gbuf[tile % 2][:, hc * 512:(hc + 1) * 512],
                                    start=(hc == 0), stop=(hc == 7)),
                                    wait=wt, inc=(sT, 1) if hc == 7 else None)
                            t_m5 = sT.n
                            if q == 0:
                                P.op("V", lambda v, c=c, tile=tile: v.tensor_copy(
                                    y2acc[:, c * T + tile * 512:c * T + tile * 512 + 512],
                                    ps[2 + c % 2][0:128, 0:512]),
                                    wait=[(sT, t_m5)], inc=(sV, 1))
                            elif q < 3:
                                P.op("V", lambda v, c=c, tile=tile: v.tensor_add(
                                    y2acc[:, c * T + tile * 512:c * T + tile * 512 + 512],
                                    y2acc[:, c * T + tile * 512:c * T + tile * 512 + 512],
                                    ps[2 + c % 2][0:128, 0:512]),
                                    wait=[(sT, t_m5)], inc=(sV, 1))
                            else:
                                bw = [(sV, s_b2a[0])] if c == 0 else []
                                P.op("V", lambda v, c=c, tile=tile: v.scalar_tensor_tensor(
                                    y2acc[:, c * T + tile * 512:c * T + tile * 512 + 512],
                                    ps[2 + c % 2][0:128, 0:512],
                                    b2a_sb[:, c:c + 1],
                                    y2acc[:, c * T + tile * 512:c * T + tile * 512 + 512],
                                    ALU.add, ALU.add),
                                    wait=[(sT, t_m5)] + bw, inc=(sV, 1))
                            ev5_hist.append(sV.n)

                    if q == 3:
                        P.op("S", lambda s: s.activation(
                            al2_sb[:, :], al2acc[:, :], AF.Identity,
                            scale=cst[:, 3:4]),
                            wait=[(sV, v_al2_fin[0])], inc=(sS, 1))
                        P.op("S", lambda s: s.activation(
                            b2a_sb[:, :], al2_sb[:, :], AF.Ln))
                        P.op("S", lambda s: s.activation(
                            b2a_sb[:, :], b2a_sb[:, :], AF.Exp,
                            scale=cst[:, 4:5]), inc=(sS, 1))
                        P.op("V", lambda v: v.tensor_mul(
                            b2a_sb[:, :], b2a_sb[:, :], b2_sb[:, :]),
                            wait=[(sS, sS.n)], inc=(sV, 1))
                        s_b2a[0] = sV.n
                    for tile in range(4):
                        for th in pending_alpha:
                            th()
                        pending_alpha = []
                        for _ in range(2):
                            if sub_i < len(subs):
                                wnum, s_idx = subs[sub_i]
                                par = sub_i % 2
                                msp = emit_prep_subload(
                                    q + 1, wnum, s_idx, sgn1[(q + 1) % 2],
                                    sgn2[(q + 1) % 2],
                                    ps[6] if (q + 1) % 2 else ps[7],
                                    first_al=(sub_i == 0), last_al=(sub_i == 7),
                                    slabT=[slab, slabC][par], wkey="AB"[par],
                                    dsem=dWp[0] if par == 0 else dWb)
                                last_sign = msp["sign"]
                                prepped.append(msp)
                                pending_alpha.append(msp["alpha_thunk"])
                                sub_i += 1
                        emit_m4(tile)
                        if tile >= 1:
                            emit_m5(tile - 1)
                    for th in pending_alpha:
                        th()
                    pending_alpha = []
                    emit_m5(3)
                    if q < 3:
                        q_sign_done[q + 1] = last_sign
                        q_alpha_done[q + 1] = prepped[-1]["alpha"]
                        ev1, v_al2 = emit_alpha_evacs(
                            q + 1, ps[6] if (q + 1) % 2 else ps[7],
                            [(sT, prepped[-1]["alpha"])], first_acc=False)
                        al1_ev[q + 1] = ev1
                        if q == 2:
                            v_al2_fin[0] = v_al2
                if DEBUG:
                    allw = [(sT, sT.n), (sV, sV.n), (sS, sS.n)]
                    P.op("Y", lambda y: y.dma_start(
                        dbg["dbg_y2acc"].rearrange("(c p) t -> p c t", p=128),
                        y2acc[:, :].rearrange("p (c t) -> p c t", c=DC)),
                        wait=allw, inc=(dDbg, 16))
                    P.op("Y", lambda y: y.dma_start(dbg["dbg_al1"], al1_sb[:, :]),
                         wait=allw, inc=(dDbg, 16))
                    P.op("Y", lambda y: y.dma_start(dbg["dbg_al2"], al2_sb[:, :]),
                         wait=allw, inc=(dDbg, 16))

                joinv2 = {}
                for eng, sem in ALLSEMS:
                    P.op(eng, lambda e: e.nop(),
                         wait=[(sem, sem.n)] if eng in ("T", "Y") else [],
                         inc=(sem, 1))
                    joinv2[eng] = sem.n
                P.flush(nc)

            with ExitStack() as S2c:
                e2c = S2c.enter_context
                y2f = [e2c(nc.sbuf_tensor(f"y2f{i}", [128, DC * 256], F32))
                       for i in range(3)]
                outt = [e2c(nc.sbuf_tensor(f"outt{i}", [128, DC * 256], F32))
                        for i in range(2)]
                cb3 = e2c(nc.sbuf_tensor("cb3", [128, DC * 256], BF16))
                sq3 = e2c(nc.sbuf_tensor("sq3", [128, DC * 128], BF16))
                rows3 = e2c(nc.sbuf_tensor("rows3", [1, 4 * 256], F32))
                ones_f = e2c(nc.sbuf_tensor("ones_f", [1, 128], F32))
                P.op("V", lambda v: v.memset(ones_f[:, :], 1.0))

                for eng, _ in ALLSEMS:
                    P.op(eng, lambda e: e.nop(),
                         wait=[(osem, joinv2[oeng]) for oeng, osem in ALLSEMS
                               if oeng != eng])

                st1c = {}
                st2c = {}
                st_out = {}
                R3 = [i * 256 for i in range(4)]

                def c_rows(st):
                    m = st1c[st]
                    war = [(sT, st1c[st - 1]["bc"])] if st >= 1 else []
                    P.op("V", lambda v: v.tensor_scalar(
                        rows3[:, R3[0]:R3[0] + 256], ps[st % 2][0:1, 0:256],
                        1.0 / D, None, ALU.mult), wait=[(sT, m["mm"])] + war)
                    P.op("V", lambda v: v.tensor_scalar(
                        rows3[:, R3[1]:R3[1] + 256], ps[2 + st % 2][0:1, 0:256],
                        1.0 / D, None, ALU.mult))
                    P.op("V", lambda v: v.tensor_mul(
                        rows3[:, R3[2]:R3[2] + 256], rows3[:, R3[0]:R3[0] + 256],
                        rows3[:, R3[0]:R3[0] + 256]))
                    P.op("V", lambda v: v.tensor_sub(
                        rows3[:, R3[1]:R3[1] + 256], rows3[:, R3[1]:R3[1] + 256],
                        rows3[:, R3[2]:R3[2] + 256]))
                    m["rows"] = sV.n
                    P.op("S", lambda s: s.activation(
                        rows3[:, R3[2]:R3[2] + 256], rows3[:, R3[1]:R3[1] + 256],
                        AF.Ln, bias=cst[0:1, 0:1]), wait=[(sV, m["rows"])])
                    P.op("S", lambda s: s.activation(
                        rows3[:, R3[3]:R3[3] + 256], rows3[:, R3[2]:R3[2] + 256],
                        AF.Exp, scale=cst[0:1, 1:2]))
                    m["rstd"] = sS.n

                def c_bc(st):
                    m = st1c[st]
                    bcp = [ps[4], ps[5]] if st % 2 == 0 else [ps[6], ps[7]]
                    war = [(sV, st2c[st - 2]["apply"])] if st >= 2 else []
                    P.op("T", lambda t, st=st: t.matmul(
                        bcp[0][0:128, 0:256], ones_f[:, :], rows3[0:1, R3[0]:R3[0] + 256],
                        start=True, stop=True), wait=[(sS, m["rstd"])] + war)
                    P.op("T", lambda t, st=st: t.matmul(
                        bcp[1][0:128, 0:256], ones_f[:, :], rows3[0:1, R3[3]:R3[3] + 256],
                        start=True, stop=True), inc=(sT, 1))
                    m["bc"] = sT.n
                    m["bcmu"] = bcp[0]
                    m["bcr"] = bcp[1]

                def c_apply(st):
                    war = [st_out[st - 2]] if st >= 2 else []
                    ot = outt[st % 2]
                    out_ap = bass.AP(ot, 0, [[256 * DC, 128], [256, DC], [1, 256]])
                    m = st1c[st]
                    sview = bass.AP(y2f[st % 3], 0, [[256 * DC, 128], [256, DC], [1, 256]])
                    P.op("V", lambda v: v.tensor_sub(
                        sview, sview,
                        bass.AP(m["bcmu"], 0, [[512, 128], [0, DC], [1, 256]])),
                        wait=[(sT, m["bc"])] + war)
                    P.op("V", lambda v, ot=ot: v.tensor_mul(
                        out_ap, sview,
                        bass.AP(m["bcr"], 0, [[512, 128], [0, DC], [1, 256]])))
                    st2c[st] = {"apply": sV.n}
                    osem = dO if st % 2 == 0 else dO1
                    P.op("Y", lambda y, st=st, ot=ot: y.dma_start(
                        outTd[:, st * 2048:(st + 1) * 2048], ot[:, :]),
                        wait=[(sV, st2c[st]["apply"])], inc=(osem, 16))
                    st_out[st] = (osem, osem.n)

                for st in range(8):
                    yf = y2f[st % 3]
                    if st >= 2:
                        c_apply(st - 2)
                    if st >= 1:
                        c_rows(st - 1)
                    for c in range(DC):
                        swt = []
                        if c == 0 and st >= 3:
                            swt.append((sV, st2c[st - 3]["apply"]))
                        P.op("V", lambda v, c=c, st=st, yf=yf: v.scalar_tensor_tensor(
                            yf[:, c * 256:(c + 1) * 256],
                            y2acc[:, c * T + st * 256:c * T + st * 256 + 256],
                            al2_sb[:, c:c + 1],
                            bass.AP(y1b, c * T + st * 256, [[T * DC, 128], [1, 256]]),
                            ALU.mult, ALU.add),
                            wait=swt, inc=(sV, 1) if c == DC - 1 else None)
                    v_y2f = sV.n
                    if st >= 1:
                        c_bc(st - 1)
                    war_cast = [(sT, st1c[st - 1]["mm"])] if st >= 1 else []
                    war_mm = ([(sV, st1c[st - 1]["rows"]), (sS, st1c[st - 1]["rstd"])]
                              if st >= 1 else [])
                    st1c[st] = ln_stage1(
                        yf, 256, 0, 256, cb3, 256, sq3,
                        ps[st % 2], ps[2 + st % 2],
                        [(sV, v_y2f)], war_cast, war_cast, war_mm)
                c_rows(7)
                c_bc(7)
                c_apply(6)
                c_apply(7)
                P.op("Y", lambda y: y.nop(),
                     wait=[(dO, dO.n), (dO1, dO1.n)] +
                          ([(dDbg, dDbg.n)] if DEBUG else []))
                P.flush(nc)

    return nc


def kernel(**inputs):
    x = inputs["L0_attn_x"]
    f = np.float32
    wgT = np.asarray(inputs["L0_attn_Wg"]).T.astype(f)
    wg_host = np.ascontiguousarray(
        wgT.reshape(DC, 128, DC, 128).transpose(1, 2, 0, 3).reshape(128, DC * D))
    w1T = np.asarray(inputs["L0_ffn_w1"]).T.astype(f)
    w1_host = np.ascontiguousarray(
        w1T.reshape(DC, 128, 16, 256).transpose(1, 2, 0, 3).reshape(128, 16 * 2048))
    w2T = np.asarray(inputs["L0_ffn_w2"]).T.astype(f)
    w2_host = np.ascontiguousarray(
        w2T.reshape(32, 128, 1024).transpose(1, 0, 2).reshape(128, 32 * 1024))
    common = {
        "wsT": np.ascontiguousarray(inputs["L0_attn_Ws"].T, dtype=f),
        "wiT": np.ascontiguousarray(inputs["L0_attn_Wi"].T, dtype=f),
        "a_t": np.ascontiguousarray(inputs["L0_attn_A"].T, dtype=f),
        "i16": np.eye(NST, dtype=f),
        "woT": np.ascontiguousarray(inputs["L0_attn_Wo"].T, dtype=f),
        "wgT": wg_host,
        "w1T": w1_host,
        "w2T": w2_host,
        "lnw8": np.ascontiguousarray(np.asarray(inputs["L0_attn_ln_w"]).reshape(DC, 128).T, dtype=f),
        "lnb8": np.ascontiguousarray(np.asarray(inputs["L0_attn_ln_b"]).reshape(DC, 128).T, dtype=f),
        "bs_c": np.ascontiguousarray(np.asarray(inputs["L0_attn_bs"]).reshape(NST, 1), dtype=f),
        "bi_c": np.ascontiguousarray(np.asarray(inputs["L0_attn_bi"]).reshape(NST, 1), dtype=f),
        "bo8": np.ascontiguousarray(np.asarray(inputs["L0_attn_bo"]).reshape(DC, 128).T, dtype=f),
        "bg8": np.ascontiguousarray(np.asarray(inputs["L0_attn_bg"]).reshape(DC, 128).T, dtype=f),
        "b18": np.ascontiguousarray(np.asarray(inputs["L0_ffn_b1"]).reshape(32, 128).T, dtype=f),
        "b28": np.ascontiguousarray(np.asarray(inputs["L0_ffn_b2"]).reshape(DC, 128).T, dtype=f),
    }
    x = np.asarray(x)
    in_maps = []
    for core in range(8):
        b, q = core // 4, core % 4
        start = q * T
        xcore = np.zeros((D, TW), dtype=f)
        xcore[:, KPRE:] = x[b, start:start + T, :].T
        if q > 0:
            xcore[:, :KPRE] = x[b, start - KPRE:start, :].T
        x8 = xcore.reshape(DC, 128, TW)
        blocks = [x8[:, :, 0:KPRE].transpose(1, 0, 2).reshape(128, DC * KPRE)]
        for g in range(8):
            blocks.append(x8[:, :, KPRE + 256 * g:KPRE + 256 * (g + 1)]
                          .transpose(1, 0, 2).reshape(128, DC * 256))
        m = dict(common)
        m["xh"] = np.ascontiguousarray(np.concatenate(blocks, axis=1))
        m["maskh"] = (np.zeros if q == 0 else np.ones)((NST, KPRE), dtype=f)
        in_maps.append(m)

    nc = build_program()
    res = run_bass_kernel_spmd(nc, in_maps, core_ids=list(range(8)))

    out = np.empty((2, 8192, D), dtype=np.float32)
    for core in range(8):
        b, q = core // 4, core % 4
        o = res.results[core]["outT"].reshape(128, 8, DC, 256)
        out[b, q * T:(q + 1) * T, :] = o.transpose(1, 3, 2, 0).reshape(T, D)
    return out


# version 39
# speedup vs baseline: 1.1883x; 1.1883x over previous
import numpy as np
import ml_dtypes
from contextlib import ExitStack

import concourse.bass as bass
import concourse.mybir as mybir
from concourse.bass_utils import run_bass_kernel_spmd

F32 = mybir.dt.float32
BF16 = mybir.dt.bfloat16
AF = mybir.ActivationFunctionType
ALU = mybir.AluOpType

DEBUG = False
SELF_EDGES = True
D = 1024
DC = 8
NST = 16
HID = 4096
T = 2048
KPRE = 24
TW = T + KPRE
CH = 16
NCHAIN = T // CH
STEPS = KPRE + CH
EPS = 1e-5


class Sem:
    def __init__(self, stack, nc, name):
        self.h = stack.enter_context(nc.semaphore(name))
        self.n = 0


class Plan:

    def __init__(self, own=None):
        self.items = []
        self.own = own or {}

    def op(self, eng, fn, wait=(), inc=None):
        own = self.own.get(eng)
        selfwait = None
        if own is not None and (inc is None or inc[0] is own):
            if SELF_EDGES:
                selfwait = (own, own.n)
            inc = (own, 1)
        self.items.append((eng, fn, list(wait), inc, selfwait))
        if inc is not None:
            sem, k = inc
            sem.n += k
            return sem.n
        return None

    def run(self, engine, name):
        for eng, fn, waits, inc, selfwait in self.items:
            if eng != name:
                continue
            if selfwait is not None and selfwait[1] > 0:
                engine.wait_ge(selfwait[0].h, selfwait[1])
            for sem, val in waits:
                engine.wait_ge(sem.h, val)
            inst = fn(engine)
            if inc is not None:
                sem, k = inc
                inst.then_inc(sem.h, k)

    def flush(self, nc):
        with nc.Block() as blk:
            blk.tensor(lambda e: self.run(e, "T"))
            blk.vector(lambda e: self.run(e, "V"))
            blk.scalar(lambda e: self.run(e, "S"))
            blk.gpsimd(lambda e: self.run(e, "G"))
            blk.sync(lambda e: self.run(e, "Y"))
        self.items = []


def sv(t, pn, col, step, count):
    return bass.AP(t, col, [[t.shape[-1], pn], [step, count]])


def build_program():
    nc = bass.Bass("TRN2", target_bir_lowering=False)

    xhd = nc.dram_tensor("xh", [128, DC * TW], F32, kind="ExternalInput").ap()
    maskh = nc.dram_tensor("maskh", [NST, KPRE], F32, kind="ExternalInput").ap()
    wsTd = nc.dram_tensor("wsT", [D, NST], F32, kind="ExternalInput").ap()
    wiTd = nc.dram_tensor("wiT", [D, NST], F32, kind="ExternalInput").ap()
    a_td = nc.dram_tensor("a_t", [NST, NST], F32, kind="ExternalInput").ap()
    i16d = nc.dram_tensor("i16", [NST, NST], F32, kind="ExternalInput").ap()
    woTd = nc.dram_tensor("woT", [NST, D], F32, kind="ExternalInput").ap()
    wgTd = nc.dram_tensor("wgT", [128, DC * D], F32, kind="ExternalInput").ap()
    w1Td = nc.dram_tensor("w1T", [128, 16 * 2048], F32, kind="ExternalInput").ap()
    w2Td = nc.dram_tensor("w2T", [128, 32 * 1024], F32, kind="ExternalInput").ap()
    lnw8d = nc.dram_tensor("lnw8", [128, DC], F32, kind="ExternalInput").ap()
    lnb8d = nc.dram_tensor("lnb8", [128, DC], F32, kind="ExternalInput").ap()
    bscd = nc.dram_tensor("bs_c", [NST, 1], F32, kind="ExternalInput").ap()
    bicd = nc.dram_tensor("bi_c", [NST, 1], F32, kind="ExternalInput").ap()
    bo8d = nc.dram_tensor("bo8", [128, DC], F32, kind="ExternalInput").ap()
    bg8d = nc.dram_tensor("bg8", [128, DC], F32, kind="ExternalInput").ap()
    b18d = nc.dram_tensor("b18", [128, 32], F32, kind="ExternalInput").ap()
    b28d = nc.dram_tensor("b28", [128, DC], F32, kind="ExternalInput").ap()
    outTd = nc.dram_tensor("outT", [128, 4 * 4096], F32, kind="ExternalOutput").ap()
    if DEBUG:
        dbg = {k: nc.dram_tensor(k, shp, dt, kind="ExternalOutput").ap()
               for k, shp, dt in [
                   ("dbg_xn", [D, TW], BF16), ("dbg_u", [NST, TW], BF16),
                   ("dbg_h", [NST, TW], BF16), ("dbg_gate", [D, T], BF16),
                   ("dbg_y1b", [D, T], BF16), ("dbg_y2acc", [D, T], F32),
                   ("dbg_al1", [128, 32], F32), ("dbg_al2", [128, DC], F32),
                   ("dbg_bgp", [128, DC], F32), ("dbg_bu", [NST, 1], F32),
               ]}

    wsT3 = wsTd.rearrange("(c p) n -> p c n", p=128)
    wiT3 = wiTd.rearrange("(c p) n -> p c n", p=128)
    XOF = lambda g: 8 * KPRE + (g - 1) * 2048 if g > 0 else 0

    with ExitStack() as S0:
        e0 = S0.enter_context
        dC = Sem(S0, nc, "dC")
        dXp = [Sem(S0, nc, f"dX{i}") for i in range(3)]
        dWp = [Sem(S0, nc, "dW0"), Sem(S0, nc, "dW1")]
        dWb = Sem(S0, nc, "dWb")
        dO = Sem(S0, nc, "dO")
        dO1 = Sem(S0, nc, "dO1")
        dDbg = Sem(S0, nc, "dDbg")
        sT = Sem(S0, nc, "sT")
        sV = Sem(S0, nc, "sV")
        sS = Sem(S0, nc, "sS")
        sG = Sem(S0, nc, "sG")
        sY = Sem(S0, nc, "sY")
        ALLSEMS = [("T", sT), ("V", sV), ("S", sS), ("G", sG), ("Y", sY)]
        P = Plan(own={"V": sV, "S": sS, "G": sG})


        lnw_sb = e0(nc.sbuf_tensor("lnw_sb", [128, DC], F32))
        lnb_sb = e0(nc.sbuf_tensor("lnb_sb", [128, DC], F32))
        bs_sb = e0(nc.sbuf_tensor("bs_sb", [NST, 1], F32))
        bi_sb = e0(nc.sbuf_tensor("bi_sb", [NST, 1], F32))
        bu_sb = e0(nc.sbuf_tensor("bu_sb", [NST, 1], F32))
        bo_sb = e0(nc.sbuf_tensor("bo_sb", [128, DC], F32))
        bgr_sb = e0(nc.sbuf_tensor("bgr_sb", [128, DC], F32))
        bgp_sb = e0(nc.sbuf_tensor("bgp_sb", [128, DC], F32))
        b1_sb = e0(nc.sbuf_tensor("b1_sb", [128, 32], F32))
        b2_sb = e0(nc.sbuf_tensor("b2_sb", [128, DC], F32))
        al1_sb = e0(nc.sbuf_tensor("al1_sb", [128, 32], F32))
        al2_sb = e0(nc.sbuf_tensor("al2_sb", [128, DC], F32))
        al2acc = e0(nc.sbuf_tensor("al2acc", [128, DC], F32))

        cst = e0(nc.sbuf_tensor("cst", [128, 5], F32))
        b2a_sb = e0(nc.sbuf_tensor("b2a_sb", [128, DC], F32))
        ones_b = e0(nc.sbuf_tensor("ones_b", [128, 1], BF16))
        ones_f = None
        y1b = e0(nc.sbuf_tensor("y1b", [128, DC * T], BF16))
        slab = e0(nc.sbuf_tensor("slab", [128, DC * 256], F32))
        slabB = e0(nc.sbuf_tensor("slabB", [128, 2048], F32))
        sgn1A = e0(nc.sbuf_tensor("sgn1A", [128, DC * 1024], BF16))
        sgn2A = e0(nc.sbuf_tensor("sgn2A", [128, DC * 1024], BF16))
        absq = [e0(nc.sbuf_tensor(f"absq{i}", [128, 2048], BF16)) for i in range(2)]
        ps = [e0(nc.psum_tensor(f"ps{i}", [128, 512], F32)) for i in range(8)]

        def ln_stage1(src, gw, off, glen, cbuf, cw, sqbuf, ps_sx, ps_sq,
                      src_wait, war_cast, war_sq, war_mm=(), sq_col=0,
                      cast_eng="V", sq0_wait=None, sq1_wait=()):
            merged = ps_sq is ps_sx
            cast_sem = {"V": sV, "G": sG}[cast_eng]
            if sq0_wait is None:
                sq0_wait = src_wait
            m = {}
            P.op(cast_eng, lambda v: v.tensor_copy(
                bass.AP(cbuf, 0, [[cw * DC, 128], [cw, DC], [1, glen]]),
                bass.AP(src, off, [[gw * DC, 128], [gw, DC], [1, glen]])),
                wait=src_wait + war_cast,
                inc=(cast_sem, 1) if cast_eng != "V" else None)
            m["cast"] = cast_sem.n
            m["cast_sem"] = cast_sem
            hw = DC // 2
            P.op("S", lambda s: s.activation(
                bass.AP(sqbuf, 0, [[cw * hw, 128], [cw, hw], [1, glen]]),
                bass.AP(src, off, [[gw * DC, 128], [gw, hw], [1, glen]]),
                AF.Square), wait=list(sq0_wait) + war_sq)
            m["sq0"] = sS.n
            for c in range(DC):
                P.op("T", lambda t, c=c: t.matmul(
                    ps_sx[0:1, 0:glen], ones_b[:, :], cbuf[:, c * cw:c * cw + glen],
                    start=(c == 0), stop=(c == DC - 1 and not merged)),
                    wait=([(m["cast_sem"], m["cast"])] + list(war_mm)) if c == 0 else [])
            for c in range(hw):
                P.op("T", lambda t, c=c: t.matmul(
                    ps_sq[0:1, sq_col:sq_col + glen], ones_b[:, :],
                    sqbuf[:, c * cw:c * cw + glen],
                    start=(c == 0 and not merged), stop=False),
                    wait=[(sS, m["sq0"])] if c == 0 else [],
                    inc=(sT, 1) if c == hw - 1 else None)
            t_half = sT.n
            P.op("S", lambda s: s.activation(
                bass.AP(sqbuf, 0, [[cw * hw, 128], [cw, hw], [1, glen]]),
                bass.AP(src, off + hw * gw, [[gw * DC, 128], [gw, hw], [1, glen]]),
                AF.Square), wait=[(sT, t_half)] + list(sq1_wait))
            m["sq1"] = sS.n
            for c in range(hw):
                P.op("T", lambda t, c=c: t.matmul(
                    ps_sq[0:1, sq_col:sq_col + glen], ones_b[:, :],
                    sqbuf[:, c * cw:c * cw + glen],
                    start=False, stop=(c == hw - 1)),
                    wait=[(sS, m["sq1"])] if c == 0 else [],
                    inc=(sT, 1) if c == hw - 1 else None)
            m["mm"] = sT.n
            return m

        class PsumRows:

            def __init__(self, bank, rw):
                self.bank = bank
                self.rw = rw

            def __getitem__(self, key):
                _, colsl = key
                a = colsl.start
                slot = a // self.rw
                off = a - slot * self.rw
                n = colsl.stop - a
                p = 32 * (slot + 1)
                return bass.AP(self.bank, p * 512 + off, [[512, 1], [1, n]])

        def ln_rows_bf(m, glen, ps_sx, ps_sq, war_rows=(), sq_col=0):
            mu = bass.AP(rows1f, 0, [[512, 1], [1, glen]])
            mu2 = bass.AP(rows1f, 256, [[512, 1], [1, glen]])
            slotA = bass.AP(ps_sx, 32 * 512, [[512, 1], [1, glen]])
            slotB = bass.AP(ps_sx, 64 * 512, [[512, 1], [1, glen]])
            P.op("V", lambda v: v.tensor_scalar(
                mu, ps_sx[0:1, 0:glen], 1.0 / D, None, ALU.mult),
                wait=[(sT, m["mm"])] + list(war_rows))
            P.op("V", lambda v: v.tensor_scalar(
                slotA, ps_sq[0:1, sq_col:sq_col + glen], 1.0 / D, None, ALU.mult))
            P.op("V", lambda v: v.tensor_mul(mu2, mu, mu))
            P.op("V", lambda v: v.tensor_sub(slotA, slotA, mu2))
            P.op("V", lambda v: v.tensor_copy(rows1b[0:1, 0:glen], mu))
            m["rows"] = sV.n
            P.op("S", lambda s: s.activation(
                slotB, slotA, AF.Ln, bias=cst[0:1, 0:1]),
                wait=[(sV, m["rows"])])
            P.op("S", lambda s: s.activation(
                rows1b[0:1, 256:256 + glen], slotB,
                AF.Exp, scale=cst[0:1, 1:2]))
            m["rstd"] = sS.n
            return m

        def ln_bc_bf(m, glen, ps_bcmu, ps_bcr, war_bc):
            P.op("T", lambda t: t.matmul(
                ps_bcmu[0:128, 0:glen], ones_1b[:, :], rows1b[0:1, 0:glen],
                start=True, stop=True), wait=[(sS, m["rstd"])] + list(war_bc))
            P.op("T", lambda t: t.matmul(
                ps_bcr[0:128, 0:glen], ones_1b[:, :], rows1b[0:1, 256:256 + glen],
                start=True, stop=True), inc=(sT, 1))
            m["bc"] = sT.n
            m["bcmu"] = ps_bcmu
            m["bcr"] = ps_bcr
            return m

        def ln_stage2a(m, glen, rowsT, rw, ps_sx, ps_sq, ps_bcmu, ps_bcr, war_bc,
                       war_rows=(), bf_bc=None, sq_col=0):
            if bf_bc is not None:
                ln_rows_bf(m, glen, ps_sx, ps_sq, war_rows, sq_col=sq_col)
                ln_bc_bf(m, glen, ps_bcmu, ps_bcr, war_bc)
                m["bcmu"] = ps_bcmu
                m["bcr"] = ps_bcr
                return m
            else:
                R = [i * rw for i in range(4)]
                P.op("V", lambda v: v.tensor_scalar(
                    rowsT[:, R[0]:R[0] + glen], ps_sx[0:1, 0:glen], 1.0 / D, None, ALU.mult),
                    wait=[(sT, m["mm"])] + list(war_rows))
                P.op("V", lambda v: v.tensor_scalar(
                    rowsT[:, R[1]:R[1] + glen], ps_sq[0:1, 0:glen], 1.0 / D, None, ALU.mult))
                P.op("V", lambda v: v.tensor_mul(
                    rowsT[:, R[2]:R[2] + glen], rowsT[:, R[0]:R[0] + glen],
                    rowsT[:, R[0]:R[0] + glen]))
                P.op("V", lambda v: v.tensor_sub(
                    rowsT[:, R[1]:R[1] + glen], rowsT[:, R[1]:R[1] + glen],
                    rowsT[:, R[2]:R[2] + glen]))
                m["rows"] = sV.n
                P.op("S", lambda s: s.activation(
                    rowsT[:, R[2]:R[2] + glen], rowsT[:, R[1]:R[1] + glen],
                    AF.Ln, bias=cst[0:1, 0:1]), wait=[(sV, m["rows"])])
                P.op("S", lambda s: s.activation(
                    rowsT[:, R[3]:R[3] + glen], rowsT[:, R[2]:R[2] + glen],
                    AF.Exp, scale=cst[0:1, 1:2]))
                m["rstd"] = sS.n
                P.op("T", lambda t: t.matmul(
                    ps_bcmu[0:128, 0:glen], ones_f[:, :], rowsT[0:1, R[0]:R[0] + glen],
                    start=True, stop=True), wait=[(sS, m["rstd"])] + war_bc)
                P.op("T", lambda t: t.matmul(
                    ps_bcr[0:128, 0:glen], ones_f[:, :], rowsT[0:1, R[3]:R[3] + glen],
                    start=True, stop=True), inc=(sT, 1))
            m["bc"] = sT.n
            m["bcmu"] = ps_bcmu
            m["bcr"] = ps_bcr
            return m

        def ln_stage2b(m, src, gw, off, glen, out_ap, affine=None):
            sview = bass.AP(src, off, [[gw * DC, 128], [gw, DC], [1, glen]])
            P.op("V", lambda v: v.tensor_sub(
                sview, sview,
                bass.AP(m["bcmu"], 0, [[512, 128], [0, DC], [1, glen]])),
                wait=[(sT, m["bc"])])
            P.op("V", lambda v: v.tensor_mul(
                out_ap, sview,
                bass.AP(m["bcr"], 0, [[512, 128], [0, DC], [1, glen]])))
            if affine is not None:
                dst, col = affine
                for c in range(DC):
                    P.op("V", lambda v, c=c: v.tensor_scalar(
                        bass.AP(dst, c * TW + col, [[TW * DC, 128], [1, glen]]),
                        cb[:, c * 256:c * 256 + glen],
                        lnw_sb[:, c:c + 1], lnb_sb[:, c:c + 1],
                        ALU.mult, ALU.add))
            m["apply"] = sV.n
            return m

        def ln_stage2(m, src, gw, off, glen, rowsT, rw, ps_sx, ps_sq,
                      ps_bcmu, ps_bcr, out_ap, war_bc):
            ln_stage2a(m, glen, rowsT, rw, ps_sx, ps_sq, ps_bcmu, ps_bcr, war_bc)
            return ln_stage2b(m, src, gw, off, glen, out_ap)

        prep_state = {"A": [], "B": [], "C": [], "prev_alpha": 0, "absq_i": 0}

        def emit_prep_subload(q, wnum, s_idx, sg1, sg2, al_ps, first_al,
                              last_al, slabT, wkey, dsem, split=False,
                              split_epochs=False):
            ms = {}
            sb = (q * 4 + s_idx)
            if wnum == 1:
                srcd = w1Td[:, sb * 2048:(sb + 1) * 2048]
                sdst = bass.AP(sg1, s_idx * 256,
                               [[1024 * DC, 128], [1024, DC], [1, 256]])
            else:
                srcd = w2Td[:, (q * 8 + 2 * s_idx) * 1024:(q * 8 + 2 * s_idx + 2) * 1024]
                sdst = bass.AP(sg2, 2 * s_idx * 1024, [[1024 * DC, 128], [1, 2048]])
            ms["dma"] = P.op("Y", lambda y: y.dma_start(slabT[:, 0:2048], srcd),
                             wait=prep_state[wkey], inc=(dsem, 16))

            def sig_thunk():
                P.op("S", lambda s: s.activation(sdst, slabT[:, 0:2048], AF.Sign),
                     wait=[(dsem, ms["dma"])], inc=(sS, 1))
                ms["sign"] = sS.n
                ab = absq[prep_state["absq_i"] % 2]
                prep_state["absq_i"] += 1
                ms["ab"] = ab
                P.op("G", lambda g: g.tensor_mul(ab[:, 0:2048], slabT[:, 0:2048], sdst),
                     wait=[(sS, ms["sign"]), (sT, prep_state["prev_alpha"])],
                     inc=(sG, 1))
                ms["abs"] = sG.n
                prep_state[wkey] = [(sG, ms["abs"])]

            def alpha_thunk():
                ab = ms["ab"]
                war0 = []
                if split_epochs and q == 3:
                    war0 = ([(sS, al1_ev[1])] if wnum == 1
                            else [(sV, v_al2_ev[1])])
                if wnum == 1:
                    for hs in range(2):
                        hcm = 8 + s_idx * 2 + hs
                        for c in range(DC):
                            st_flag = first_al and hs == 0 and c == 0 and s_idx == 0
                            sp_flag = (split_epochs and last_al and hs == 1
                                       and c == DC - 1 and s_idx == 3)
                            P.op("T", lambda t, c=c, hs=hs, hcm=hcm, st_flag=st_flag, sp_flag=sp_flag: t.matmul(
                                al_ps[0:128, hcm:hcm + 1],
                                ab[:, c * 256 + hs * 128:c * 256 + hs * 128 + 128],
                                ones_b[:, :], start=st_flag, stop=sp_flag),
                                wait=([(sG, ms["abs"])] + war0) if (c == 0 and hs == 0) else [],
                                inc=(sT, 1) if (c == DC - 1 and hs == 1) else None)
                else:
                    for hs in range(2):
                        for dsx in range(DC):
                            st_flag = first_al and hs == 0 and dsx == 0 and s_idx == 0
                            sp_flag = last_al and hs == 1 and dsx == DC - 1 and s_idx == 3
                            P.op("T", lambda t, hs=hs, dsx=dsx, st_flag=st_flag, sp_flag=sp_flag: t.matmul(
                                al_ps[0:128, dsx:dsx + 1],
                                ab[:, hs * 1024 + dsx * 128:hs * 1024 + dsx * 128 + 128],
                                ones_b[:, :], start=st_flag, stop=sp_flag),
                                wait=([(sG, ms["abs"])] + war0) if (dsx == 0 and hs == 0) else [],
                                inc=(sT, 1) if (dsx == DC - 1 and hs == 1) else None)
                ms["alpha"] = sT.n
                prep_state["prev_alpha"] = ms["alpha"]
            ms["sig_thunk"] = sig_thunk
            ms["alpha_thunk"] = alpha_thunk
            if not split:
                sig_thunk()
            return ms

        def emit_alpha_evacs(q, al_ps, al_wait, first_acc):
            P.op("S", lambda s, q=q: s.activation(
                al1_sb[:, q * 8:(q + 1) * 8], al_ps[0:128, 8:16],
                AF.Identity, scale=cst[:, 2:3]), wait=al_wait, inc=(sS, 1))
            ev1 = sS.n
            if first_acc:
                P.op("V", lambda v: v.tensor_copy(al2acc[:, :], al_ps[0:128, 0:8]),
                     wait=al_wait, inc=(sV, 1))
            else:
                P.op("V", lambda v: v.tensor_add(
                    al2acc[:, :], al2acc[:, :], al_ps[0:128, 0:8]),
                    wait=al_wait, inc=(sV, 1))
            return ev1, sV.n

        with ExitStack() as S1:
            e1 = S1.enter_context
            ws_f = slabB[:, 0:DC * NST]
            wi_f = slabB[:, DC * NST:2 * DC * NST]
            wsum_f = ws_f
            wsum_b = e1(nc.sbuf_tensor("wsum_b", [128, DC * NST], BF16))
            wo_b = e1(nc.sbuf_tensor("wo_b", [NST, D], BF16))
            xst = [e1(nc.sbuf_tensor(f"xst{i}", [128, DC * 256], F32)) for i in range(3)]
            xn = e1(nc.sbuf_tensor("xn", [128, DC * TW], BF16))
            gate = e1(nc.sbuf_tensor("gate", [128, DC * T], BF16))
            cb = e1(nc.sbuf_tensor("cb", [128, DC * 256], BF16))
            sq = e1(nc.sbuf_tensor("sq", [128, DC * 128], BF16))
            tmps = [e1(nc.sbuf_tensor(f"tmps{i}", [128, 256], F32)) for i in range(3)]
            tmpu = tmps[0]
            rows1f = e1(nc.sbuf_tensor("rows1f", [1, 2 * 256], F32))
            a_sb = bass.AP(slabB, 2 * DC * NST, [[2048, NST], [1, NST]])
            i_sb = bass.AP(slabB, 2 * DC * NST + NST, [[2048, NST], [1, NST]])
            mask_sb = bass.AP(tmps[1], 208, [[256, NST], [1, KPRE]])
            hbuf = e1(nc.sbuf_tensor("hbuf", [NST, TW], BF16))
            ubuf = e1(nc.sbuf_tensor("ubuf", [NST, TW], BF16))
            a_bf = e1(nc.sbuf_tensor("a_bf", [NST, NST], BF16))
            i_bf = e1(nc.sbuf_tensor("i_bf", [NST, NST], BF16))
            rows1b = e1(nc.sbuf_tensor("rows1b", [1, 2 * 256], BF16))
            ones_1b = e1(nc.sbuf_tensor("ones_1b", [1, 128], BF16))

            for dst, src in [
                (a_sb, a_td), (i_sb, i16d),
                (lnw_sb[:, :], lnw8d), (lnb_sb[:, :], lnb8d),
                (bs_sb[:, :], bscd), (bi_sb[:, :], bicd),
                (bo_sb[:, :], bo8d), (bgr_sb[:, :], bg8d),
                (b1_sb[:, :], b18d), (b2_sb[:, :], b28d),
                (mask_sb, maskh),
                (slab[0:NST, 0:D], woTd),
            ]:
                P.op("G", lambda g, d=dst, s=src: g.dma_start(d, s), inc=(dC, 16))
            P.op("G", lambda g: g.dma_start(
                ws_f.rearrange("p (c n) -> p c n", c=DC), wsT3), inc=(dC, 16))
            P.op("G", lambda g: g.dma_start(
                wi_f.rearrange("p (c n) -> p c n", c=DC), wiT3), inc=(dC, 16))
            d_const = dC.n

            P.op("V", lambda v: v.memset(ones_b[:, :], 1.0))
            P.op("V", lambda v: v.memset(cst[:, 0:1], EPS))
            P.op("V", lambda v: v.memset(cst[:, 1:2], -0.5))
            P.op("V", lambda v: v.memset(cst[:, 2:3], 1.0 / D))
            P.op("V", lambda v: v.memset(cst[:, 3:4], 1.0 / HID))
            P.op("V", lambda v: v.memset(cst[:, 4:5], -1.0))
            P.op("V", lambda v: v.tensor_add(wsum_f, ws_f, wi_f),
                 wait=[(dC, d_const)])
            P.op("V", lambda v: v.tensor_copy(wsum_b[:, :], slabB[:, 0:DC * NST]))
            P.op("V", lambda v: v.tensor_copy(a_bf[:, :], a_sb))
            P.op("V", lambda v: v.tensor_copy(i_bf[:, :], i_sb))
            P.op("V", lambda v: v.memset(ones_1b[:, :], 1.0))
            P.op("V", lambda v: v.tensor_copy(wo_b[:, :], slab[0:NST, 0:D]),
                 inc=(sV, 1))
            v_prep = sV.n
            P.op("V", lambda v: v.tensor_add(bu_sb[:, :], bs_sb[:, :], bi_sb[:, :]),
                 inc=(sV, 1))
            v_bu = sV.n
            t_bu = sT.n

            d_x = [None] * 9
            d_x[0] = P.op("S", lambda g: g.dma_start(
                bass.AP(xst[0], 0, [[2048, 128], [256, DC], [1, KPRE]]),
                xhd[:, 0:8 * KPRE].rearrange("p (c t) -> p c t", c=DC)),
                inc=(dXp[0], 16))
            d_x[1] = P.op("S", lambda g: g.dma_start(
                xst[1][:, :], xhd[:, XOF(1):XOF(1) + 2048]), inc=(dXp[1], 16))
            d_x[2] = P.op("S", lambda g: g.dma_start(
                xst[2][:, :], xhd[:, XOF(2):XOF(2) + 2048]), inc=(dXp[2], 16))

            st1m = {}
            st2m = {}
            q0_subs = []
            prep_state["B"] = [(sT, t_bu)]

            def ln1_s1(gi):
                glen = KPRE if gi == 0 else 256
                war_cast = [(sT, st1m[gi - 1]["mm"])] if gi >= 1 else []
                war_mm = ([(sV, st1m[gi - 1]["rows"]), (sS, st1m[gi - 1]["rstd"])]
                          if gi >= 1 else [])
                st1m[gi] = ln_stage1(
                    xst[gi % 3], 256, 0, glen, cb, 256, sq,
                    ps[0], ps[0],
                    [(dXp[gi % 3], d_x[gi])], war_cast, war_cast, war_mm,
                    sq_col=256)

            d_wg = [None] * 8
            cast_done = [None] * 8
            chain_t = {}
            chain_bank = {}
            bank_last_sig = {}
            gate_mode = ["ln1"]
            ln1_rot = [0]
            scan_rot = [0]
            wg_all = y1b

            def emit_wg_dma(em, wait):
                d_wg[em] = P.op("Y", lambda y, em=em: y.dma_start(
                    bass.AP(slab, (em % 2) * 128, [[256 * DC, 128], [256, DC], [1, 128]]),
                    wgTd[:, em * 1024:(em + 1) * 1024].rearrange(
                        "p (c e) -> p c e", c=DC)),
                    wait=wait, inc=(dWp[em % 2], 16))

            emit_wg_dma(0, [(sV, v_prep)])
            emit_wg_dma(1, [(sV, v_prep)])

            def th_cast(em, c):
                wt = [(dWp[em % 2], d_wg[em])] if c == 0 else []
                P.op("G", lambda g, c=c, em=em: g.tensor_copy(
                    wg_all[:, em * 1024 + c * 128:em * 1024 + (c + 1) * 128],
                    slab[:, c * 256 + (em % 2) * 128:c * 256 + (em % 2) * 128 + 128]),
                    wait=wt, inc=(sG, 1) if c == DC - 1 else None)
                if c == DC - 1:
                    cast_done[em] = sG.n

            def th_chain(em, tile, c):
                ci = em * 4 + tile
                if c == 0:
                    if gate_mode[0] == "ln1":
                        bank = (6, 1)[ln1_rot[0] % 2]
                        ln1_rot[0] += 1
                    else:
                        bank = scan_rot[0] % 4
                        scan_rot[0] += 1
                    chain_bank[ci] = bank
                    wt = [(sG, cast_done[em]),
                          (sV, st2m[2 * tile + 2]["apply"])]
                    if bank in bank_last_sig:
                        wt.append((sS, bank_last_sig[bank]))
                else:
                    bank = chain_bank[ci]
                    wt = []
                P.op("T", lambda t, c=c, em=em, tile=tile, bank=bank: t.matmul(
                    ps[bank][0:128, 0:512],
                    wg_all[:, em * 1024 + c * 128:em * 1024 + (c + 1) * 128],
                    bass.AP(xn, c * TW + KPRE + tile * 512, [[TW * DC, 128], [1, 512]]),
                    start=(c == 0), stop=(c == DC - 1)),
                    wait=wt, inc=(sT, 1) if c == DC - 1 else None)
                if c == DC - 1:
                    chain_t[ci] = sT.n

            def th_sig(em, tile):
                ci = em * 4 + tile
                bank = chain_bank[ci]
                P.op("S", lambda s, em=em, tile=tile, bank=bank: s.activation(
                    bass.AP(gate, em * T + tile * 512, [[T * DC, 128], [1, 512]]),
                    ps[bank][0:128, 0:512], AF.Sigmoid,
                    bias=bgr_sb[:, em:em + 1]),
                    wait=[(sT, chain_t[ci])], inc=(sS, 1))
                bank_last_sig[bank] = sS.n

            def gate_gen():
                sched = []
                for em in range(8):
                    ctag = max(1, (em + 1) // 2)
                    for c in range(DC):
                        sched.append((ctag, em * 20 + c, "G",
                                      lambda em=em, c=c: th_cast(em, c)))
                    if em + 2 < 8:
                        sched.append((ctag, em * 20 + 15, "Y",
                                      lambda em=em: emit_wg_dma(
                                          em + 2, [(sG, cast_done[em])])))
                for tile in range(4):
                    for em in range(8):
                        tag = max(2 * tile + 2, 1, (em + 1) // 2)
                        base = 1000 + (tile * 8 + em) * 20
                        for c in range(DC):
                            sched.append((tag, base + c, "T",
                                          lambda em=em, tile=tile, c=c:
                                          th_chain(em, tile, c)))
                        sched.append((tag, base + 10, "S",
                                      lambda em=em, tile=tile: th_sig(em, tile)))
                sched.sort(key=lambda it: (it[0], it[1]))
                for tag, _, eng, thunk in sched:
                    yield (eng, tag, thunk)

            gen = gate_gen()
            gen_done = False
            peeked = [None]

            def pull_gate(budget, cur_g=8):
                nonlocal gen_done
                while not gen_done:
                    if peeked[0] is None:
                        try:
                            peeked[0] = next(gen)
                        except StopIteration:
                            gen_done = True
                            break
                    eng, need, thunk = peeked[0]
                    if need > cur_g:
                        break
                    if eng == "T" and budget <= 0:
                        break
                    peeked[0] = None
                    thunk()
                    if eng == "T":
                        budget -= 1

            ln1_s1(0)
            for gi in range(9):
                glen = KPRE if gi == 0 else 256
                bcp = [ps[2], ps[3]] if gi % 2 == 0 else [ps[4], ps[5]]
                war_bc = [(sV, st2m[gi - 2]["apply"])] if gi - 2 in st2m else [(sV, v_bu)]
                war_rows = [(sT, st1m[gi - 1]["bc"])] if gi >= 1 else []
                ln_stage2a(st1m[gi], glen, None, 256,
                           ps[0], ps[0], bcp[0], bcp[1], war_bc,
                           war_rows, bf_bc=rows1b, sq_col=256)
                if gi >= 1:
                    q0_subs[gi - 1]["sig_thunk"]()
                if gi >= 2:
                    q0_subs[gi - 2]["alpha_thunk"]()
                if gi < 8:
                    q0_subs.append(emit_prep_subload(
                        0, 1 if gi < 4 else 2, gi % 4, sgn1A, sgn2A, ps[7],
                        first_al=(gi == 0), last_al=(gi == 7),
                        slabT=slabB, wkey="B", dsem=dWb, split=True))
                if gi >= 1:
                    g2 = gi - 1
                    gl2 = KPRE if g2 == 0 else 256
                    xncol = 0 if g2 == 0 else KPRE + 256 * (g2 - 1)
                    out_ap = bass.AP(cb, 0, [[256 * DC, 128], [256, DC], [1, gl2]])
                    st2m[g2] = ln_stage2b(st1m[g2], xst[g2 % 3], 256, 0, gl2, out_ap,
                                          affine=(xn, xncol))
                    if g2 + 3 <= 8:
                        h2 = g2 + 2
                        d_x[g2 + 3] = P.op("S", lambda g, h2=h2, g2=g2: g.dma_start(
                            xst[g2 % 3][:, :], xhd[:, XOF(h2 + 1):XOF(h2 + 1) + 2048]),
                            wait=[(sV, st2m[g2]["apply"])], inc=(dXp[g2 % 3], 16))
                if gi + 1 <= 8:
                    ln1_s1(gi + 1)
                if gi >= 3:
                    pull_gate(32, gi - 1)
            out_ap = bass.AP(cb, 0, [[256 * DC, 128], [256, DC], [1, 256]])
            st2m[8] = ln_stage2b(st1m[8], xst[8 % 3], 256, 0, 256, out_ap,
                                 affine=(xn, KPRE + 256 * 7))
            v_ln1 = st2m[8]["apply"]
            q0_subs[7]["alpha_thunk"]()

            gate_mode[0] = "scan"
            ugroups = [(0, KPRE)] + [(KPRE + 512 * i, 512) for i in range(4)]
            s_uev = sS.n
            v_umask = sV.n
            for ui, (g0, glen) in enumerate(ugroups):
                for c in range(DC):
                    wt = []
                    if c == 0:
                        wt = [(sS, s_uev)]
                        if ui == 0:
                            wt.append((sV, v_ln1))
                    P.op("T", lambda t, c=c, g0=g0, glen=glen: t.matmul(
                        ps[4][0:NST, 0:glen], wsum_b[:, c * NST:(c + 1) * NST],
                        bass.AP(xn, c * TW + g0, [[TW * DC, 128], [1, glen]]),
                        start=(c == 0), stop=(c == DC - 1)),
                        wait=wt, inc=(sT, 1) if c == DC - 1 else None)
                t_u = sT.n
                if ui == 0:
                    P.op("S", lambda s: s.activation(
                        tmpu[0:NST, 0:KPRE], ps[4][0:NST, 0:KPRE], AF.Identity,
                        bias=bu_sb[0:NST, 0:1]),
                        wait=[(sT, t_u), (sV, v_bu), (sV, v_ln1)], inc=(sS, 1))
                    s_uev = sS.n
                    P.op("V", lambda v: v.tensor_mul(
                        ubuf[0:NST, 0:KPRE], tmpu[0:NST, 0:KPRE], mask_sb),
                        wait=[(sS, s_uev)], inc=(sV, 1))
                    v_umask = sV.n
                else:
                    P.op("S", lambda s, g0=g0, glen=glen: s.activation(
                        ubuf[0:NST, g0:g0 + glen], ps[4][0:NST, 0:glen], AF.Identity,
                        bias=bu_sb[0:NST, 0:1]),
                        wait=[(sT, t_u)], inc=(sS, 1))
                    s_uev = sS.n
                if ui >= 1:
                    pull_gate(16, 8)
            s_u_done = sS.n
            while not gen_done:
                pull_gate(8)
            ev1_q0, v_al2_q0 = emit_alpha_evacs(
                0, ps[7], [(sT, q0_subs[7]["alpha"])], first_acc=True)
            al1_ev0 = ev1_q0

            w_ps6 = [(sS, bank_last_sig[6])] if 6 in bank_last_sig else []
            NH = NCHAIN // 2
            hoff = NH * CH
            s_tanh_h = [None, None]
            for r in range(STEPS):
                for h, (bank, off) in enumerate(((6, 0), (5, hoff))):
                    if r == 0:
                        wt = [(sS, s_u_done), (sV, v_umask), (sV, v_bu)]
                        if h == 0:
                            wt += w_ps6
                        P.op("T", lambda t, bank=bank, off=off: t.matmul(
                            ps[bank][0:NST, 0:NH], i_bf[:, :],
                            sv(ubuf, NST, off, CH, NH),
                            start=True, stop=True),
                            wait=wt, inc=(sT, 1))
                    else:
                        P.op("T", lambda t, r=r, bank=bank, off=off, h=h: t.matmul(
                            ps[bank][0:NST, 0:NH], a_bf[:, :],
                            sv(hbuf, NST, off + r - 1, CH, NH),
                            start=True, stop=False),
                            wait=[(sS, s_tanh_h[h])])
                        P.op("T", lambda t, r=r, bank=bank, off=off: t.matmul(
                            ps[bank][0:NST, 0:NH], i_bf[:, :],
                            sv(ubuf, NST, off + r, CH, NH),
                            start=False, stop=True), inc=(sT, 1))
                    t_mm = sT.n
                    P.op("S", lambda s, r=r, bank=bank, off=off: s.activation(
                        sv(hbuf, NST, off + r, CH, NH), ps[bank][0:NST, 0:NH],
                        AF.Tanh),
                        wait=[(sT, t_mm)], inc=(sS, 1))
                    s_tanh_h[h] = sS.n
            s_tanh = s_tanh_h[1]
            t_scan_end = sT.n
            s_gate_done = sS.n
            P.op("V", lambda v: v.nop(), wait=[(sS, s_tanh)], inc=(sV, 1))
            v_hb = sV.n

            if DEBUG:
                P.op("Y", lambda y: y.dma_start(dbg["dbg_u"], ubuf[:, :]),
                     wait=[(sT, t_scan_end), (sS, sS.n), (sV, sV.n)], inc=(dDbg, 16))

            d_xb = [None] * 8
            stripe_apply = [None] * 8
            prevm = None
            vy_hist = []
            evac_hist = []
            stt_part = {}
            st1b = {}
            st2b = {}
            for st in range(3):
                d_xb[st] = P.op("S", lambda g, st=st: g.dma_start(
                    xst[st % 3][:, :], xhd[:, XOF(st + 1):XOF(st + 1) + 2048]),
                    wait=[(sV, v_ln1)], inc=(dXp[st % 3], 16))
            def b_rows(st):
                war_rows = [(sT, st1b[st - 1]["bc"])] if st >= 1 else \
                    [(sG, q0_subs[7]["abs"]), (sS, q0_subs[7]["sign"])]
                ln_rows_bf(st1b[st], 256, ps[4], ps[4], war_rows, sq_col=256)

            def b_bc(st):
                bcp = [ps[2], ps[3]] if st % 2 == 0 else [ps[6], ps[7]]
                war_bc = [(sV, st2b[st - 2]["apply"])] if st >= 2 else \
                    [(sV, v_ln1), (sS, ev1_q0), (sV, v_al2_q0)]
                ln_bc_bf(st1b[st], 256, bcp[0], bcp[1], war_bc)

            def b_apply(st):
                out_ap = bass.AP(y1b, 256 * st, [[T * DC, 128], [T, DC], [1, 256]])
                st2b[st] = ln_stage2b(st1b[st], xst[st % 3], 256, 0, 256, out_ap)
                stripe_apply[st] = st2b[st]["apply"]
                if st + 3 < 8:
                    d_xb[st + 3] = P.op("S", lambda g, st=st: g.dma_start(
                        xst[(st + 3) % 3][:, :],
                        xhd[:, XOF(st + 4):XOF(st + 4) + 2048]),
                        wait=[(sV, st2b[st]["apply"])],
                        inc=(dXp[(st + 3) % 3], 16))

            for st in range(8):
                for c in range(DC):
                    b3 = (0, 1, 5)[(st * DC + c) % 3]
                    wt = []
                    if c == 0 and st == 0:
                        wt = [(sV, v_hb), (sV, v_ln1), (sS, s_gate_done)]
                    if len(evac_hist) >= 3:
                        wt.append((sS, evac_hist[-3]))
                    P.op("T", lambda t, c=c, st=st, b3=b3: t.matmul(
                        ps[b3][0:128, 0:256], wo_b[0:NST, c * 128:(c + 1) * 128],
                        hbuf[0:NST, KPRE + 256 * st:KPRE + 256 * (st + 1)],
                        start=True, stop=True),
                        wait=wt, inc=(sT, 1))
                    t_m3 = sT.n
                    ew = [(sT, t_m3)]
                    if len(vy_hist) >= 3:
                        ew.append(vy_hist[-3])
                    P.op("S", lambda s, c=c, b3=b3: s.activation(
                        tmps[c % 3][:, 0:256], ps[b3][0:128, 0:256],
                        AF.Identity, bias=bo_sb[:, c:c + 1]),
                        wait=ew, inc=(sS, 1))
                    evac_hist.append(sS.n)
                    P.op("G", lambda g, c=c, st=st: g.tensor_mul(
                        tmps[c % 3][:, 0:256], tmps[c % 3][:, 0:256],
                        bass.AP(gate, c * T + 256 * st, [[T * DC, 128], [1, 256]])),
                        wait=[(sS, evac_hist[-1])], inc=(sG, 1))
                    g_mul = sG.n
                    P.op("V", lambda v, c=c, st=st: v.scalar_tensor_tensor(
                        xst[st % 3][:, c * 256:(c + 1) * 256],
                        xst[st % 3][:, c * 256:(c + 1) * 256], 2.0,
                        tmps[c % 3][:, 0:256], ALU.mult, ALU.add),
                        wait=[(sG, g_mul)] +
                             ([(dXp[st % 3], d_xb[st])] if c == 0 else []),
                        inc=(sV, 1))
                    vy_hist.append((sV, sV.n))
                    if c in (3, 7):
                        stt_part[(st, c)] = (sV, sV.n)
                v_y0 = vy_hist[-1]
                if st >= 1:
                    b_rows(st - 1)
                war_cast = [(sT, st1b[st - 1]["mm"])] if st >= 1 else [(sT, st1m[8]["mm"])]
                war_mm = ([(sV, st1b[st - 1]["rows"]), (sS, st1b[st - 1]["rstd"])]
                          if st >= 1 else [])
                st1b[st] = ln_stage1(
                    xst[st % 3], 256, 0, 256, cb, 256, sq, ps[4], ps[4],
                    [v_y0], war_cast, war_cast, war_mm, sq_col=256,
                    sq0_wait=[stt_part[(st, 3)]], sq1_wait=[stt_part[(st, 7)]])
                if st >= 1:
                    b_bc(st - 1)
                if st >= 2:
                    b_apply(st - 2)
            b_rows(7)
            b_bc(7)
            b_apply(6)
            b_apply(7)
            v_y1_done = st2b[7]["apply"]
            if DEBUG:
                allw = [(sT, sT.n), (sV, sV.n), (sS, sS.n), (sG, sG.n)]
                for name, t, cview in [
                    ("dbg_xn", xn, ("(c p) t -> p c t", TW)),
                    ("dbg_h", hbuf, None),
                    ("dbg_gate", gate, ("(c p) t -> p c t", T)),
                    ("dbg_y1b", y1b, ("(c p) t -> p c t", T)),
                    ("dbg_bu", bu_sb, None),
                ]:
                    dst = dbg[name]
                    if cview is not None:
                        dst = dst.rearrange("(c p) t -> p c t", p=128)
                        srcv = t[:, :].rearrange("p (c t) -> p c t", c=DC)
                    else:
                        srcv = t[:, :]
                    P.op("Y", lambda y, d=dst, s=srcv: y.dma_start(d, s),
                         wait=allw, inc=(dDbg, 16))
            joinv = {}
            for eng, sem in ALLSEMS:
                P.op(eng, lambda e: e.nop(),
                     wait=[(sem, sem.n)] if eng in ("T", "Y") else [],
                     inc=(sem, 1))
                joinv[eng] = sem.n
            P.flush(nc)

        with ExitStack() as S2a:
            e2 = S2a.enter_context
            y2acc = e2(nc.sbuf_tensor("y2acc", [128, DC * T], F32))
            with ExitStack() as S2b:
                e2b = S2b.enter_context
                sgn1B = e2b(nc.sbuf_tensor("sgn1B", [128, DC * 1024], BF16))
                sgn2B = e2b(nc.sbuf_tensor("sgn2B", [128, DC * 1024], BF16))
                gbuf = [e2b(nc.sbuf_tensor(f"gbuf{i}", [128, DC * 512], BF16))
                        for i in range(2)]
                slabC = slabB

                for eng, _ in ALLSEMS:
                    P.op(eng, lambda e: e.nop(),
                         wait=[(osem, joinv[oeng]) for oeng, osem in ALLSEMS
                               if oeng != eng] +
                              ([(dDbg, dDbg.n)] if DEBUG else []))

                v_al2_fin = [0]
                s_b2a = [0]
                sgn1 = [sgn1A, sgn1B]
                sgn2 = [sgn2A, sgn2B]
                q_sign_done = [q0_subs[7]["sign"], None, None, None]
                q_sign1_done = [q0_subs[3]["sign"], None, None, None]
                q_alpha_done = [q0_subs[7]["alpha"], None, None, None]
                al1_ev = [al1_ev0, None, None, None]
                v_al2_ev = {0: v_al2_q0}
                gelu_hist = []
                ev5_hist = []
                for q in range(4):
                    s1q = sgn1[q % 2]
                    s2q = sgn2[q % 2]
                    guard = [(sT, sT.n), (sG, sG.n)]
                    prep_state["A"] = prep_state["A"] + guard + \
                        ([(sV, v_y1_done)] if q == 0 else [])
                    prep_state["B"] = prep_state["B"] + guard + [(sS, sS.n)] + \
                        ([(sV, v_y1_done)] if q == 0 else [])
                    subs = []
                    if q < 3:
                        for s_idx in range(4):
                            subs.append((1, s_idx))
                        for s_idx in range(4):
                            subs.append((2, s_idx))
                    sub_i = 0
                    last_sign = None
                    pending_alpha = []
                    prepped = []
                    gelu_tile_done = [None] * 4

                    def emit_m4(tile, s1q=s1q, q=q):
                        for hc in range(8):
                            ghc = q * 8 + hc
                            for c in range(DC):
                                wt = []
                                if c == 0:
                                    if len(gelu_hist) >= 2:
                                        wt.append((sS, gelu_hist[-2]))
                                    if hc == 0 and tile == 0:
                                        wt.append((sS, q_sign1_done[q]))
                                        wt.append((sV, v_y1_done))
                                P.op("T", lambda t, c=c, hc=hc, tile=tile: t.matmul(
                                    ps[hc % 2][0:128, 0:512],
                                    s1q[:, c * 1024 + hc * 128:c * 1024 + hc * 128 + 128],
                                    bass.AP(y1b, c * T + tile * 512, [[T * DC, 128], [1, 512]]),
                                    start=(c == 0), stop=(c == DC - 1)),
                                    wait=wt, inc=(sT, 1) if c == DC - 1 else None)
                            t_m4 = sT.n
                            gwt = [(sT, t_m4)]
                            if hc == 0 and tile == 0:
                                gwt.append((sS, al1_ev[q]))
                            ghc = q * 8 + hc
                            P.op("S", lambda s, hc=hc, ghc=ghc, tile=tile: s.activation(
                                gbuf[tile % 2][:, hc * 512:(hc + 1) * 512],
                                ps[hc % 2][0:128, 0:512],
                                AF.Gelu, bias=b1_sb[:, ghc:ghc + 1],
                                scale=al1_sb[:, ghc:ghc + 1]),
                                wait=gwt, inc=(sS, 1))
                            gelu_hist.append(sS.n)
                        gelu_tile_done[tile] = sS.n

                    def emit_m5(tile, s2q=s2q, q=q):
                        for c in range(DC):
                            for hc in range(8):
                                wt = []
                                if hc == 0:
                                    wt = [(sS, gelu_tile_done[tile])]
                                    if tile == 0 and c == 0:
                                        wt.append((sS, q_sign_done[q]))
                                    if len(ev5_hist) >= 2:
                                        wt.append((sV, ev5_hist[-2]))
                                P.op("T", lambda t, c=c, hc=hc, tile=tile: t.matmul(
                                    ps[2 + c % 2][0:128, 0:512],
                                    s2q[:, hc * 1024 + c * 128:hc * 1024 + c * 128 + 128],
                                    gbuf[tile % 2][:, hc * 512:(hc + 1) * 512],
                                    start=(hc == 0), stop=(hc == 7)),
                                    wait=wt, inc=(sT, 1) if hc == 7 else None)
                            t_m5 = sT.n
                            if q == 0:
                                P.op("V", lambda v, c=c, tile=tile: v.tensor_copy(
                                    y2acc[:, c * T + tile * 512:c * T + tile * 512 + 512],
                                    ps[2 + c % 2][0:128, 0:512]),
                                    wait=[(sT, t_m5)], inc=(sV, 1))
                            elif q < 3:
                                P.op("V", lambda v, c=c, tile=tile: v.tensor_add(
                                    y2acc[:, c * T + tile * 512:c * T + tile * 512 + 512],
                                    y2acc[:, c * T + tile * 512:c * T + tile * 512 + 512],
                                    ps[2 + c % 2][0:128, 0:512]),
                                    wait=[(sT, t_m5)], inc=(sV, 1))
                            else:
                                bw = [(sV, s_b2a[0])] if c == 0 else []
                                P.op("V", lambda v, c=c, tile=tile: v.scalar_tensor_tensor(
                                    y2acc[:, c * T + tile * 512:c * T + tile * 512 + 512],
                                    ps[2 + c % 2][0:128, 0:512],
                                    b2a_sb[:, c:c + 1],
                                    y2acc[:, c * T + tile * 512:c * T + tile * 512 + 512],
                                    ALU.add, ALU.add),
                                    wait=[(sT, t_m5)] + bw, inc=(sV, 1))
                            ev5_hist.append(sV.n)

                    if q == 3:
                        P.op("S", lambda s: s.activation(
                            al2_sb[:, :], al2acc[:, :], AF.Identity,
                            scale=cst[:, 3:4]),
                            wait=[(sV, v_al2_fin[0])], inc=(sS, 1))
                        P.op("S", lambda s: s.activation(
                            b2a_sb[:, :], al2_sb[:, :], AF.Ln))
                        P.op("S", lambda s: s.activation(
                            b2a_sb[:, :], b2a_sb[:, :], AF.Exp,
                            scale=cst[:, 4:5]), inc=(sS, 1))
                        P.op("V", lambda v: v.tensor_mul(
                            b2a_sb[:, :], b2a_sb[:, :], b2_sb[:, :]),
                            wait=[(sS, sS.n)], inc=(sV, 1))
                        s_b2a[0] = sV.n
                    a1b = ps[(6, 7, 6)[q]] if q < 3 else None
                    a2b = ps[(4, 5, 4)[q]] if q < 3 else None
                    for tile in range(4):
                        for th in pending_alpha:
                            th()
                        pending_alpha = []
                        if (q < 3 and al1_ev[q + 1] is None and
                                len(prepped) >= 4 and "alpha" in prepped[3]):
                            P.op("S", lambda s, q=q, a1b=a1b: s.activation(
                                al1_sb[:, (q + 1) * 8:(q + 2) * 8],
                                a1b[0:128, 8:16],
                                AF.Identity, scale=cst[:, 2:3]),
                                wait=[(sT, prepped[3]["alpha"])], inc=(sS, 1))
                            al1_ev[q + 1] = sS.n
                        for _ in range(2):
                            if sub_i < len(subs):
                                wnum, s_idx = subs[sub_i]
                                par = sub_i % 2
                                msp = emit_prep_subload(
                                    q + 1, wnum, s_idx, sgn1[(q + 1) % 2],
                                    sgn2[(q + 1) % 2],
                                    a1b if wnum == 1 else a2b,
                                    first_al=(s_idx == 0), last_al=(s_idx == 3),
                                    slabT=[slab, slabC][par], wkey="AB"[par],
                                    dsem=dWp[0] if par == 0 else dWb,
                                    split_epochs=True)
                                last_sign = msp["sign"]
                                prepped.append(msp)
                                pending_alpha.append(msp["alpha_thunk"])
                                sub_i += 1
                        emit_m4(tile)
                        if tile >= 1:
                            emit_m5(tile - 1)
                    for th in pending_alpha:
                        th()
                    pending_alpha = []
                    emit_m5(3)
                    if q < 3:
                        q_sign_done[q + 1] = last_sign
                        q_sign1_done[q + 1] = prepped[3]["sign"]
                        q_alpha_done[q + 1] = prepped[-1]["alpha"]
                        if al1_ev[q + 1] is None:
                            P.op("S", lambda s, q=q, a1b=a1b: s.activation(
                                al1_sb[:, (q + 1) * 8:(q + 2) * 8],
                                a1b[0:128, 8:16],
                                AF.Identity, scale=cst[:, 2:3]),
                                wait=[(sT, prepped[3]["alpha"])], inc=(sS, 1))
                            al1_ev[q + 1] = sS.n
                        P.op("V", lambda v, a2b=a2b: v.tensor_add(
                            al2acc[:, :], al2acc[:, :], a2b[0:128, 0:8]),
                            wait=[(sT, prepped[-1]["alpha"])], inc=(sV, 1))
                        v_al2_ev[q + 1] = sV.n
                        if q == 2:
                            v_al2_fin[0] = sV.n
                if DEBUG:
                    allw = [(sT, sT.n), (sV, sV.n), (sS, sS.n)]
                    P.op("Y", lambda y: y.dma_start(
                        dbg["dbg_y2acc"].rearrange("(c p) t -> p c t", p=128),
                        y2acc[:, :].rearrange("p (c t) -> p c t", c=DC)),
                        wait=allw, inc=(dDbg, 16))
                    P.op("Y", lambda y: y.dma_start(dbg["dbg_al1"], al1_sb[:, :]),
                         wait=allw, inc=(dDbg, 16))
                    P.op("Y", lambda y: y.dma_start(dbg["dbg_al2"], al2_sb[:, :]),
                         wait=allw, inc=(dDbg, 16))

                joinv2 = {}
                for eng, sem in ALLSEMS:
                    P.op(eng, lambda e: e.nop(),
                         wait=[(sem, sem.n)] if eng in ("T", "Y") else [],
                         inc=(sem, 1))
                    joinv2[eng] = sem.n
                P.flush(nc)

            with ExitStack() as S2c:
                e2c = S2c.enter_context
                y2f = [e2c(nc.sbuf_tensor(f"y2f{i}", [128, DC * 256], F32))
                       for i in range(3)]
                outt = [e2c(nc.sbuf_tensor(f"outt{i}", [128, DC * 256], F32))
                        for i in range(2)]
                cb3 = e2c(nc.sbuf_tensor("cb3", [128, DC * 256], BF16))
                sq3 = e2c(nc.sbuf_tensor("sq3", [128, DC * 128], BF16))
                rows3 = e2c(nc.sbuf_tensor("rows3", [1, 4 * 256], F32))
                ones_f = e2c(nc.sbuf_tensor("ones_f", [1, 128], F32))
                P.op("V", lambda v: v.memset(ones_f[:, :], 1.0))

                for eng, _ in ALLSEMS:
                    P.op(eng, lambda e: e.nop(),
                         wait=[(osem, joinv2[oeng]) for oeng, osem in ALLSEMS
                               if oeng != eng])

                st1c = {}
                st2c = {}
                st_out = {}
                R3 = [i * 256 for i in range(4)]

                def c_rows(st):
                    m = st1c[st]
                    war = [(sT, st1c[st - 1]["bc"])] if st >= 1 else []
                    P.op("V", lambda v: v.tensor_scalar(
                        rows3[:, R3[0]:R3[0] + 256], ps[st % 2][0:1, 0:256],
                        1.0 / D, None, ALU.mult), wait=[(sT, m["mm"])] + war)
                    P.op("V", lambda v: v.tensor_scalar(
                        rows3[:, R3[1]:R3[1] + 256], ps[2 + st % 2][0:1, 0:256],
                        1.0 / D, None, ALU.mult))
                    P.op("V", lambda v: v.tensor_mul(
                        rows3[:, R3[2]:R3[2] + 256], rows3[:, R3[0]:R3[0] + 256],
                        rows3[:, R3[0]:R3[0] + 256]))
                    P.op("V", lambda v: v.tensor_sub(
                        rows3[:, R3[1]:R3[1] + 256], rows3[:, R3[1]:R3[1] + 256],
                        rows3[:, R3[2]:R3[2] + 256]))
                    m["rows"] = sV.n
                    P.op("S", lambda s: s.activation(
                        rows3[:, R3[2]:R3[2] + 256], rows3[:, R3[1]:R3[1] + 256],
                        AF.Ln, bias=cst[0:1, 0:1]), wait=[(sV, m["rows"])])
                    P.op("S", lambda s: s.activation(
                        rows3[:, R3[3]:R3[3] + 256], rows3[:, R3[2]:R3[2] + 256],
                        AF.Exp, scale=cst[0:1, 1:2]))
                    m["rstd"] = sS.n

                def c_bc(st):
                    m = st1c[st]
                    bcp = [ps[4], ps[5]] if st % 2 == 0 else [ps[6], ps[7]]
                    war = [(sV, st2c[st - 2]["apply"])] if st >= 2 else []
                    P.op("T", lambda t, st=st: t.matmul(
                        bcp[0][0:128, 0:256], ones_f[:, :], rows3[0:1, R3[0]:R3[0] + 256],
                        start=True, stop=True), wait=[(sS, m["rstd"])] + war)
                    P.op("T", lambda t, st=st: t.matmul(
                        bcp[1][0:128, 0:256], ones_f[:, :], rows3[0:1, R3[3]:R3[3] + 256],
                        start=True, stop=True), inc=(sT, 1))
                    m["bc"] = sT.n
                    m["bcmu"] = bcp[0]
                    m["bcr"] = bcp[1]

                def c_apply(st):
                    war = [st_out[st - 2]] if st >= 2 else []
                    ot = outt[st % 2]
                    out_ap = bass.AP(ot, 0, [[256 * DC, 128], [256, DC], [1, 256]])
                    m = st1c[st]
                    sview = bass.AP(y2f[st % 3], 0, [[256 * DC, 128], [256, DC], [1, 256]])
                    P.op("V", lambda v: v.tensor_sub(
                        sview, sview,
                        bass.AP(m["bcmu"], 0, [[512, 128], [0, DC], [1, 256]])),
                        wait=[(sT, m["bc"])] + war)
                    P.op("V", lambda v, ot=ot: v.tensor_mul(
                        out_ap, sview,
                        bass.AP(m["bcr"], 0, [[512, 128], [0, DC], [1, 256]])))
                    st2c[st] = {"apply": sV.n}
                    osem = dO if st % 2 == 0 else dO1
                    P.op("Y", lambda y, st=st, ot=ot: y.dma_start(
                        outTd[:, st * 2048:(st + 1) * 2048], ot[:, :]),
                        wait=[(sV, st2c[st]["apply"])], inc=(osem, 16))
                    st_out[st] = (osem, osem.n)

                for st in range(8):
                    yf = y2f[st % 3]
                    if st >= 2:
                        c_apply(st - 2)
                    if st >= 1:
                        c_rows(st - 1)
                    for c in range(DC):
                        swt = []
                        if c == 0 and st >= 3:
                            swt.append((sV, st2c[st - 3]["apply"]))
                        P.op("V", lambda v, c=c, st=st, yf=yf: v.scalar_tensor_tensor(
                            yf[:, c * 256:(c + 1) * 256],
                            y2acc[:, c * T + st * 256:c * T + st * 256 + 256],
                            al2_sb[:, c:c + 1],
                            bass.AP(y1b, c * T + st * 256, [[T * DC, 128], [1, 256]]),
                            ALU.mult, ALU.add),
                            wait=swt, inc=(sV, 1) if c == DC - 1 else None)
                    v_y2f = sV.n
                    if st >= 1:
                        c_bc(st - 1)
                    war_cast = [(sT, st1c[st - 1]["mm"])] if st >= 1 else []
                    war_mm = ([(sV, st1c[st - 1]["rows"]), (sS, st1c[st - 1]["rstd"])]
                              if st >= 1 else [])
                    st1c[st] = ln_stage1(
                        yf, 256, 0, 256, cb3, 256, sq3,
                        ps[st % 2], ps[2 + st % 2],
                        [(sV, v_y2f)], war_cast, war_cast, war_mm)
                c_rows(7)
                c_bc(7)
                c_apply(6)
                c_apply(7)
                P.op("Y", lambda y: y.nop(),
                     wait=[(dO, dO.n), (dO1, dO1.n)] +
                          ([(dDbg, dDbg.n)] if DEBUG else []))
                P.flush(nc)

    return nc


def kernel(**inputs):
    x = inputs["L0_attn_x"]
    f = np.float32
    wgT = np.asarray(inputs["L0_attn_Wg"]).T.astype(f)
    wg_host = np.ascontiguousarray(
        wgT.reshape(DC, 128, DC, 128).transpose(1, 2, 0, 3).reshape(128, DC * D))
    w1T = np.asarray(inputs["L0_ffn_w1"]).T.astype(f)
    w1_host = np.ascontiguousarray(
        w1T.reshape(DC, 128, 16, 256).transpose(1, 2, 0, 3).reshape(128, 16 * 2048))
    w2T = np.asarray(inputs["L0_ffn_w2"]).T.astype(f)
    w2_host = np.ascontiguousarray(
        w2T.reshape(32, 128, 1024).transpose(1, 0, 2).reshape(128, 32 * 1024))
    common = {
        "wsT": np.ascontiguousarray(inputs["L0_attn_Ws"].T, dtype=f),
        "wiT": np.ascontiguousarray(inputs["L0_attn_Wi"].T, dtype=f),
        "a_t": np.ascontiguousarray(inputs["L0_attn_A"].T, dtype=f),
        "i16": np.eye(NST, dtype=f),
        "woT": np.ascontiguousarray(inputs["L0_attn_Wo"].T, dtype=f),
        "wgT": wg_host,
        "w1T": w1_host,
        "w2T": w2_host,
        "lnw8": np.ascontiguousarray(np.asarray(inputs["L0_attn_ln_w"]).reshape(DC, 128).T, dtype=f),
        "lnb8": np.ascontiguousarray(np.asarray(inputs["L0_attn_ln_b"]).reshape(DC, 128).T, dtype=f),
        "bs_c": np.ascontiguousarray(np.asarray(inputs["L0_attn_bs"]).reshape(NST, 1), dtype=f),
        "bi_c": np.ascontiguousarray(np.asarray(inputs["L0_attn_bi"]).reshape(NST, 1), dtype=f),
        "bo8": np.ascontiguousarray(np.asarray(inputs["L0_attn_bo"]).reshape(DC, 128).T, dtype=f),
        "bg8": np.ascontiguousarray(np.asarray(inputs["L0_attn_bg"]).reshape(DC, 128).T, dtype=f),
        "b18": np.ascontiguousarray(np.asarray(inputs["L0_ffn_b1"]).reshape(32, 128).T, dtype=f),
        "b28": np.ascontiguousarray(np.asarray(inputs["L0_ffn_b2"]).reshape(DC, 128).T, dtype=f),
    }
    x = np.asarray(x)
    in_maps = []
    for core in range(8):
        b, q = core // 4, core % 4
        start = q * T
        xcore = np.zeros((D, TW), dtype=f)
        xcore[:, KPRE:] = x[b, start:start + T, :].T
        if q > 0:
            xcore[:, :KPRE] = x[b, start - KPRE:start, :].T
        x8 = xcore.reshape(DC, 128, TW)
        blocks = [x8[:, :, 0:KPRE].transpose(1, 0, 2).reshape(128, DC * KPRE)]
        for g in range(8):
            blocks.append(x8[:, :, KPRE + 256 * g:KPRE + 256 * (g + 1)]
                          .transpose(1, 0, 2).reshape(128, DC * 256))
        m = dict(common)
        m["xh"] = np.ascontiguousarray(np.concatenate(blocks, axis=1))
        m["maskh"] = (np.zeros if q == 0 else np.ones)((NST, KPRE), dtype=f)
        in_maps.append(m)

    nc = build_program()
    res = run_bass_kernel_spmd(nc, in_maps, core_ids=list(range(8)))

    out = np.empty((2, 8192, D), dtype=np.float32)
    for core in range(8):
        b, q = core // 4, core % 4
        o = res.results[core]["outT"].reshape(128, 8, DC, 256)
        out[b, q * T:(q + 1) * T, :] = o.transpose(1, 3, 2, 0).reshape(T, D)
    return out


# version 40
# speedup vs baseline: 1.1943x; 1.0051x over previous
import numpy as np
import ml_dtypes
from contextlib import ExitStack

import concourse.bass as bass
import concourse.mybir as mybir
from concourse.bass_utils import run_bass_kernel_spmd

F32 = mybir.dt.float32
BF16 = mybir.dt.bfloat16
AF = mybir.ActivationFunctionType
ALU = mybir.AluOpType

DEBUG = False
SELF_EDGES = True
D = 1024
DC = 8
NST = 16
HID = 4096
T = 2048
KPRE = 24
TW = T + KPRE
CH = 16
NCHAIN = T // CH
STEPS = KPRE + CH
EPS = 1e-5


class Sem:
    def __init__(self, stack, nc, name):
        self.h = stack.enter_context(nc.semaphore(name))
        self.n = 0


class Plan:

    def __init__(self, own=None):
        self.items = []
        self.own = own or {}

    def op(self, eng, fn, wait=(), inc=None):
        own = self.own.get(eng)
        selfwait = None
        if own is not None and (inc is None or inc[0] is own):
            if SELF_EDGES:
                selfwait = (own, own.n)
            inc = (own, 1)
        self.items.append((eng, fn, list(wait), inc, selfwait))
        if inc is not None:
            sem, k = inc
            sem.n += k
            return sem.n
        return None

    def run(self, engine, name):
        for eng, fn, waits, inc, selfwait in self.items:
            if eng != name:
                continue
            if selfwait is not None and selfwait[1] > 0:
                engine.wait_ge(selfwait[0].h, selfwait[1])
            for sem, val in waits:
                engine.wait_ge(sem.h, val)
            inst = fn(engine)
            if inc is not None:
                sem, k = inc
                inst.then_inc(sem.h, k)

    def flush(self, nc):
        with nc.Block() as blk:
            blk.tensor(lambda e: self.run(e, "T"))
            blk.vector(lambda e: self.run(e, "V"))
            blk.scalar(lambda e: self.run(e, "S"))
            blk.gpsimd(lambda e: self.run(e, "G"))
            blk.sync(lambda e: self.run(e, "Y"))
        self.items = []


def sv(t, pn, col, step, count):
    return bass.AP(t, col, [[t.shape[-1], pn], [step, count]])


def build_program():
    nc = bass.Bass("TRN2", target_bir_lowering=False)

    xhd = nc.dram_tensor("xh", [128, DC * TW], F32, kind="ExternalInput").ap()
    maskh = nc.dram_tensor("maskh", [NST, KPRE], F32, kind="ExternalInput").ap()
    wsTd = nc.dram_tensor("wsT", [D, NST], F32, kind="ExternalInput").ap()
    wiTd = nc.dram_tensor("wiT", [D, NST], F32, kind="ExternalInput").ap()
    a_td = nc.dram_tensor("a_t", [NST, NST], F32, kind="ExternalInput").ap()
    i16d = nc.dram_tensor("i16", [NST, NST], F32, kind="ExternalInput").ap()
    woTd = nc.dram_tensor("woT", [NST, D], F32, kind="ExternalInput").ap()
    wgTd = nc.dram_tensor("wgT", [128, DC * D], F32, kind="ExternalInput").ap()
    w1Td = nc.dram_tensor("w1T", [128, 16 * 2048], F32, kind="ExternalInput").ap()
    w2Td = nc.dram_tensor("w2T", [128, 32 * 1024], F32, kind="ExternalInput").ap()
    lnw8d = nc.dram_tensor("lnw8", [128, DC], F32, kind="ExternalInput").ap()
    lnb8d = nc.dram_tensor("lnb8", [128, DC], F32, kind="ExternalInput").ap()
    bscd = nc.dram_tensor("bs_c", [NST, 1], F32, kind="ExternalInput").ap()
    bicd = nc.dram_tensor("bi_c", [NST, 1], F32, kind="ExternalInput").ap()
    bo8d = nc.dram_tensor("bo8", [128, DC], F32, kind="ExternalInput").ap()
    bg8d = nc.dram_tensor("bg8", [128, DC], F32, kind="ExternalInput").ap()
    b18d = nc.dram_tensor("b18", [128, 32], F32, kind="ExternalInput").ap()
    b28d = nc.dram_tensor("b28", [128, DC], F32, kind="ExternalInput").ap()
    outTd = nc.dram_tensor("outT", [128, 4 * 4096], F32, kind="ExternalOutput").ap()
    if DEBUG:
        dbg = {k: nc.dram_tensor(k, shp, dt, kind="ExternalOutput").ap()
               for k, shp, dt in [
                   ("dbg_xn", [D, TW], BF16), ("dbg_u", [NST, TW], BF16),
                   ("dbg_h", [NST, TW], BF16), ("dbg_gate", [D, T], BF16),
                   ("dbg_y1b", [D, T], BF16), ("dbg_y2acc", [D, T], F32),
                   ("dbg_al1", [128, 32], F32), ("dbg_al2", [128, DC], F32),
                   ("dbg_bgp", [128, DC], F32), ("dbg_bu", [NST, 1], F32),
               ]}

    wsT3 = wsTd.rearrange("(c p) n -> p c n", p=128)
    wiT3 = wiTd.rearrange("(c p) n -> p c n", p=128)
    XOF = lambda g: 8 * KPRE + (g - 1) * 2048 if g > 0 else 0

    with ExitStack() as S0:
        e0 = S0.enter_context
        dC = Sem(S0, nc, "dC")
        dXp = [Sem(S0, nc, f"dX{i}") for i in range(3)]
        dWp = [Sem(S0, nc, "dW0"), Sem(S0, nc, "dW1")]
        dWb = Sem(S0, nc, "dWb")
        dO = Sem(S0, nc, "dO")
        dO1 = Sem(S0, nc, "dO1")
        dDbg = Sem(S0, nc, "dDbg")
        sT = Sem(S0, nc, "sT")
        sV = Sem(S0, nc, "sV")
        sS = Sem(S0, nc, "sS")
        sG = Sem(S0, nc, "sG")
        sY = Sem(S0, nc, "sY")
        ALLSEMS = [("T", sT), ("V", sV), ("S", sS), ("G", sG), ("Y", sY)]
        P = Plan(own={"V": sV, "S": sS, "G": sG})


        lnw_sb = e0(nc.sbuf_tensor("lnw_sb", [128, DC], F32))
        lnb_sb = e0(nc.sbuf_tensor("lnb_sb", [128, DC], F32))
        bs_sb = e0(nc.sbuf_tensor("bs_sb", [NST, 1], F32))
        bi_sb = e0(nc.sbuf_tensor("bi_sb", [NST, 1], F32))
        bu_sb = e0(nc.sbuf_tensor("bu_sb", [NST, 1], F32))
        bo_sb = e0(nc.sbuf_tensor("bo_sb", [128, DC], F32))
        bgr_sb = e0(nc.sbuf_tensor("bgr_sb", [128, DC], F32))
        bgp_sb = e0(nc.sbuf_tensor("bgp_sb", [128, DC], F32))
        b1_sb = e0(nc.sbuf_tensor("b1_sb", [128, 32], F32))
        b2_sb = e0(nc.sbuf_tensor("b2_sb", [128, DC], F32))
        al1_sb = e0(nc.sbuf_tensor("al1_sb", [128, 32], F32))
        al2_sb = e0(nc.sbuf_tensor("al2_sb", [128, DC], F32))
        al2acc = e0(nc.sbuf_tensor("al2acc", [128, DC], F32))

        cst = e0(nc.sbuf_tensor("cst", [128, 5], F32))
        b2a_sb = e0(nc.sbuf_tensor("b2a_sb", [128, DC], F32))
        ones_b = e0(nc.sbuf_tensor("ones_b", [128, 1], BF16))
        ones_f = None
        y1b = e0(nc.sbuf_tensor("y1b", [128, DC * T], BF16))
        slab = e0(nc.sbuf_tensor("slab", [128, DC * 256], F32))
        slabB = e0(nc.sbuf_tensor("slabB", [128, 2048], F32))
        sgn1A = e0(nc.sbuf_tensor("sgn1A", [128, DC * 1024], BF16))
        sgn2A = e0(nc.sbuf_tensor("sgn2A", [128, DC * 1024], BF16))
        absq = [e0(nc.sbuf_tensor(f"absq{i}", [128, 2048], BF16)) for i in range(2)]
        ps = [e0(nc.psum_tensor(f"ps{i}", [128, 512], F32)) for i in range(8)]

        def ln_stage1(src, gw, off, glen, cbuf, cw, sqbuf, ps_sx, ps_sq,
                      src_wait, war_cast, war_sq, war_mm=(), sq_col=0,
                      cast_eng="V", sq_eng="S"):
            merged = ps_sq is ps_sx
            cast_sem = {"V": sV, "G": sG}[cast_eng]
            m = {}
            P.op(cast_eng, lambda v: v.tensor_copy(
                bass.AP(cbuf, 0, [[cw * DC, 128], [cw, DC], [1, glen]]),
                bass.AP(src, off, [[gw * DC, 128], [gw, DC], [1, glen]])),
                wait=src_wait + war_cast,
                inc=(cast_sem, 1) if cast_eng != "V" else None)
            m["cast"] = cast_sem.n
            m["cast_sem"] = cast_sem
            hw = DC // 2
            sq_sem = {"S": sS, "V": sV}[sq_eng]
            if sq_eng == "V":
                P.op("V", lambda v: v.tensor_mul(
                    bass.AP(sqbuf, 0, [[cw * hw, 128], [cw, hw], [1, glen]]),
                    bass.AP(cbuf, 0, [[cw * DC, 128], [cw, hw], [1, glen]]),
                    bass.AP(cbuf, 0, [[cw * DC, 128], [cw, hw], [1, glen]])),
                    wait=war_sq)
            else:
                P.op("S", lambda s: s.activation(
                    bass.AP(sqbuf, 0, [[cw * hw, 128], [cw, hw], [1, glen]]),
                    bass.AP(src, off, [[gw * DC, 128], [gw, hw], [1, glen]]),
                    AF.Square), wait=src_wait + war_sq)
            m["sq0"] = sq_sem.n
            for c in range(DC):
                P.op("T", lambda t, c=c: t.matmul(
                    ps_sx[0:1, 0:glen], ones_b[:, :], cbuf[:, c * cw:c * cw + glen],
                    start=(c == 0), stop=(c == DC - 1 and not merged)),
                    wait=([(m["cast_sem"], m["cast"])] + list(war_mm)) if c == 0 else [])
            for c in range(hw):
                P.op("T", lambda t, c=c: t.matmul(
                    ps_sq[0:1, sq_col:sq_col + glen], ones_b[:, :],
                    sqbuf[:, c * cw:c * cw + glen],
                    start=(c == 0 and not merged), stop=False),
                    wait=[(sq_sem, m["sq0"])] if c == 0 else [],
                    inc=(sT, 1) if c == hw - 1 else None)
            t_half = sT.n
            if sq_eng == "V":
                P.op("V", lambda v: v.tensor_mul(
                    bass.AP(sqbuf, 0, [[cw * hw, 128], [cw, hw], [1, glen]]),
                    bass.AP(cbuf, hw * cw, [[cw * DC, 128], [cw, hw], [1, glen]]),
                    bass.AP(cbuf, hw * cw, [[cw * DC, 128], [cw, hw], [1, glen]])),
                    wait=[(sT, t_half)])
            else:
                P.op("S", lambda s: s.activation(
                    bass.AP(sqbuf, 0, [[cw * hw, 128], [cw, hw], [1, glen]]),
                    bass.AP(src, off + hw * gw, [[gw * DC, 128], [gw, hw], [1, glen]]),
                    AF.Square), wait=[(sT, t_half)])
            m["sq1"] = sq_sem.n
            for c in range(hw):
                P.op("T", lambda t, c=c: t.matmul(
                    ps_sq[0:1, sq_col:sq_col + glen], ones_b[:, :],
                    sqbuf[:, c * cw:c * cw + glen],
                    start=False, stop=(c == hw - 1)),
                    wait=[(sq_sem, m["sq1"])] if c == 0 else [],
                    inc=(sT, 1) if c == hw - 1 else None)
            m["mm"] = sT.n
            return m

        class PsumRows:

            def __init__(self, bank, rw):
                self.bank = bank
                self.rw = rw

            def __getitem__(self, key):
                _, colsl = key
                a = colsl.start
                slot = a // self.rw
                off = a - slot * self.rw
                n = colsl.stop - a
                p = 32 * (slot + 1)
                return bass.AP(self.bank, p * 512 + off, [[512, 1], [1, n]])

        def ln_rows_bf(m, glen, ps_sx, ps_sq, war_rows=(), sq_col=0):
            mu = bass.AP(rows1f, 0, [[512, 1], [1, glen]])
            mu2 = bass.AP(rows1f, 256, [[512, 1], [1, glen]])
            slotA = bass.AP(ps_sx, 32 * 512, [[512, 1], [1, glen]])
            slotB = bass.AP(ps_sx, 64 * 512, [[512, 1], [1, glen]])
            P.op("V", lambda v: v.tensor_scalar(
                mu, ps_sx[0:1, 0:glen], 1.0 / D, None, ALU.mult),
                wait=[(sT, m["mm"])] + list(war_rows))
            P.op("V", lambda v: v.tensor_scalar(
                slotA, ps_sq[0:1, sq_col:sq_col + glen], 1.0 / D, None, ALU.mult))
            P.op("V", lambda v: v.tensor_mul(mu2, mu, mu))
            P.op("V", lambda v: v.tensor_sub(slotA, slotA, mu2))
            P.op("V", lambda v: v.tensor_copy(rows1b[0:1, 0:glen], mu))
            m["rows"] = sV.n
            P.op("S", lambda s: s.activation(
                slotB, slotA, AF.Ln, bias=cst[0:1, 0:1]),
                wait=[(sV, m["rows"])])
            P.op("S", lambda s: s.activation(
                rows1b[0:1, 256:256 + glen], slotB,
                AF.Exp, scale=cst[0:1, 1:2]))
            m["rstd"] = sS.n
            return m

        def ln_bc_bf(m, glen, ps_bcmu, ps_bcr, war_bc):
            P.op("T", lambda t: t.matmul(
                ps_bcmu[0:128, 0:glen], ones_1b[:, :], rows1b[0:1, 0:glen],
                start=True, stop=True), wait=[(sS, m["rstd"])] + list(war_bc))
            P.op("T", lambda t: t.matmul(
                ps_bcr[0:128, 0:glen], ones_1b[:, :], rows1b[0:1, 256:256 + glen],
                start=True, stop=True), inc=(sT, 1))
            m["bc"] = sT.n
            m["bcmu"] = ps_bcmu
            m["bcr"] = ps_bcr
            return m

        def ln_stage2a(m, glen, rowsT, rw, ps_sx, ps_sq, ps_bcmu, ps_bcr, war_bc,
                       war_rows=(), bf_bc=None, sq_col=0):
            if bf_bc is not None:
                ln_rows_bf(m, glen, ps_sx, ps_sq, war_rows, sq_col=sq_col)
                ln_bc_bf(m, glen, ps_bcmu, ps_bcr, war_bc)
                m["bcmu"] = ps_bcmu
                m["bcr"] = ps_bcr
                return m
            else:
                R = [i * rw for i in range(4)]
                P.op("V", lambda v: v.tensor_scalar(
                    rowsT[:, R[0]:R[0] + glen], ps_sx[0:1, 0:glen], 1.0 / D, None, ALU.mult),
                    wait=[(sT, m["mm"])] + list(war_rows))
                P.op("V", lambda v: v.tensor_scalar(
                    rowsT[:, R[1]:R[1] + glen], ps_sq[0:1, 0:glen], 1.0 / D, None, ALU.mult))
                P.op("V", lambda v: v.tensor_mul(
                    rowsT[:, R[2]:R[2] + glen], rowsT[:, R[0]:R[0] + glen],
                    rowsT[:, R[0]:R[0] + glen]))
                P.op("V", lambda v: v.tensor_sub(
                    rowsT[:, R[1]:R[1] + glen], rowsT[:, R[1]:R[1] + glen],
                    rowsT[:, R[2]:R[2] + glen]))
                m["rows"] = sV.n
                P.op("S", lambda s: s.activation(
                    rowsT[:, R[2]:R[2] + glen], rowsT[:, R[1]:R[1] + glen],
                    AF.Ln, bias=cst[0:1, 0:1]), wait=[(sV, m["rows"])])
                P.op("S", lambda s: s.activation(
                    rowsT[:, R[3]:R[3] + glen], rowsT[:, R[2]:R[2] + glen],
                    AF.Exp, scale=cst[0:1, 1:2]))
                m["rstd"] = sS.n
                P.op("T", lambda t: t.matmul(
                    ps_bcmu[0:128, 0:glen], ones_f[:, :], rowsT[0:1, R[0]:R[0] + glen],
                    start=True, stop=True), wait=[(sS, m["rstd"])] + war_bc)
                P.op("T", lambda t: t.matmul(
                    ps_bcr[0:128, 0:glen], ones_f[:, :], rowsT[0:1, R[3]:R[3] + glen],
                    start=True, stop=True), inc=(sT, 1))
            m["bc"] = sT.n
            m["bcmu"] = ps_bcmu
            m["bcr"] = ps_bcr
            return m

        def ln_stage2b(m, src, gw, off, glen, out_ap, affine=None):
            sview = bass.AP(src, off, [[gw * DC, 128], [gw, DC], [1, glen]])
            P.op("V", lambda v: v.tensor_sub(
                sview, sview,
                bass.AP(m["bcmu"], 0, [[512, 128], [0, DC], [1, glen]])),
                wait=[(sT, m["bc"])])
            P.op("V", lambda v: v.tensor_mul(
                out_ap, sview,
                bass.AP(m["bcr"], 0, [[512, 128], [0, DC], [1, glen]])))
            if affine is not None:
                dst, col = affine
                for c in range(DC):
                    P.op("V", lambda v, c=c: v.tensor_scalar(
                        bass.AP(dst, c * TW + col, [[TW * DC, 128], [1, glen]]),
                        cb[:, c * 256:c * 256 + glen],
                        lnw_sb[:, c:c + 1], lnb_sb[:, c:c + 1],
                        ALU.mult, ALU.add))
            m["apply"] = sV.n
            return m

        def ln_stage2(m, src, gw, off, glen, rowsT, rw, ps_sx, ps_sq,
                      ps_bcmu, ps_bcr, out_ap, war_bc):
            ln_stage2a(m, glen, rowsT, rw, ps_sx, ps_sq, ps_bcmu, ps_bcr, war_bc)
            return ln_stage2b(m, src, gw, off, glen, out_ap)

        prep_state = {"A": [], "B": [], "C": [], "prev_alpha": 0, "absq_i": 0}

        def emit_prep_subload(q, wnum, s_idx, sg1, sg2, al_ps, first_al,
                              last_al, slabT, wkey, dsem, split=False):
            ms = {}
            sb = (q * 4 + s_idx)
            if wnum == 1:
                srcd = w1Td[:, sb * 2048:(sb + 1) * 2048]
                sdst = bass.AP(sg1, s_idx * 256,
                               [[1024 * DC, 128], [1024, DC], [1, 256]])
            else:
                srcd = w2Td[:, (q * 8 + 2 * s_idx) * 1024:(q * 8 + 2 * s_idx + 2) * 1024]
                sdst = bass.AP(sg2, 2 * s_idx * 1024, [[1024 * DC, 128], [1, 2048]])
            ms["dma"] = P.op("Y", lambda y: y.dma_start(slabT[:, 0:2048], srcd),
                             wait=prep_state[wkey], inc=(dsem, 16))

            def sig_thunk():
                P.op("S", lambda s: s.activation(sdst, slabT[:, 0:2048], AF.Sign),
                     wait=[(dsem, ms["dma"])], inc=(sS, 1))
                ms["sign"] = sS.n
                ab = absq[prep_state["absq_i"] % 2]
                prep_state["absq_i"] += 1
                ms["ab"] = ab
                P.op("G", lambda g: g.tensor_mul(ab[:, 0:2048], slabT[:, 0:2048], sdst),
                     wait=[(sS, ms["sign"]), (sT, prep_state["prev_alpha"])],
                     inc=(sG, 1))
                ms["abs"] = sG.n
                prep_state[wkey] = [(sG, ms["abs"])]

            def alpha_thunk():
                ab = ms["ab"]
                if wnum == 1:
                    for hs in range(2):
                        hcm = 8 + s_idx * 2 + hs
                        for c in range(DC):
                            st_flag = first_al and hs == 0 and c == 0 and s_idx == 0
                            P.op("T", lambda t, c=c, hs=hs, hcm=hcm, st_flag=st_flag: t.matmul(
                                al_ps[0:128, hcm:hcm + 1],
                                ab[:, c * 256 + hs * 128:c * 256 + hs * 128 + 128],
                                ones_b[:, :], start=st_flag, stop=False),
                                wait=[(sG, ms["abs"])] if (c == 0 and hs == 0) else [],
                                inc=(sT, 1) if (c == DC - 1 and hs == 1) else None)
                else:
                    for hs in range(2):
                        for dsx in range(DC):
                            st_flag = first_al and hs == 0 and dsx == 0 and s_idx == 0
                            sp_flag = last_al and hs == 1 and dsx == DC - 1 and s_idx == 3
                            P.op("T", lambda t, hs=hs, dsx=dsx, st_flag=st_flag, sp_flag=sp_flag: t.matmul(
                                al_ps[0:128, dsx:dsx + 1],
                                ab[:, hs * 1024 + dsx * 128:hs * 1024 + dsx * 128 + 128],
                                ones_b[:, :], start=st_flag, stop=sp_flag),
                                wait=[(sG, ms["abs"])] if (dsx == 0 and hs == 0) else [],
                                inc=(sT, 1) if (dsx == DC - 1 and hs == 1) else None)
                ms["alpha"] = sT.n
                prep_state["prev_alpha"] = ms["alpha"]
            ms["sig_thunk"] = sig_thunk
            ms["alpha_thunk"] = alpha_thunk
            if not split:
                sig_thunk()
            return ms

        def emit_alpha_evacs(q, al_ps, al_wait, first_acc):
            P.op("S", lambda s, q=q: s.activation(
                al1_sb[:, q * 8:(q + 1) * 8], al_ps[0:128, 8:16],
                AF.Identity, scale=cst[:, 2:3]), wait=al_wait, inc=(sS, 1))
            ev1 = sS.n
            if first_acc:
                P.op("V", lambda v: v.tensor_copy(al2acc[:, :], al_ps[0:128, 0:8]),
                     wait=al_wait, inc=(sV, 1))
            else:
                P.op("V", lambda v: v.tensor_add(
                    al2acc[:, :], al2acc[:, :], al_ps[0:128, 0:8]),
                    wait=al_wait, inc=(sV, 1))
            return ev1, sV.n

        with ExitStack() as S1:
            e1 = S1.enter_context
            ws_f = slabB[:, 0:DC * NST]
            wi_f = slabB[:, DC * NST:2 * DC * NST]
            wsum_f = ws_f
            wsum_b = e1(nc.sbuf_tensor("wsum_b", [128, DC * NST], BF16))
            wo_b = e1(nc.sbuf_tensor("wo_b", [NST, D], BF16))
            xst = [e1(nc.sbuf_tensor(f"xst{i}", [128, DC * 256], F32)) for i in range(3)]
            xn = e1(nc.sbuf_tensor("xn", [128, DC * TW], BF16))
            gate = e1(nc.sbuf_tensor("gate", [128, DC * T], BF16))
            cb = e1(nc.sbuf_tensor("cb", [128, DC * 256], BF16))
            sq = e1(nc.sbuf_tensor("sq", [128, DC * 128], BF16))
            tmps = [e1(nc.sbuf_tensor(f"tmps{i}", [128, 256], F32)) for i in range(3)]
            tmpu = tmps[0]
            rows1f = e1(nc.sbuf_tensor("rows1f", [1, 2 * 256], F32))
            a_sb = bass.AP(slabB, 2 * DC * NST, [[2048, NST], [1, NST]])
            i_sb = bass.AP(slabB, 2 * DC * NST + NST, [[2048, NST], [1, NST]])
            mask_sb = bass.AP(tmps[1], 208, [[256, NST], [1, KPRE]])
            hbuf = e1(nc.sbuf_tensor("hbuf", [NST, TW], BF16))
            ubuf = e1(nc.sbuf_tensor("ubuf", [NST, TW], BF16))
            a_bf = e1(nc.sbuf_tensor("a_bf", [NST, NST], BF16))
            i_bf = e1(nc.sbuf_tensor("i_bf", [NST, NST], BF16))
            rows1b = e1(nc.sbuf_tensor("rows1b", [1, 2 * 256], BF16))
            ones_1b = e1(nc.sbuf_tensor("ones_1b", [1, 128], BF16))

            for dst, src in [
                (a_sb, a_td), (i_sb, i16d),
                (lnw_sb[:, :], lnw8d), (lnb_sb[:, :], lnb8d),
                (bs_sb[:, :], bscd), (bi_sb[:, :], bicd),
                (bo_sb[:, :], bo8d), (bgr_sb[:, :], bg8d),
                (b1_sb[:, :], b18d), (b2_sb[:, :], b28d),
                (mask_sb, maskh),
                (slab[0:NST, 0:D], woTd),
            ]:
                P.op("G", lambda g, d=dst, s=src: g.dma_start(d, s), inc=(dC, 16))
            P.op("G", lambda g: g.dma_start(
                ws_f.rearrange("p (c n) -> p c n", c=DC), wsT3), inc=(dC, 16))
            P.op("G", lambda g: g.dma_start(
                wi_f.rearrange("p (c n) -> p c n", c=DC), wiT3), inc=(dC, 16))
            d_const = dC.n

            P.op("V", lambda v: v.memset(ones_b[:, :], 1.0))
            P.op("V", lambda v: v.memset(cst[:, 0:1], EPS))
            P.op("V", lambda v: v.memset(cst[:, 1:2], -0.5))
            P.op("V", lambda v: v.memset(cst[:, 2:3], 1.0 / D))
            P.op("V", lambda v: v.memset(cst[:, 3:4], 1.0 / HID))
            P.op("V", lambda v: v.memset(cst[:, 4:5], -1.0))
            P.op("V", lambda v: v.tensor_add(wsum_f, ws_f, wi_f),
                 wait=[(dC, d_const)])
            P.op("V", lambda v: v.tensor_copy(wsum_b[:, :], slabB[:, 0:DC * NST]))
            P.op("V", lambda v: v.tensor_copy(a_bf[:, :], a_sb))
            P.op("V", lambda v: v.tensor_copy(i_bf[:, :], i_sb))
            P.op("V", lambda v: v.memset(ones_1b[:, :], 1.0))
            P.op("V", lambda v: v.tensor_copy(wo_b[:, :], slab[0:NST, 0:D]),
                 inc=(sV, 1))
            v_prep = sV.n
            P.op("V", lambda v: v.tensor_add(bu_sb[:, :], bs_sb[:, :], bi_sb[:, :]),
                 inc=(sV, 1))
            v_bu = sV.n
            t_bu = sT.n

            d_x = [None] * 9
            d_x[0] = P.op("S", lambda g: g.dma_start(
                bass.AP(xst[0], 0, [[2048, 128], [256, DC], [1, KPRE]]),
                xhd[:, 0:8 * KPRE].rearrange("p (c t) -> p c t", c=DC)),
                inc=(dXp[0], 16))
            d_x[1] = P.op("S", lambda g: g.dma_start(
                xst[1][:, :], xhd[:, XOF(1):XOF(1) + 2048]), inc=(dXp[1], 16))
            d_x[2] = P.op("S", lambda g: g.dma_start(
                xst[2][:, :], xhd[:, XOF(2):XOF(2) + 2048]), inc=(dXp[2], 16))

            st1m = {}
            st2m = {}
            q0_subs = []
            prep_state["B"] = [(sT, t_bu)]

            def ln1_s1(gi):
                glen = KPRE if gi == 0 else 256
                war_cast = [(sT, st1m[gi - 1]["mm"])] if gi >= 1 else []
                war_mm = ([(sV, st1m[gi - 1]["rows"]), (sS, st1m[gi - 1]["rstd"])]
                          if gi >= 1 else [])
                st1m[gi] = ln_stage1(
                    xst[gi % 3], 256, 0, glen, cb, 256, sq,
                    ps[0], ps[0],
                    [(dXp[gi % 3], d_x[gi])], war_cast, war_cast, war_mm,
                    sq_col=256, sq_eng="V")

            d_wg = [None] * 8
            cast_done = [None] * 8
            chain_t = {}
            chain_bank = {}
            bank_last_sig = {}
            gate_mode = ["ln1"]
            ln1_rot = [0]
            scan_rot = [0]
            wg_all = y1b

            def emit_wg_dma(em, wait):
                d_wg[em] = P.op("Y", lambda y, em=em: y.dma_start(
                    bass.AP(slab, (em % 2) * 128, [[256 * DC, 128], [256, DC], [1, 128]]),
                    wgTd[:, em * 1024:(em + 1) * 1024].rearrange(
                        "p (c e) -> p c e", c=DC)),
                    wait=wait, inc=(dWp[em % 2], 16))

            emit_wg_dma(0, [(sV, v_prep)])
            emit_wg_dma(1, [(sV, v_prep)])

            def th_cast(em, c):
                wt = [(dWp[em % 2], d_wg[em])] if c == 0 else []
                P.op("G", lambda g, c=c, em=em: g.tensor_copy(
                    wg_all[:, em * 1024 + c * 128:em * 1024 + (c + 1) * 128],
                    slab[:, c * 256 + (em % 2) * 128:c * 256 + (em % 2) * 128 + 128]),
                    wait=wt, inc=(sG, 1) if c == DC - 1 else None)
                if c == DC - 1:
                    cast_done[em] = sG.n

            def th_chain(em, tile, c):
                ci = em * 4 + tile
                if c == 0:
                    if gate_mode[0] == "ln1":
                        bank = (6, 1)[ln1_rot[0] % 2]
                        ln1_rot[0] += 1
                    else:
                        bank = scan_rot[0] % 4
                        scan_rot[0] += 1
                    chain_bank[ci] = bank
                    wt = [(sG, cast_done[em]),
                          (sV, st2m[2 * tile + 2]["apply"])]
                    if bank in bank_last_sig:
                        wt.append((sS, bank_last_sig[bank]))
                else:
                    bank = chain_bank[ci]
                    wt = []
                P.op("T", lambda t, c=c, em=em, tile=tile, bank=bank: t.matmul(
                    ps[bank][0:128, 0:512],
                    wg_all[:, em * 1024 + c * 128:em * 1024 + (c + 1) * 128],
                    bass.AP(xn, c * TW + KPRE + tile * 512, [[TW * DC, 128], [1, 512]]),
                    start=(c == 0), stop=(c == DC - 1)),
                    wait=wt, inc=(sT, 1) if c == DC - 1 else None)
                if c == DC - 1:
                    chain_t[ci] = sT.n

            def th_sig(em, tile):
                ci = em * 4 + tile
                bank = chain_bank[ci]
                P.op("S", lambda s, em=em, tile=tile, bank=bank: s.activation(
                    bass.AP(gate, em * T + tile * 512, [[T * DC, 128], [1, 512]]),
                    ps[bank][0:128, 0:512], AF.Sigmoid,
                    bias=bgr_sb[:, em:em + 1]),
                    wait=[(sT, chain_t[ci])], inc=(sS, 1))
                bank_last_sig[bank] = sS.n

            def gate_gen():
                sched = []
                for em in range(8):
                    ctag = max(1, (em + 1) // 2)
                    for c in range(DC):
                        sched.append((ctag, em * 20 + c, "G",
                                      lambda em=em, c=c: th_cast(em, c)))
                    if em + 2 < 8:
                        sched.append((ctag, em * 20 + 15, "Y",
                                      lambda em=em: emit_wg_dma(
                                          em + 2, [(sG, cast_done[em])])))
                for tile in range(4):
                    for em in range(8):
                        tag = max(2 * tile + 2, 1, (em + 1) // 2)
                        base = 1000 + (tile * 8 + em) * 20
                        for c in range(DC):
                            sched.append((tag, base + c, "T",
                                          lambda em=em, tile=tile, c=c:
                                          th_chain(em, tile, c)))
                        sched.append((tag, base + 10, "S",
                                      lambda em=em, tile=tile: th_sig(em, tile)))
                sched.sort(key=lambda it: (it[0], it[1]))
                for tag, _, eng, thunk in sched:
                    yield (eng, tag, thunk)

            gen = gate_gen()
            gen_done = False
            peeked = [None]

            def pull_gate(budget, cur_g=8):
                nonlocal gen_done
                while not gen_done:
                    if peeked[0] is None:
                        try:
                            peeked[0] = next(gen)
                        except StopIteration:
                            gen_done = True
                            break
                    eng, need, thunk = peeked[0]
                    if need > cur_g:
                        break
                    if eng == "T" and budget <= 0:
                        break
                    peeked[0] = None
                    thunk()
                    if eng == "T":
                        budget -= 1

            ln1_s1(0)
            for gi in range(9):
                glen = KPRE if gi == 0 else 256
                bcp = [ps[2], ps[3]] if gi % 2 == 0 else [ps[4], ps[5]]
                war_bc = [(sV, st2m[gi - 2]["apply"])] if gi - 2 in st2m else [(sV, v_bu)]
                war_rows = [(sT, st1m[gi - 1]["bc"])] if gi >= 1 else []
                ln_stage2a(st1m[gi], glen, None, 256,
                           ps[0], ps[0], bcp[0], bcp[1], war_bc,
                           war_rows, bf_bc=rows1b, sq_col=256)
                if gi >= 1:
                    q0_subs[gi - 1]["sig_thunk"]()
                if gi >= 2:
                    q0_subs[gi - 2]["alpha_thunk"]()
                if gi < 8:
                    q0_subs.append(emit_prep_subload(
                        0, 1 if gi < 4 else 2, gi % 4, sgn1A, sgn2A, ps[7],
                        first_al=(gi == 0), last_al=(gi == 7),
                        slabT=slabB, wkey="B", dsem=dWb, split=True))
                if gi >= 1:
                    g2 = gi - 1
                    gl2 = KPRE if g2 == 0 else 256
                    xncol = 0 if g2 == 0 else KPRE + 256 * (g2 - 1)
                    out_ap = bass.AP(cb, 0, [[256 * DC, 128], [256, DC], [1, gl2]])
                    st2m[g2] = ln_stage2b(st1m[g2], xst[g2 % 3], 256, 0, gl2, out_ap,
                                          affine=(xn, xncol))
                    if g2 + 3 <= 8:
                        h2 = g2 + 2
                        d_x[g2 + 3] = P.op("S", lambda g, h2=h2, g2=g2: g.dma_start(
                            xst[g2 % 3][:, :], xhd[:, XOF(h2 + 1):XOF(h2 + 1) + 2048]),
                            wait=[(sV, st2m[g2]["apply"])], inc=(dXp[g2 % 3], 16))
                if gi + 1 <= 8:
                    ln1_s1(gi + 1)
                if gi >= 3:
                    pull_gate(32, gi - 1)
            out_ap = bass.AP(cb, 0, [[256 * DC, 128], [256, DC], [1, 256]])
            st2m[8] = ln_stage2b(st1m[8], xst[8 % 3], 256, 0, 256, out_ap,
                                 affine=(xn, KPRE + 256 * 7))
            v_ln1 = st2m[8]["apply"]
            q0_subs[7]["alpha_thunk"]()

            gate_mode[0] = "scan"
            ugroups = [(0, KPRE)] + [(KPRE + 512 * i, 512) for i in range(4)]
            s_uev = sS.n
            v_umask = sV.n
            for ui, (g0, glen) in enumerate(ugroups):
                for c in range(DC):
                    wt = []
                    if c == 0:
                        wt = [(sS, s_uev)]
                        if ui == 0:
                            wt.append((sV, v_ln1))
                    P.op("T", lambda t, c=c, g0=g0, glen=glen: t.matmul(
                        ps[4][0:NST, 0:glen], wsum_b[:, c * NST:(c + 1) * NST],
                        bass.AP(xn, c * TW + g0, [[TW * DC, 128], [1, glen]]),
                        start=(c == 0), stop=(c == DC - 1)),
                        wait=wt, inc=(sT, 1) if c == DC - 1 else None)
                t_u = sT.n
                if ui == 0:
                    P.op("S", lambda s: s.activation(
                        tmpu[0:NST, 0:KPRE], ps[4][0:NST, 0:KPRE], AF.Identity,
                        bias=bu_sb[0:NST, 0:1]),
                        wait=[(sT, t_u), (sV, v_bu), (sV, v_ln1)], inc=(sS, 1))
                    s_uev = sS.n
                    P.op("V", lambda v: v.tensor_mul(
                        ubuf[0:NST, 0:KPRE], tmpu[0:NST, 0:KPRE], mask_sb),
                        wait=[(sS, s_uev)], inc=(sV, 1))
                    v_umask = sV.n
                else:
                    P.op("S", lambda s, g0=g0, glen=glen: s.activation(
                        ubuf[0:NST, g0:g0 + glen], ps[4][0:NST, 0:glen], AF.Identity,
                        bias=bu_sb[0:NST, 0:1]),
                        wait=[(sT, t_u)], inc=(sS, 1))
                    s_uev = sS.n
                if ui >= 1:
                    pull_gate(16, 8)
            s_u_done = sS.n
            while not gen_done:
                pull_gate(8)
            ev1_q0, v_al2_q0 = emit_alpha_evacs(
                0, ps[7], [(sT, q0_subs[7]["alpha"])], first_acc=True)
            al1_ev0 = ev1_q0

            w_ps6 = [(sS, bank_last_sig[6])] if 6 in bank_last_sig else []
            NH = NCHAIN // 2
            hoff = NH * CH
            s_tanh_h = [None, None]
            for r in range(STEPS):
                for h, (bank, off) in enumerate(((6, 0), (5, hoff))):
                    if r == 0:
                        wt = [(sS, s_u_done), (sV, v_umask), (sV, v_bu)]
                        if h == 0:
                            wt += w_ps6
                        P.op("T", lambda t, bank=bank, off=off: t.matmul(
                            ps[bank][0:NST, 0:NH], i_bf[:, :],
                            sv(ubuf, NST, off, CH, NH),
                            start=True, stop=True),
                            wait=wt, inc=(sT, 1))
                    else:
                        P.op("T", lambda t, r=r, bank=bank, off=off, h=h: t.matmul(
                            ps[bank][0:NST, 0:NH], a_bf[:, :],
                            sv(hbuf, NST, off + r - 1, CH, NH),
                            start=True, stop=False),
                            wait=[(sS, s_tanh_h[h])])
                        P.op("T", lambda t, r=r, bank=bank, off=off: t.matmul(
                            ps[bank][0:NST, 0:NH], i_bf[:, :],
                            sv(ubuf, NST, off + r, CH, NH),
                            start=False, stop=True), inc=(sT, 1))
                    t_mm = sT.n
                    P.op("S", lambda s, r=r, bank=bank, off=off: s.activation(
                        sv(hbuf, NST, off + r, CH, NH), ps[bank][0:NST, 0:NH],
                        AF.Tanh),
                        wait=[(sT, t_mm)], inc=(sS, 1))
                    s_tanh_h[h] = sS.n
            s_tanh = s_tanh_h[1]
            t_scan_end = sT.n
            s_gate_done = sS.n
            P.op("V", lambda v: v.nop(), wait=[(sS, s_tanh)], inc=(sV, 1))
            v_hb = sV.n

            if DEBUG:
                P.op("Y", lambda y: y.dma_start(dbg["dbg_u"], ubuf[:, :]),
                     wait=[(sT, t_scan_end), (sS, sS.n), (sV, sV.n)], inc=(dDbg, 16))

            d_xb = [None] * 8
            stripe_apply = [None] * 8
            prevm = None
            vy_hist = []
            evac_hist = []
            st1b = {}
            st2b = {}
            for st in range(3):
                d_xb[st] = P.op("S", lambda g, st=st: g.dma_start(
                    xst[st % 3][:, :], xhd[:, XOF(st + 1):XOF(st + 1) + 2048]),
                    wait=[(sV, v_ln1)], inc=(dXp[st % 3], 16))
            def b_rows(st):
                war_rows = [(sT, st1b[st - 1]["bc"])] if st >= 1 else \
                    [(sG, q0_subs[7]["abs"]), (sS, q0_subs[7]["sign"])]
                ln_rows_bf(st1b[st], 256, ps[4], ps[4], war_rows, sq_col=256)

            def b_bc(st):
                bcp = [ps[2], ps[3]] if st % 2 == 0 else [ps[6], ps[7]]
                war_bc = [(sV, st2b[st - 2]["apply"])] if st >= 2 else \
                    [(sV, v_ln1), (sS, ev1_q0), (sV, v_al2_q0)]
                ln_bc_bf(st1b[st], 256, bcp[0], bcp[1], war_bc)

            def b_apply(st):
                out_ap = bass.AP(y1b, 256 * st, [[T * DC, 128], [T, DC], [1, 256]])
                st2b[st] = ln_stage2b(st1b[st], xst[st % 3], 256, 0, 256, out_ap)
                stripe_apply[st] = st2b[st]["apply"]
                if st + 3 < 8:
                    d_xb[st + 3] = P.op("S", lambda g, st=st: g.dma_start(
                        xst[(st + 3) % 3][:, :],
                        xhd[:, XOF(st + 4):XOF(st + 4) + 2048]),
                        wait=[(sV, st2b[st]["apply"])],
                        inc=(dXp[(st + 3) % 3], 16))

            for st in range(8):
                if st >= 1:
                    b_rows(st - 1)
                for c in range(DC):
                    b3 = (0, 1, 5)[(st * DC + c) % 3]
                    wt = []
                    if c == 0 and st == 0:
                        wt = [(sV, v_hb), (sV, v_ln1), (sS, s_gate_done)]
                    if len(evac_hist) >= 3:
                        wt.append((sS, evac_hist[-3]))
                    P.op("T", lambda t, c=c, st=st, b3=b3: t.matmul(
                        ps[b3][0:128, 0:256], wo_b[0:NST, c * 128:(c + 1) * 128],
                        hbuf[0:NST, KPRE + 256 * st:KPRE + 256 * (st + 1)],
                        start=True, stop=True),
                        wait=wt, inc=(sT, 1))
                    t_m3 = sT.n
                    ew = [(sT, t_m3)]
                    if len(vy_hist) >= 3:
                        ew.append(vy_hist[-3])
                    P.op("S", lambda s, c=c, b3=b3: s.activation(
                        tmps[c % 3][:, 0:256], ps[b3][0:128, 0:256],
                        AF.Identity, bias=bo_sb[:, c:c + 1]),
                        wait=ew, inc=(sS, 1))
                    evac_hist.append(sS.n)
                    P.op("G", lambda g, c=c, st=st: g.tensor_mul(
                        tmps[c % 3][:, 0:256], tmps[c % 3][:, 0:256],
                        bass.AP(gate, c * T + 256 * st, [[T * DC, 128], [1, 256]])),
                        wait=[(sS, evac_hist[-1])], inc=(sG, 1))
                    g_mul = sG.n
                    P.op("V", lambda v, c=c, st=st: v.scalar_tensor_tensor(
                        xst[st % 3][:, c * 256:(c + 1) * 256],
                        xst[st % 3][:, c * 256:(c + 1) * 256], 2.0,
                        tmps[c % 3][:, 0:256], ALU.mult, ALU.add),
                        wait=[(sG, g_mul)] +
                             ([(dXp[st % 3], d_xb[st])] if c == 0 else []),
                        inc=(sV, 1))
                    vy_hist.append((sV, sV.n))
                v_y0 = vy_hist[-1]
                war_cast = [(sT, st1b[st - 1]["mm"])] if st >= 1 else [(sT, st1m[8]["mm"])]
                war_mm = ([(sV, st1b[st - 1]["rows"]), (sS, st1b[st - 1]["rstd"])]
                          if st >= 1 else [])
                st1b[st] = ln_stage1(
                    xst[st % 3], 256, 0, 256, cb, 256, sq, ps[4], ps[4],
                    [v_y0], war_cast, war_cast, war_mm, sq_col=256)
                if st >= 1:
                    b_bc(st - 1)
                if st >= 2:
                    b_apply(st - 2)
            b_rows(7)
            b_bc(7)
            b_apply(6)
            b_apply(7)
            v_y1_done = st2b[7]["apply"]
            if DEBUG:
                allw = [(sT, sT.n), (sV, sV.n), (sS, sS.n), (sG, sG.n)]
                for name, t, cview in [
                    ("dbg_xn", xn, ("(c p) t -> p c t", TW)),
                    ("dbg_h", hbuf, None),
                    ("dbg_gate", gate, ("(c p) t -> p c t", T)),
                    ("dbg_y1b", y1b, ("(c p) t -> p c t", T)),
                    ("dbg_bu", bu_sb, None),
                ]:
                    dst = dbg[name]
                    if cview is not None:
                        dst = dst.rearrange("(c p) t -> p c t", p=128)
                        srcv = t[:, :].rearrange("p (c t) -> p c t", c=DC)
                    else:
                        srcv = t[:, :]
                    P.op("Y", lambda y, d=dst, s=srcv: y.dma_start(d, s),
                         wait=allw, inc=(dDbg, 16))
            joinv = {}
            for eng, sem in ALLSEMS:
                P.op(eng, lambda e: e.nop(),
                     wait=[(sem, sem.n)] if eng in ("T", "Y") else [],
                     inc=(sem, 1))
                joinv[eng] = sem.n
            P.flush(nc)

        with ExitStack() as S2a:
            e2 = S2a.enter_context
            y2acc = e2(nc.sbuf_tensor("y2acc", [128, DC * T], F32))
            with ExitStack() as S2b:
                e2b = S2b.enter_context
                sgn1B = e2b(nc.sbuf_tensor("sgn1B", [128, DC * 1024], BF16))
                sgn2B = e2b(nc.sbuf_tensor("sgn2B", [128, DC * 1024], BF16))
                gbuf = [e2b(nc.sbuf_tensor(f"gbuf{i}", [128, DC * 512], BF16))
                        for i in range(2)]
                slabC = slabB

                for eng, _ in ALLSEMS:
                    P.op(eng, lambda e: e.nop(),
                         wait=[(osem, joinv[oeng]) for oeng, osem in ALLSEMS
                               if oeng != eng] +
                              ([(dDbg, dDbg.n)] if DEBUG else []))

                v_al2_fin = [0]
                s_b2a = [0]
                sgn1 = [sgn1A, sgn1B]
                sgn2 = [sgn2A, sgn2B]
                q_sign_done = [q0_subs[7]["sign"], None, None, None]
                q_alpha_done = [q0_subs[7]["alpha"], None, None, None]
                al1_ev = [al1_ev0, None, None, None]
                gelu_hist = []
                ev5_hist = []
                for q in range(4):
                    s1q = sgn1[q % 2]
                    s2q = sgn2[q % 2]
                    guard = [(sT, sT.n), (sG, sG.n)]
                    prep_state["A"] = prep_state["A"] + guard + \
                        ([(sV, v_y1_done)] if q == 0 else [])
                    prep_state["B"] = prep_state["B"] + guard + [(sS, sS.n)] + \
                        ([(sV, v_y1_done)] if q == 0 else [])
                    subs = []
                    if q < 3:
                        for s_idx in range(4):
                            subs.append((1, s_idx))
                            subs.append((2, s_idx))
                    sub_i = 0
                    last_sign = None
                    pending_alpha = []
                    prepped = []
                    gelu_tile_done = [None] * 4

                    def emit_m4(tile, s1q=s1q, q=q):
                        for hc in range(8):
                            ghc = q * 8 + hc
                            for c in range(DC):
                                wt = []
                                if c == 0:
                                    if len(gelu_hist) >= 2:
                                        wt.append((sS, gelu_hist[-2]))
                                    if hc == 0 and tile == 0:
                                        wt.append((sS, q_sign_done[q]))
                                        wt.append((sV, v_y1_done))
                                P.op("T", lambda t, c=c, hc=hc, tile=tile: t.matmul(
                                    ps[hc % 2][0:128, 0:512],
                                    s1q[:, c * 1024 + hc * 128:c * 1024 + hc * 128 + 128],
                                    bass.AP(y1b, c * T + tile * 512, [[T * DC, 128], [1, 512]]),
                                    start=(c == 0), stop=(c == DC - 1)),
                                    wait=wt, inc=(sT, 1) if c == DC - 1 else None)
                            t_m4 = sT.n
                            gwt = [(sT, t_m4)]
                            if hc == 0 and tile == 0:
                                gwt.append((sS, al1_ev[q]))
                            ghc = q * 8 + hc
                            P.op("S", lambda s, hc=hc, ghc=ghc, tile=tile: s.activation(
                                gbuf[tile % 2][:, hc * 512:(hc + 1) * 512],
                                ps[hc % 2][0:128, 0:512],
                                AF.Gelu, bias=b1_sb[:, ghc:ghc + 1],
                                scale=al1_sb[:, ghc:ghc + 1]),
                                wait=gwt, inc=(sS, 1))
                            gelu_hist.append(sS.n)
                        gelu_tile_done[tile] = sS.n

                    def emit_m5(tile, s2q=s2q, q=q):
                        for c in range(DC):
                            for hc in range(8):
                                wt = []
                                if hc == 0:
                                    wt = [(sS, gelu_tile_done[tile])]
                                    if len(ev5_hist) >= 2:
                                        wt.append((sV, ev5_hist[-2]))
                                P.op("T", lambda t, c=c, hc=hc, tile=tile: t.matmul(
                                    ps[2 + c % 2][0:128, 0:512],
                                    s2q[:, hc * 1024 + c * 128:hc * 1024 + c * 128 + 128],
                                    gbuf[tile % 2][:, hc * 512:(hc + 1) * 512],
                                    start=(hc == 0), stop=(hc == 7)),
                                    wait=wt, inc=(sT, 1) if hc == 7 else None)
                            t_m5 = sT.n
                            if q == 0:
                                P.op("V", lambda v, c=c, tile=tile: v.tensor_copy(
                                    y2acc[:, c * T + tile * 512:c * T + tile * 512 + 512],
                                    ps[2 + c % 2][0:128, 0:512]),
                                    wait=[(sT, t_m5)], inc=(sV, 1))
                            elif q < 3:
                                P.op("V", lambda v, c=c, tile=tile: v.tensor_add(
                                    y2acc[:, c * T + tile * 512:c * T + tile * 512 + 512],
                                    y2acc[:, c * T + tile * 512:c * T + tile * 512 + 512],
                                    ps[2 + c % 2][0:128, 0:512]),
                                    wait=[(sT, t_m5)], inc=(sV, 1))
                            else:
                                bw = [(sV, s_b2a[0])] if c == 0 else []
                                P.op("V", lambda v, c=c, tile=tile: v.scalar_tensor_tensor(
                                    y2acc[:, c * T + tile * 512:c * T + tile * 512 + 512],
                                    ps[2 + c % 2][0:128, 0:512],
                                    b2a_sb[:, c:c + 1],
                                    y2acc[:, c * T + tile * 512:c * T + tile * 512 + 512],
                                    ALU.add, ALU.add),
                                    wait=[(sT, t_m5)] + bw, inc=(sV, 1))
                            ev5_hist.append(sV.n)

                    if q == 3:
                        P.op("S", lambda s: s.activation(
                            al2_sb[:, :], al2acc[:, :], AF.Identity,
                            scale=cst[:, 3:4]),
                            wait=[(sV, v_al2_fin[0])], inc=(sS, 1))
                        P.op("S", lambda s: s.activation(
                            b2a_sb[:, :], al2_sb[:, :], AF.Ln))
                        P.op("S", lambda s: s.activation(
                            b2a_sb[:, :], b2a_sb[:, :], AF.Exp,
                            scale=cst[:, 4:5]), inc=(sS, 1))
                        P.op("V", lambda v: v.tensor_mul(
                            b2a_sb[:, :], b2a_sb[:, :], b2_sb[:, :]),
                            wait=[(sS, sS.n)], inc=(sV, 1))
                        s_b2a[0] = sV.n
                    for tile in range(4):
                        for th in pending_alpha:
                            th()
                        pending_alpha = []
                        for _ in range(2):
                            if sub_i < len(subs):
                                wnum, s_idx = subs[sub_i]
                                par = sub_i % 2
                                msp = emit_prep_subload(
                                    q + 1, wnum, s_idx, sgn1[(q + 1) % 2],
                                    sgn2[(q + 1) % 2],
                                    ps[6] if (q + 1) % 2 else ps[7],
                                    first_al=(sub_i == 0), last_al=(sub_i == 7),
                                    slabT=[slab, slabC][par], wkey="AB"[par],
                                    dsem=dWp[0] if par == 0 else dWb)
                                last_sign = msp["sign"]
                                prepped.append(msp)
                                pending_alpha.append(msp["alpha_thunk"])
                                sub_i += 1
                        emit_m4(tile)
                        if tile >= 1:
                            emit_m5(tile - 1)
                    for th in pending_alpha:
                        th()
                    pending_alpha = []
                    emit_m5(3)
                    if q < 3:
                        q_sign_done[q + 1] = last_sign
                        q_alpha_done[q + 1] = prepped[-1]["alpha"]
                        ev1, v_al2 = emit_alpha_evacs(
                            q + 1, ps[6] if (q + 1) % 2 else ps[7],
                            [(sT, prepped[-1]["alpha"])], first_acc=False)
                        al1_ev[q + 1] = ev1
                        if q == 2:
                            v_al2_fin[0] = v_al2
                if DEBUG:
                    allw = [(sT, sT.n), (sV, sV.n), (sS, sS.n)]
                    P.op("Y", lambda y: y.dma_start(
                        dbg["dbg_y2acc"].rearrange("(c p) t -> p c t", p=128),
                        y2acc[:, :].rearrange("p (c t) -> p c t", c=DC)),
                        wait=allw, inc=(dDbg, 16))
                    P.op("Y", lambda y: y.dma_start(dbg["dbg_al1"], al1_sb[:, :]),
                         wait=allw, inc=(dDbg, 16))
                    P.op("Y", lambda y: y.dma_start(dbg["dbg_al2"], al2_sb[:, :]),
                         wait=allw, inc=(dDbg, 16))

                joinv2 = {}
                for eng, sem in ALLSEMS:
                    P.op(eng, lambda e: e.nop(),
                         wait=[(sem, sem.n)] if eng in ("T", "Y") else [],
                         inc=(sem, 1))
                    joinv2[eng] = sem.n
                P.flush(nc)

            with ExitStack() as S2c:
                e2c = S2c.enter_context
                y2f = [e2c(nc.sbuf_tensor(f"y2f{i}", [128, DC * 256], F32))
                       for i in range(3)]
                outt = [e2c(nc.sbuf_tensor(f"outt{i}", [128, DC * 256], F32))
                        for i in range(2)]
                cb3 = e2c(nc.sbuf_tensor("cb3", [128, DC * 256], BF16))
                sq3 = e2c(nc.sbuf_tensor("sq3", [128, DC * 128], BF16))
                rows3 = e2c(nc.sbuf_tensor("rows3", [1, 4 * 256], F32))
                ones_f = e2c(nc.sbuf_tensor("ones_f", [1, 128], F32))
                P.op("V", lambda v: v.memset(ones_f[:, :], 1.0))

                for eng, _ in ALLSEMS:
                    P.op(eng, lambda e: e.nop(),
                         wait=[(osem, joinv2[oeng]) for oeng, osem in ALLSEMS
                               if oeng != eng])

                st1c = {}
                st2c = {}
                st_out = {}
                R3 = [i * 256 for i in range(4)]

                def c_rows(st):
                    m = st1c[st]
                    war = [(sT, st1c[st - 1]["bc"])] if st >= 1 else []
                    P.op("V", lambda v: v.tensor_scalar(
                        rows3[:, R3[0]:R3[0] + 256], ps[st % 2][0:1, 0:256],
                        1.0 / D, None, ALU.mult), wait=[(sT, m["mm"])] + war)
                    P.op("V", lambda v: v.tensor_scalar(
                        rows3[:, R3[1]:R3[1] + 256], ps[2 + st % 2][0:1, 0:256],
                        1.0 / D, None, ALU.mult))
                    P.op("V", lambda v: v.tensor_mul(
                        rows3[:, R3[2]:R3[2] + 256], rows3[:, R3[0]:R3[0] + 256],
                        rows3[:, R3[0]:R3[0] + 256]))
                    P.op("V", lambda v: v.tensor_sub(
                        rows3[:, R3[1]:R3[1] + 256], rows3[:, R3[1]:R3[1] + 256],
                        rows3[:, R3[2]:R3[2] + 256]))
                    m["rows"] = sV.n
                    P.op("S", lambda s: s.activation(
                        rows3[:, R3[2]:R3[2] + 256], rows3[:, R3[1]:R3[1] + 256],
                        AF.Ln, bias=cst[0:1, 0:1]), wait=[(sV, m["rows"])])
                    P.op("S", lambda s: s.activation(
                        rows3[:, R3[3]:R3[3] + 256], rows3[:, R3[2]:R3[2] + 256],
                        AF.Exp, scale=cst[0:1, 1:2]))
                    m["rstd"] = sS.n

                def c_bc(st):
                    m = st1c[st]
                    bcp = [ps[4], ps[5]] if st % 2 == 0 else [ps[6], ps[7]]
                    war = [(sV, st2c[st - 2]["apply"])] if st >= 2 else []
                    P.op("T", lambda t, st=st: t.matmul(
                        bcp[0][0:128, 0:256], ones_f[:, :], rows3[0:1, R3[0]:R3[0] + 256],
                        start=True, stop=True), wait=[(sS, m["rstd"])] + war)
                    P.op("T", lambda t, st=st: t.matmul(
                        bcp[1][0:128, 0:256], ones_f[:, :], rows3[0:1, R3[3]:R3[3] + 256],
                        start=True, stop=True), inc=(sT, 1))
                    m["bc"] = sT.n
                    m["bcmu"] = bcp[0]
                    m["bcr"] = bcp[1]

                def c_apply(st):
                    war = [st_out[st - 2]] if st >= 2 else []
                    ot = outt[st % 2]
                    out_ap = bass.AP(ot, 0, [[256 * DC, 128], [256, DC], [1, 256]])
                    m = st1c[st]
                    sview = bass.AP(y2f[st % 3], 0, [[256 * DC, 128], [256, DC], [1, 256]])
                    P.op("V", lambda v: v.tensor_sub(
                        sview, sview,
                        bass.AP(m["bcmu"], 0, [[512, 128], [0, DC], [1, 256]])),
                        wait=[(sT, m["bc"])] + war)
                    P.op("V", lambda v, ot=ot: v.tensor_mul(
                        out_ap, sview,
                        bass.AP(m["bcr"], 0, [[512, 128], [0, DC], [1, 256]])))
                    st2c[st] = {"apply": sV.n}
                    osem = dO if st % 2 == 0 else dO1
                    P.op("Y", lambda y, st=st, ot=ot: y.dma_start(
                        outTd[:, st * 2048:(st + 1) * 2048], ot[:, :]),
                        wait=[(sV, st2c[st]["apply"])], inc=(osem, 16))
                    st_out[st] = (osem, osem.n)

                for st in range(8):
                    yf = y2f[st % 3]
                    if st >= 2:
                        c_apply(st - 2)
                    if st >= 1:
                        c_rows(st - 1)
                    for c in range(DC):
                        swt = []
                        if c == 0 and st >= 3:
                            swt.append((sV, st2c[st - 3]["apply"]))
                        P.op("V", lambda v, c=c, st=st, yf=yf: v.scalar_tensor_tensor(
                            yf[:, c * 256:(c + 1) * 256],
                            y2acc[:, c * T + st * 256:c * T + st * 256 + 256],
                            al2_sb[:, c:c + 1],
                            bass.AP(y1b, c * T + st * 256, [[T * DC, 128], [1, 256]]),
                            ALU.mult, ALU.add),
                            wait=swt, inc=(sV, 1) if c == DC - 1 else None)
                    v_y2f = sV.n
                    if st >= 1:
                        c_bc(st - 1)
                    war_cast = [(sT, st1c[st - 1]["mm"])] if st >= 1 else []
                    war_mm = ([(sV, st1c[st - 1]["rows"]), (sS, st1c[st - 1]["rstd"])]
                              if st >= 1 else [])
                    st1c[st] = ln_stage1(
                        yf, 256, 0, 256, cb3, 256, sq3,
                        ps[st % 2], ps[2 + st % 2],
                        [(sV, v_y2f)], war_cast, war_cast, war_mm)
                c_rows(7)
                c_bc(7)
                c_apply(6)
                c_apply(7)
                P.op("Y", lambda y: y.nop(),
                     wait=[(dO, dO.n), (dO1, dO1.n)] +
                          ([(dDbg, dDbg.n)] if DEBUG else []))
                P.flush(nc)

    return nc


def kernel(**inputs):
    x = inputs["L0_attn_x"]
    f = np.float32
    wgT = np.asarray(inputs["L0_attn_Wg"]).T.astype(f)
    wg_host = np.ascontiguousarray(
        wgT.reshape(DC, 128, DC, 128).transpose(1, 2, 0, 3).reshape(128, DC * D))
    w1T = np.asarray(inputs["L0_ffn_w1"]).T.astype(f)
    w1_host = np.ascontiguousarray(
        w1T.reshape(DC, 128, 16, 256).transpose(1, 2, 0, 3).reshape(128, 16 * 2048))
    w2T = np.asarray(inputs["L0_ffn_w2"]).T.astype(f)
    w2_host = np.ascontiguousarray(
        w2T.reshape(32, 128, 1024).transpose(1, 0, 2).reshape(128, 32 * 1024))
    common = {
        "wsT": np.ascontiguousarray(inputs["L0_attn_Ws"].T, dtype=f),
        "wiT": np.ascontiguousarray(inputs["L0_attn_Wi"].T, dtype=f),
        "a_t": np.ascontiguousarray(inputs["L0_attn_A"].T, dtype=f),
        "i16": np.eye(NST, dtype=f),
        "woT": np.ascontiguousarray(inputs["L0_attn_Wo"].T, dtype=f),
        "wgT": wg_host,
        "w1T": w1_host,
        "w2T": w2_host,
        "lnw8": np.ascontiguousarray(np.asarray(inputs["L0_attn_ln_w"]).reshape(DC, 128).T, dtype=f),
        "lnb8": np.ascontiguousarray(np.asarray(inputs["L0_attn_ln_b"]).reshape(DC, 128).T, dtype=f),
        "bs_c": np.ascontiguousarray(np.asarray(inputs["L0_attn_bs"]).reshape(NST, 1), dtype=f),
        "bi_c": np.ascontiguousarray(np.asarray(inputs["L0_attn_bi"]).reshape(NST, 1), dtype=f),
        "bo8": np.ascontiguousarray(np.asarray(inputs["L0_attn_bo"]).reshape(DC, 128).T, dtype=f),
        "bg8": np.ascontiguousarray(np.asarray(inputs["L0_attn_bg"]).reshape(DC, 128).T, dtype=f),
        "b18": np.ascontiguousarray(np.asarray(inputs["L0_ffn_b1"]).reshape(32, 128).T, dtype=f),
        "b28": np.ascontiguousarray(np.asarray(inputs["L0_ffn_b2"]).reshape(DC, 128).T, dtype=f),
    }
    x = np.asarray(x)
    in_maps = []
    for core in range(8):
        b, q = core // 4, core % 4
        start = q * T
        xcore = np.zeros((D, TW), dtype=f)
        xcore[:, KPRE:] = x[b, start:start + T, :].T
        if q > 0:
            xcore[:, :KPRE] = x[b, start - KPRE:start, :].T
        x8 = xcore.reshape(DC, 128, TW)
        blocks = [x8[:, :, 0:KPRE].transpose(1, 0, 2).reshape(128, DC * KPRE)]
        for g in range(8):
            blocks.append(x8[:, :, KPRE + 256 * g:KPRE + 256 * (g + 1)]
                          .transpose(1, 0, 2).reshape(128, DC * 256))
        m = dict(common)
        m["xh"] = np.ascontiguousarray(np.concatenate(blocks, axis=1))
        m["maskh"] = (np.zeros if q == 0 else np.ones)((NST, KPRE), dtype=f)
        in_maps.append(m)

    nc = build_program()
    res = run_bass_kernel_spmd(nc, in_maps, core_ids=list(range(8)))

    out = np.empty((2, 8192, D), dtype=np.float32)
    for core in range(8):
        b, q = core // 4, core % 4
        o = res.results[core]["outT"].reshape(128, 8, DC, 256)
        out[b, q * T:(q + 1) * T, :] = o.transpose(1, 3, 2, 0).reshape(T, D)
    return out


# version 41
# speedup vs baseline: 1.1982x; 1.0032x over previous
import numpy as np
import ml_dtypes
from contextlib import ExitStack

import concourse.bass as bass
import concourse.mybir as mybir
from concourse.bass_utils import run_bass_kernel_spmd

F32 = mybir.dt.float32
BF16 = mybir.dt.bfloat16
AF = mybir.ActivationFunctionType
ALU = mybir.AluOpType

DEBUG = False
SELF_EDGES = True
D = 1024
DC = 8
NST = 16
HID = 4096
T = 2048
KPRE = 24
TW = T + KPRE
CH = 16
NCHAIN = T // CH
STEPS = KPRE + CH
EPS = 1e-5


class Sem:
    def __init__(self, stack, nc, name):
        self.h = stack.enter_context(nc.semaphore(name))
        self.n = 0


class Plan:

    def __init__(self, own=None):
        self.items = []
        self.own = own or {}

    def op(self, eng, fn, wait=(), inc=None):
        own = self.own.get(eng)
        selfwait = None
        if own is not None and (inc is None or inc[0] is own):
            if SELF_EDGES:
                selfwait = (own, own.n)
            inc = (own, 1)
        self.items.append((eng, fn, list(wait), inc, selfwait))
        if inc is not None:
            sem, k = inc
            sem.n += k
            return sem.n
        return None

    def run(self, engine, name):
        for eng, fn, waits, inc, selfwait in self.items:
            if eng != name:
                continue
            if selfwait is not None and selfwait[1] > 0:
                engine.wait_ge(selfwait[0].h, selfwait[1])
            for sem, val in waits:
                engine.wait_ge(sem.h, val)
            inst = fn(engine)
            if inc is not None:
                sem, k = inc
                inst.then_inc(sem.h, k)

    def flush(self, nc):
        with nc.Block() as blk:
            blk.tensor(lambda e: self.run(e, "T"))
            blk.vector(lambda e: self.run(e, "V"))
            blk.scalar(lambda e: self.run(e, "S"))
            blk.gpsimd(lambda e: self.run(e, "G"))
            blk.sync(lambda e: self.run(e, "Y"))
        self.items = []


def sv(t, pn, col, step, count):
    return bass.AP(t, col, [[t.shape[-1], pn], [step, count]])


def build_program():
    nc = bass.Bass("TRN2", target_bir_lowering=False)

    xhd = nc.dram_tensor("xh", [128, DC * TW], F32, kind="ExternalInput").ap()
    maskh = nc.dram_tensor("maskh", [NST, KPRE], F32, kind="ExternalInput").ap()
    wsTd = nc.dram_tensor("wsT", [D, NST], F32, kind="ExternalInput").ap()
    wiTd = nc.dram_tensor("wiT", [D, NST], F32, kind="ExternalInput").ap()
    a_td = nc.dram_tensor("a_t", [NST, NST], F32, kind="ExternalInput").ap()
    i16d = nc.dram_tensor("i16", [NST, NST], F32, kind="ExternalInput").ap()
    woTd = nc.dram_tensor("woT", [NST, D], F32, kind="ExternalInput").ap()
    wgTd = nc.dram_tensor("wgT", [128, DC * D], F32, kind="ExternalInput").ap()
    w1Td = nc.dram_tensor("w1T", [128, 16 * 2048], F32, kind="ExternalInput").ap()
    w2Td = nc.dram_tensor("w2T", [128, 32 * 1024], F32, kind="ExternalInput").ap()
    lnw8d = nc.dram_tensor("lnw8", [128, DC], F32, kind="ExternalInput").ap()
    lnb8d = nc.dram_tensor("lnb8", [128, DC], F32, kind="ExternalInput").ap()
    bscd = nc.dram_tensor("bs_c", [NST, 1], F32, kind="ExternalInput").ap()
    bicd = nc.dram_tensor("bi_c", [NST, 1], F32, kind="ExternalInput").ap()
    bo8d = nc.dram_tensor("bo8", [128, DC], F32, kind="ExternalInput").ap()
    bg8d = nc.dram_tensor("bg8", [128, DC], F32, kind="ExternalInput").ap()
    b18d = nc.dram_tensor("b18", [128, 32], F32, kind="ExternalInput").ap()
    b28d = nc.dram_tensor("b28", [128, DC], F32, kind="ExternalInput").ap()
    outTd = nc.dram_tensor("outT", [128, 4 * 4096], F32, kind="ExternalOutput").ap()
    if DEBUG:
        dbg = {k: nc.dram_tensor(k, shp, dt, kind="ExternalOutput").ap()
               for k, shp, dt in [
                   ("dbg_xn", [D, TW], BF16), ("dbg_u", [NST, TW], BF16),
                   ("dbg_h", [NST, TW], BF16), ("dbg_gate", [D, T], BF16),
                   ("dbg_y1b", [D, T], BF16), ("dbg_y2acc", [D, T], F32),
                   ("dbg_al1", [128, 32], F32), ("dbg_al2", [128, DC], F32),
                   ("dbg_bgp", [128, DC], F32), ("dbg_bu", [NST, 1], F32),
               ]}

    wsT3 = wsTd.rearrange("(c p) n -> p c n", p=128)
    wiT3 = wiTd.rearrange("(c p) n -> p c n", p=128)
    XOF = lambda g: 8 * KPRE + (g - 1) * 2048 if g > 0 else 0

    with ExitStack() as S0:
        e0 = S0.enter_context
        dC = Sem(S0, nc, "dC")
        dXp = [Sem(S0, nc, f"dX{i}") for i in range(3)]
        dWp = [Sem(S0, nc, "dW0"), Sem(S0, nc, "dW1")]
        dWb = Sem(S0, nc, "dWb")
        dO = Sem(S0, nc, "dO")
        dO1 = Sem(S0, nc, "dO1")
        dDbg = Sem(S0, nc, "dDbg")
        sT = Sem(S0, nc, "sT")
        sV = Sem(S0, nc, "sV")
        sS = Sem(S0, nc, "sS")
        sG = Sem(S0, nc, "sG")
        sY = Sem(S0, nc, "sY")
        ALLSEMS = [("T", sT), ("V", sV), ("S", sS), ("G", sG), ("Y", sY)]
        P = Plan(own={"V": sV, "S": sS, "G": sG})


        lnw_sb = e0(nc.sbuf_tensor("lnw_sb", [128, DC], F32))
        lnb_sb = e0(nc.sbuf_tensor("lnb_sb", [128, DC], F32))
        bs_sb = e0(nc.sbuf_tensor("bs_sb", [NST, 1], F32))
        bi_sb = e0(nc.sbuf_tensor("bi_sb", [NST, 1], F32))
        bu_sb = e0(nc.sbuf_tensor("bu_sb", [NST, 1], F32))
        bo_sb = e0(nc.sbuf_tensor("bo_sb", [128, DC], F32))
        bgr_sb = e0(nc.sbuf_tensor("bgr_sb", [128, DC], F32))
        bgp_sb = e0(nc.sbuf_tensor("bgp_sb", [128, DC], F32))
        b1_sb = e0(nc.sbuf_tensor("b1_sb", [128, 32], F32))
        b2_sb = e0(nc.sbuf_tensor("b2_sb", [128, DC], F32))
        al1_sb = e0(nc.sbuf_tensor("al1_sb", [128, 32], F32))
        al2_sb = e0(nc.sbuf_tensor("al2_sb", [128, DC], F32))
        al2acc = e0(nc.sbuf_tensor("al2acc", [128, DC], F32))

        cst = e0(nc.sbuf_tensor("cst", [128, 5], F32))
        b2a_sb = e0(nc.sbuf_tensor("b2a_sb", [128, DC], F32))
        ones_b = e0(nc.sbuf_tensor("ones_b", [128, 1], BF16))
        ones_f = None
        y1b = e0(nc.sbuf_tensor("y1b", [128, DC * T], BF16))
        slab = e0(nc.sbuf_tensor("slab", [128, DC * 256], F32))
        slabB = e0(nc.sbuf_tensor("slabB", [128, 2048], F32))
        sgn1A = e0(nc.sbuf_tensor("sgn1A", [128, DC * 1024], BF16))
        sgn2A = e0(nc.sbuf_tensor("sgn2A", [128, DC * 1024], BF16))
        absq = [e0(nc.sbuf_tensor(f"absq{i}", [128, 2048], BF16)) for i in range(2)]
        ps = [e0(nc.psum_tensor(f"ps{i}", [128, 512], F32)) for i in range(8)]

        def ln_stage1(src, gw, off, glen, cbuf, cw, sqbuf, ps_sx, ps_sq,
                      src_wait, war_cast, war_sq, war_mm=(), sq_col=0,
                      cast_eng="V"):
            merged = ps_sq is ps_sx
            cast_sem = {"V": sV, "G": sG}[cast_eng]
            m = {}
            P.op(cast_eng, lambda v: v.tensor_copy(
                bass.AP(cbuf, 0, [[cw * DC, 128], [cw, DC], [1, glen]]),
                bass.AP(src, off, [[gw * DC, 128], [gw, DC], [1, glen]])),
                wait=src_wait + war_cast,
                inc=(cast_sem, 1) if cast_eng != "V" else None)
            m["cast"] = cast_sem.n
            m["cast_sem"] = cast_sem
            hw = DC // 2
            P.op("S", lambda s: s.activation(
                bass.AP(sqbuf, 0, [[cw * hw, 128], [cw, hw], [1, glen]]),
                bass.AP(src, off, [[gw * DC, 128], [gw, hw], [1, glen]]),
                AF.Square), wait=src_wait + war_sq)
            m["sq0"] = sS.n
            for c in range(DC):
                P.op("T", lambda t, c=c: t.matmul(
                    ps_sx[0:1, 0:glen], ones_b[:, :], cbuf[:, c * cw:c * cw + glen],
                    start=(c == 0), stop=(c == DC - 1 and not merged)),
                    wait=([(m["cast_sem"], m["cast"])] + list(war_mm)) if c == 0 else [])
            for c in range(hw):
                P.op("T", lambda t, c=c: t.matmul(
                    ps_sq[0:1, sq_col:sq_col + glen], ones_b[:, :],
                    sqbuf[:, c * cw:c * cw + glen],
                    start=(c == 0 and not merged), stop=False),
                    wait=[(sS, m["sq0"])] if c == 0 else [],
                    inc=(sT, 1) if c == hw - 1 else None)
            t_half = sT.n
            P.op("S", lambda s: s.activation(
                bass.AP(sqbuf, 0, [[cw * hw, 128], [cw, hw], [1, glen]]),
                bass.AP(src, off + hw * gw, [[gw * DC, 128], [gw, hw], [1, glen]]),
                AF.Square), wait=[(sT, t_half)])
            m["sq1"] = sS.n
            for c in range(hw):
                P.op("T", lambda t, c=c: t.matmul(
                    ps_sq[0:1, sq_col:sq_col + glen], ones_b[:, :],
                    sqbuf[:, c * cw:c * cw + glen],
                    start=False, stop=(c == hw - 1)),
                    wait=[(sS, m["sq1"])] if c == 0 else [],
                    inc=(sT, 1) if c == hw - 1 else None)
            m["mm"] = sT.n
            return m

        class PsumRows:

            def __init__(self, bank, rw):
                self.bank = bank
                self.rw = rw

            def __getitem__(self, key):
                _, colsl = key
                a = colsl.start
                slot = a // self.rw
                off = a - slot * self.rw
                n = colsl.stop - a
                p = 32 * (slot + 1)
                return bass.AP(self.bank, p * 512 + off, [[512, 1], [1, n]])

        def ln_rows_bf(m, glen, ps_sx, ps_sq, war_rows=(), sq_col=0):
            mu = bass.AP(rows1f, 0, [[512, 1], [1, glen]])
            mu2 = bass.AP(rows1f, 256, [[512, 1], [1, glen]])
            slotA = bass.AP(ps_sx, 32 * 512, [[512, 1], [1, glen]])
            slotB = bass.AP(ps_sx, 64 * 512, [[512, 1], [1, glen]])
            P.op("V", lambda v: v.tensor_scalar(
                mu, ps_sx[0:1, 0:glen], 1.0 / D, None, ALU.mult),
                wait=[(sT, m["mm"])] + list(war_rows))
            P.op("V", lambda v: v.tensor_scalar(
                slotA, ps_sq[0:1, sq_col:sq_col + glen], 1.0 / D, None, ALU.mult))
            P.op("V", lambda v: v.tensor_mul(mu2, mu, mu))
            P.op("V", lambda v: v.tensor_sub(slotA, slotA, mu2))
            P.op("V", lambda v: v.tensor_copy(rows1b[0:1, 0:glen], mu))
            m["rows"] = sV.n
            P.op("S", lambda s: s.activation(
                slotB, slotA, AF.Ln, bias=cst[0:1, 0:1]),
                wait=[(sV, m["rows"])])
            P.op("S", lambda s: s.activation(
                rows1b[0:1, 256:256 + glen], slotB,
                AF.Exp, scale=cst[0:1, 1:2]))
            m["rstd"] = sS.n
            return m

        def ln_bc_bf(m, glen, ps_bcmu, ps_bcr, war_bc):
            P.op("T", lambda t: t.matmul(
                ps_bcmu[0:128, 0:glen], ones_1b[:, :], rows1b[0:1, 0:glen],
                start=True, stop=True), wait=[(sS, m["rstd"])] + list(war_bc))
            P.op("T", lambda t: t.matmul(
                ps_bcr[0:128, 0:glen], ones_1b[:, :], rows1b[0:1, 256:256 + glen],
                start=True, stop=True), inc=(sT, 1))
            m["bc"] = sT.n
            m["bcmu"] = ps_bcmu
            m["bcr"] = ps_bcr
            return m

        def ln_stage2a(m, glen, rowsT, rw, ps_sx, ps_sq, ps_bcmu, ps_bcr, war_bc,
                       war_rows=(), bf_bc=None, sq_col=0):
            if bf_bc is not None:
                ln_rows_bf(m, glen, ps_sx, ps_sq, war_rows, sq_col=sq_col)
                ln_bc_bf(m, glen, ps_bcmu, ps_bcr, war_bc)
                m["bcmu"] = ps_bcmu
                m["bcr"] = ps_bcr
                return m
            else:
                R = [i * rw for i in range(4)]
                P.op("V", lambda v: v.tensor_scalar(
                    rowsT[:, R[0]:R[0] + glen], ps_sx[0:1, 0:glen], 1.0 / D, None, ALU.mult),
                    wait=[(sT, m["mm"])] + list(war_rows))
                P.op("V", lambda v: v.tensor_scalar(
                    rowsT[:, R[1]:R[1] + glen], ps_sq[0:1, 0:glen], 1.0 / D, None, ALU.mult))
                P.op("V", lambda v: v.tensor_mul(
                    rowsT[:, R[2]:R[2] + glen], rowsT[:, R[0]:R[0] + glen],
                    rowsT[:, R[0]:R[0] + glen]))
                P.op("V", lambda v: v.tensor_sub(
                    rowsT[:, R[1]:R[1] + glen], rowsT[:, R[1]:R[1] + glen],
                    rowsT[:, R[2]:R[2] + glen]))
                m["rows"] = sV.n
                P.op("S", lambda s: s.activation(
                    rowsT[:, R[2]:R[2] + glen], rowsT[:, R[1]:R[1] + glen],
                    AF.Ln, bias=cst[0:1, 0:1]), wait=[(sV, m["rows"])])
                P.op("S", lambda s: s.activation(
                    rowsT[:, R[3]:R[3] + glen], rowsT[:, R[2]:R[2] + glen],
                    AF.Exp, scale=cst[0:1, 1:2]))
                m["rstd"] = sS.n
                P.op("T", lambda t: t.matmul(
                    ps_bcmu[0:128, 0:glen], ones_f[:, :], rowsT[0:1, R[0]:R[0] + glen],
                    start=True, stop=True), wait=[(sS, m["rstd"])] + war_bc)
                P.op("T", lambda t: t.matmul(
                    ps_bcr[0:128, 0:glen], ones_f[:, :], rowsT[0:1, R[3]:R[3] + glen],
                    start=True, stop=True), inc=(sT, 1))
            m["bc"] = sT.n
            m["bcmu"] = ps_bcmu
            m["bcr"] = ps_bcr
            return m

        def ln_stage2b(m, src, gw, off, glen, out_ap, affine=None):
            sview = bass.AP(src, off, [[gw * DC, 128], [gw, DC], [1, glen]])
            P.op("V", lambda v: v.tensor_sub(
                sview, sview,
                bass.AP(m["bcmu"], 0, [[512, 128], [0, DC], [1, glen]])),
                wait=[(sT, m["bc"])])
            P.op("V", lambda v: v.tensor_mul(
                out_ap, sview,
                bass.AP(m["bcr"], 0, [[512, 128], [0, DC], [1, glen]])))
            if affine is not None:
                dst, col = affine
                for c in range(DC):
                    P.op("V", lambda v, c=c: v.tensor_scalar(
                        bass.AP(dst, c * TW + col, [[TW * DC, 128], [1, glen]]),
                        cb[:, c * 256:c * 256 + glen],
                        lnw_sb[:, c:c + 1], lnb_sb[:, c:c + 1],
                        ALU.mult, ALU.add))
            m["apply"] = sV.n
            return m

        def ln_stage2(m, src, gw, off, glen, rowsT, rw, ps_sx, ps_sq,
                      ps_bcmu, ps_bcr, out_ap, war_bc):
            ln_stage2a(m, glen, rowsT, rw, ps_sx, ps_sq, ps_bcmu, ps_bcr, war_bc)
            return ln_stage2b(m, src, gw, off, glen, out_ap)

        prep_state = {"A": [], "B": [], "C": [], "prev_alpha": 0, "absq_i": 0}

        def emit_prep_subload(q, wnum, s_idx, sg1, sg2, al_ps, first_al,
                              last_al, slabT, wkey, dsem, split=False):
            ms = {}
            sb = (q * 4 + s_idx)
            if wnum == 1:
                srcd = w1Td[:, sb * 2048:(sb + 1) * 2048]
                sdst = bass.AP(sg1, s_idx * 256,
                               [[1024 * DC, 128], [1024, DC], [1, 256]])
            else:
                srcd = w2Td[:, (q * 8 + 2 * s_idx) * 1024:(q * 8 + 2 * s_idx + 2) * 1024]
                sdst = bass.AP(sg2, 2 * s_idx * 1024, [[1024 * DC, 128], [1, 2048]])
            ms["dma"] = P.op("Y", lambda y: y.dma_start(slabT[:, 0:2048], srcd),
                             wait=prep_state[wkey], inc=(dsem, 16))

            def sig_thunk():
                P.op("S", lambda s: s.activation(sdst, slabT[:, 0:2048], AF.Sign),
                     wait=[(dsem, ms["dma"])], inc=(sS, 1))
                ms["sign"] = sS.n
                ab = absq[prep_state["absq_i"] % 2]
                prep_state["absq_i"] += 1
                ms["ab"] = ab
                P.op("G", lambda g: g.tensor_mul(ab[:, 0:2048], slabT[:, 0:2048], sdst),
                     wait=[(sS, ms["sign"]), (sT, prep_state["prev_alpha"])],
                     inc=(sG, 1))
                ms["abs"] = sG.n
                prep_state[wkey] = [(sG, ms["abs"])]

            def alpha_thunk():
                ab = ms["ab"]
                if wnum == 1:
                    for hs in range(2):
                        hcm = 8 + s_idx * 2 + hs
                        for c in range(DC):
                            st_flag = first_al and hs == 0 and c == 0 and s_idx == 0
                            P.op("T", lambda t, c=c, hs=hs, hcm=hcm, st_flag=st_flag: t.matmul(
                                al_ps[0:128, hcm:hcm + 1],
                                ab[:, c * 256 + hs * 128:c * 256 + hs * 128 + 128],
                                ones_b[:, :], start=st_flag, stop=False),
                                wait=[(sG, ms["abs"])] if (c == 0 and hs == 0) else [],
                                inc=(sT, 1) if (c == DC - 1 and hs == 1) else None)
                else:
                    for hs in range(2):
                        for dsx in range(DC):
                            st_flag = first_al and hs == 0 and dsx == 0 and s_idx == 0
                            sp_flag = last_al and hs == 1 and dsx == DC - 1 and s_idx == 3
                            P.op("T", lambda t, hs=hs, dsx=dsx, st_flag=st_flag, sp_flag=sp_flag: t.matmul(
                                al_ps[0:128, dsx:dsx + 1],
                                ab[:, hs * 1024 + dsx * 128:hs * 1024 + dsx * 128 + 128],
                                ones_b[:, :], start=st_flag, stop=sp_flag),
                                wait=[(sG, ms["abs"])] if (dsx == 0 and hs == 0) else [],
                                inc=(sT, 1) if (dsx == DC - 1 and hs == 1) else None)
                ms["alpha"] = sT.n
                prep_state["prev_alpha"] = ms["alpha"]
            ms["sig_thunk"] = sig_thunk
            ms["alpha_thunk"] = alpha_thunk
            if not split:
                sig_thunk()
            return ms

        def emit_alpha_evacs(q, al_ps, al_wait, first_acc):
            P.op("S", lambda s, q=q: s.activation(
                al1_sb[:, q * 8:(q + 1) * 8], al_ps[0:128, 8:16],
                AF.Identity, scale=cst[:, 2:3]), wait=al_wait, inc=(sS, 1))
            ev1 = sS.n
            if first_acc:
                P.op("V", lambda v: v.tensor_copy(al2acc[:, :], al_ps[0:128, 0:8]),
                     wait=al_wait, inc=(sV, 1))
            else:
                P.op("V", lambda v: v.tensor_add(
                    al2acc[:, :], al2acc[:, :], al_ps[0:128, 0:8]),
                    wait=al_wait, inc=(sV, 1))
            return ev1, sV.n

        with ExitStack() as S1:
            e1 = S1.enter_context
            ws_f = slabB[:, 0:DC * NST]
            wi_f = slabB[:, DC * NST:2 * DC * NST]
            wsum_f = ws_f
            wsum_b = e1(nc.sbuf_tensor("wsum_b", [128, DC * NST], BF16))
            wo_b = e1(nc.sbuf_tensor("wo_b", [NST, D], BF16))
            xst = [e1(nc.sbuf_tensor(f"xst{i}", [128, DC * 256], F32)) for i in range(3)]
            xn = e1(nc.sbuf_tensor("xn", [128, DC * TW], BF16))
            gate = e1(nc.sbuf_tensor("gate", [128, DC * T], BF16))
            cb = e1(nc.sbuf_tensor("cb", [128, DC * 256], BF16))
            sq = e1(nc.sbuf_tensor("sq", [128, DC * 128], BF16))
            tmps = [e1(nc.sbuf_tensor(f"tmps{i}", [128, 256], F32)) for i in range(3)]
            tmpu = tmps[0]
            rows1f = e1(nc.sbuf_tensor("rows1f", [1, 2 * 256], F32))
            a_sb = bass.AP(slabB, 2 * DC * NST, [[2048, NST], [1, NST]])
            i_sb = bass.AP(slabB, 2 * DC * NST + NST, [[2048, NST], [1, NST]])
            mask_sb = bass.AP(tmps[1], 208, [[256, NST], [1, KPRE]])
            hbuf = e1(nc.sbuf_tensor("hbuf", [NST, TW], BF16))
            ubuf = e1(nc.sbuf_tensor("ubuf", [NST, TW], BF16))
            a_bf = e1(nc.sbuf_tensor("a_bf", [NST, NST], BF16))
            i_bf = e1(nc.sbuf_tensor("i_bf", [NST, NST], BF16))
            rows1b = e1(nc.sbuf_tensor("rows1b", [1, 2 * 256], BF16))
            ones_1b = e1(nc.sbuf_tensor("ones_1b", [1, 128], BF16))

            for dst, src in [
                (a_sb, a_td), (i_sb, i16d),
                (lnw_sb[:, :], lnw8d), (lnb_sb[:, :], lnb8d),
                (bs_sb[:, :], bscd), (bi_sb[:, :], bicd),
                (bo_sb[:, :], bo8d), (bgr_sb[:, :], bg8d),
                (b1_sb[:, :], b18d), (b2_sb[:, :], b28d),
                (mask_sb, maskh),
                (slab[0:NST, 0:D], woTd),
            ]:
                P.op("G", lambda g, d=dst, s=src: g.dma_start(d, s), inc=(dC, 16))
            P.op("G", lambda g: g.dma_start(
                ws_f.rearrange("p (c n) -> p c n", c=DC), wsT3), inc=(dC, 16))
            P.op("G", lambda g: g.dma_start(
                wi_f.rearrange("p (c n) -> p c n", c=DC), wiT3), inc=(dC, 16))
            d_const = dC.n

            P.op("V", lambda v: v.memset(ones_b[:, :], 1.0))
            P.op("V", lambda v: v.memset(cst[:, 0:1], EPS))
            P.op("V", lambda v: v.memset(cst[:, 1:2], -0.5))
            P.op("V", lambda v: v.memset(cst[:, 2:3], 1.0 / D))
            P.op("V", lambda v: v.memset(cst[:, 3:4], 1.0 / HID))
            P.op("V", lambda v: v.memset(cst[:, 4:5], -1.0))
            P.op("V", lambda v: v.tensor_add(wsum_f, ws_f, wi_f),
                 wait=[(dC, d_const)])
            P.op("V", lambda v: v.tensor_copy(wsum_b[:, :], slabB[:, 0:DC * NST]))
            P.op("V", lambda v: v.tensor_copy(a_bf[:, :], a_sb))
            P.op("V", lambda v: v.tensor_copy(i_bf[:, :], i_sb))
            P.op("V", lambda v: v.memset(ones_1b[:, :], 1.0))
            P.op("V", lambda v: v.tensor_copy(wo_b[:, :], slab[0:NST, 0:D]),
                 inc=(sV, 1))
            v_prep = sV.n
            P.op("V", lambda v: v.tensor_add(bu_sb[:, :], bs_sb[:, :], bi_sb[:, :]),
                 inc=(sV, 1))
            v_bu = sV.n
            t_bu = sT.n

            d_x = [None] * 9
            d_x[0] = P.op("S", lambda g: g.dma_start(
                bass.AP(xst[0], 0, [[2048, 128], [256, DC], [1, KPRE]]),
                xhd[:, 0:8 * KPRE].rearrange("p (c t) -> p c t", c=DC)),
                inc=(dXp[0], 16))
            d_x[1] = P.op("S", lambda g: g.dma_start(
                xst[1][:, :], xhd[:, XOF(1):XOF(1) + 2048]), inc=(dXp[1], 16))
            d_x[2] = P.op("S", lambda g: g.dma_start(
                xst[2][:, :], xhd[:, XOF(2):XOF(2) + 2048]), inc=(dXp[2], 16))

            st1m = {}
            st2m = {}
            q0_subs = []
            prep_state["B"] = [(sT, t_bu)]

            def ln1_s1(gi):
                glen = KPRE if gi == 0 else 256
                war_cast = [(sT, st1m[gi - 1]["mm"])] if gi >= 1 else []
                war_mm = ([(sV, st1m[gi - 1]["rows"]), (sS, st1m[gi - 1]["rstd"])]
                          if gi >= 1 else [])
                st1m[gi] = ln_stage1(
                    xst[gi % 3], 256, 0, glen, cb, 256, sq,
                    ps[0], ps[0],
                    [(dXp[gi % 3], d_x[gi])], war_cast, war_cast, war_mm,
                    sq_col=256)

            d_wg = [None] * 8
            cast_done = [None] * 8
            chain_t = {}
            chain_bank = {}
            bank_last_sig = {}
            gate_mode = ["ln1"]
            ln1_rot = [0]
            scan_rot = [0]
            wg_all = y1b

            def emit_wg_dma(em, wait):
                d_wg[em] = P.op("Y", lambda y, em=em: y.dma_start(
                    bass.AP(slab, (em % 2) * 128, [[256 * DC, 128], [256, DC], [1, 128]]),
                    wgTd[:, em * 1024:(em + 1) * 1024].rearrange(
                        "p (c e) -> p c e", c=DC)),
                    wait=wait, inc=(dWp[em % 2], 16))

            emit_wg_dma(0, [(sV, v_prep)])
            emit_wg_dma(1, [(sV, v_prep)])

            def th_cast(em, c):
                wt = [(dWp[em % 2], d_wg[em])] if c == 0 else []
                P.op("G", lambda g, c=c, em=em: g.tensor_copy(
                    wg_all[:, em * 1024 + c * 128:em * 1024 + (c + 1) * 128],
                    slab[:, c * 256 + (em % 2) * 128:c * 256 + (em % 2) * 128 + 128]),
                    wait=wt, inc=(sG, 1) if c == DC - 1 else None)
                if c == DC - 1:
                    cast_done[em] = sG.n

            def th_chain(em, tile, c):
                ci = em * 4 + tile
                if c == 0:
                    if gate_mode[0] == "ln1":
                        bank = (6, 1)[ln1_rot[0] % 2]
                        ln1_rot[0] += 1
                    else:
                        bank = (0, 1, 3)[scan_rot[0] % 3]
                        scan_rot[0] += 1
                    chain_bank[ci] = bank
                    wt = [(sG, cast_done[em]),
                          (sV, st2m[2 * tile + 2]["apply"])]
                    if bank in bank_last_sig:
                        wt.append((sS, bank_last_sig[bank]))
                else:
                    bank = chain_bank[ci]
                    wt = []
                P.op("T", lambda t, c=c, em=em, tile=tile, bank=bank: t.matmul(
                    ps[bank][0:128, 0:512],
                    wg_all[:, em * 1024 + c * 128:em * 1024 + (c + 1) * 128],
                    bass.AP(xn, c * TW + KPRE + tile * 512, [[TW * DC, 128], [1, 512]]),
                    start=(c == 0), stop=(c == DC - 1)),
                    wait=wt, inc=(sT, 1) if c == DC - 1 else None)
                if c == DC - 1:
                    chain_t[ci] = sT.n

            def th_sig(em, tile):
                ci = em * 4 + tile
                bank = chain_bank[ci]
                P.op("S", lambda s, em=em, tile=tile, bank=bank: s.activation(
                    bass.AP(gate, em * T + tile * 512, [[T * DC, 128], [1, 512]]),
                    ps[bank][0:128, 0:512], AF.Sigmoid,
                    bias=bgr_sb[:, em:em + 1]),
                    wait=[(sT, chain_t[ci])], inc=(sS, 1))
                bank_last_sig[bank] = sS.n

            def gate_gen():
                sched = []
                for em in range(8):
                    ctag = max(1, (em + 1) // 2)
                    for c in range(DC):
                        sched.append((ctag, em * 20 + c, "G",
                                      lambda em=em, c=c: th_cast(em, c)))
                    if em + 2 < 8:
                        sched.append((ctag, em * 20 + 15, "Y",
                                      lambda em=em: emit_wg_dma(
                                          em + 2, [(sG, cast_done[em])])))
                for tile in range(4):
                    for em in range(8):
                        tag = max(2 * tile + 2, 1, (em + 1) // 2)
                        base = 1000 + (tile * 8 + em) * 20
                        for c in range(DC):
                            sched.append((tag, base + c, "T",
                                          lambda em=em, tile=tile, c=c:
                                          th_chain(em, tile, c)))
                        sched.append((tag, base + 10, "S",
                                      lambda em=em, tile=tile: th_sig(em, tile)))
                sched.sort(key=lambda it: (it[0], it[1]))
                for tag, _, eng, thunk in sched:
                    yield (eng, tag, thunk)

            gen = gate_gen()
            gen_done = False
            peeked = [None]

            def pull_gate(budget, cur_g=8):
                nonlocal gen_done
                while not gen_done:
                    if peeked[0] is None:
                        try:
                            peeked[0] = next(gen)
                        except StopIteration:
                            gen_done = True
                            break
                    eng, need, thunk = peeked[0]
                    if need > cur_g:
                        break
                    if eng == "T" and budget <= 0:
                        break
                    peeked[0] = None
                    thunk()
                    if eng == "T":
                        budget -= 1

            ln1_s1(0)
            for gi in range(9):
                glen = KPRE if gi == 0 else 256
                bcp = [ps[2], ps[3]] if gi % 2 == 0 else [ps[4], ps[5]]
                war_bc = [(sV, st2m[gi - 2]["apply"])] if gi - 2 in st2m else [(sV, v_bu)]
                war_rows = [(sT, st1m[gi - 1]["bc"])] if gi >= 1 else []
                ln_stage2a(st1m[gi], glen, None, 256,
                           ps[0], ps[0], bcp[0], bcp[1], war_bc,
                           war_rows, bf_bc=rows1b, sq_col=256)
                if gi >= 1:
                    q0_subs[gi - 1]["sig_thunk"]()
                if gi >= 2:
                    q0_subs[gi - 2]["alpha_thunk"]()
                if gi < 8:
                    q0_subs.append(emit_prep_subload(
                        0, 1 if gi < 4 else 2, gi % 4, sgn1A, sgn2A, ps[7],
                        first_al=(gi == 0), last_al=(gi == 7),
                        slabT=slabB, wkey="B", dsem=dWb, split=True))
                if gi >= 1:
                    g2 = gi - 1
                    gl2 = KPRE if g2 == 0 else 256
                    xncol = 0 if g2 == 0 else KPRE + 256 * (g2 - 1)
                    out_ap = bass.AP(cb, 0, [[256 * DC, 128], [256, DC], [1, gl2]])
                    st2m[g2] = ln_stage2b(st1m[g2], xst[g2 % 3], 256, 0, gl2, out_ap,
                                          affine=(xn, xncol))
                    if g2 + 3 <= 8:
                        h2 = g2 + 2
                        d_x[g2 + 3] = P.op("S", lambda g, h2=h2, g2=g2: g.dma_start(
                            xst[g2 % 3][:, :], xhd[:, XOF(h2 + 1):XOF(h2 + 1) + 2048]),
                            wait=[(sV, st2m[g2]["apply"])], inc=(dXp[g2 % 3], 16))
                if gi + 1 <= 8:
                    ln1_s1(gi + 1)
                if gi >= 3:
                    pull_gate(32, gi - 1)
            out_ap = bass.AP(cb, 0, [[256 * DC, 128], [256, DC], [1, 256]])
            st2m[8] = ln_stage2b(st1m[8], xst[8 % 3], 256, 0, 256, out_ap,
                                 affine=(xn, KPRE + 256 * 7))
            v_ln1 = st2m[8]["apply"]
            q0_subs[7]["alpha_thunk"]()

            gate_mode[0] = "scan"
            ugroups = [(0, KPRE)] + [(KPRE + 512 * i, 512) for i in range(4)]
            uev = {}
            v_umask = sV.n
            for ui, (g0, glen) in enumerate(ugroups):
                psu = ps[4] if ui % 2 == 0 else ps[2]
                for c in range(DC):
                    wt = []
                    if c == 0:
                        if ui - 2 in uev:
                            wt.append((sS, uev[ui - 2]))
                        if ui == 0:
                            wt.append((sV, v_ln1))
                    P.op("T", lambda t, c=c, g0=g0, glen=glen, psu=psu: t.matmul(
                        psu[0:NST, 0:glen], wsum_b[:, c * NST:(c + 1) * NST],
                        bass.AP(xn, c * TW + g0, [[TW * DC, 128], [1, glen]]),
                        start=(c == 0), stop=(c == DC - 1)),
                        wait=wt, inc=(sT, 1) if c == DC - 1 else None)
                t_u = sT.n
                if ui == 0:
                    P.op("S", lambda s: s.activation(
                        tmpu[0:NST, 0:KPRE], ps[4][0:NST, 0:KPRE], AF.Identity,
                        bias=bu_sb[0:NST, 0:1]),
                        wait=[(sT, t_u), (sV, v_bu), (sV, v_ln1)], inc=(sS, 1))
                    uev[0] = sS.n
                    P.op("V", lambda v: v.tensor_mul(
                        ubuf[0:NST, 0:KPRE], tmpu[0:NST, 0:KPRE], mask_sb),
                        wait=[(sS, uev[0])], inc=(sV, 1))
                    v_umask = sV.n
                else:
                    P.op("S", lambda s, g0=g0, glen=glen, psu=psu: s.activation(
                        ubuf[0:NST, g0:g0 + glen], psu[0:NST, 0:glen], AF.Identity,
                        bias=bu_sb[0:NST, 0:1]),
                        wait=[(sT, t_u)], inc=(sS, 1))
                    uev[ui] = sS.n
                if ui >= 1:
                    pull_gate(16, 8)
            s_uev = uev[4]
            s_u_done = sS.n
            while not gen_done:
                pull_gate(8)
            ev1_q0, v_al2_q0 = emit_alpha_evacs(
                0, ps[7], [(sT, q0_subs[7]["alpha"])], first_acc=True)
            al1_ev0 = ev1_q0

            w_ps6 = [(sS, bank_last_sig[6])] if 6 in bank_last_sig else []
            NH = NCHAIN // 2
            hoff = NH * CH
            s_tanh_h = [None, None]
            for r in range(STEPS):
                for h, (bank, off) in enumerate(((6, 0), (5, hoff))):
                    if r == 0:
                        wt = [(sS, s_u_done), (sV, v_umask), (sV, v_bu)]
                        if h == 0:
                            wt += w_ps6
                        P.op("T", lambda t, bank=bank, off=off: t.matmul(
                            ps[bank][0:NST, 0:NH], i_bf[:, :],
                            sv(ubuf, NST, off, CH, NH),
                            start=True, stop=True),
                            wait=wt, inc=(sT, 1))
                    else:
                        P.op("T", lambda t, r=r, bank=bank, off=off, h=h: t.matmul(
                            ps[bank][0:NST, 0:NH], a_bf[:, :],
                            sv(hbuf, NST, off + r - 1, CH, NH),
                            start=True, stop=False),
                            wait=[(sS, s_tanh_h[h])])
                        P.op("T", lambda t, r=r, bank=bank, off=off: t.matmul(
                            ps[bank][0:NST, 0:NH], i_bf[:, :],
                            sv(ubuf, NST, off + r, CH, NH),
                            start=False, stop=True), inc=(sT, 1))
                    t_mm = sT.n
                    P.op("S", lambda s, r=r, bank=bank, off=off: s.activation(
                        sv(hbuf, NST, off + r, CH, NH), ps[bank][0:NST, 0:NH],
                        AF.Tanh),
                        wait=[(sT, t_mm)], inc=(sS, 1))
                    s_tanh_h[h] = sS.n
            s_tanh = s_tanh_h[1]
            t_scan_end = sT.n
            s_gate_done = sS.n
            P.op("V", lambda v: v.nop(), wait=[(sS, s_tanh)], inc=(sV, 1))
            v_hb = sV.n

            if DEBUG:
                P.op("Y", lambda y: y.dma_start(dbg["dbg_u"], ubuf[:, :]),
                     wait=[(sT, t_scan_end), (sS, sS.n), (sV, sV.n)], inc=(dDbg, 16))

            d_xb = [None] * 8
            stripe_apply = [None] * 8
            prevm = None
            vy_hist = []
            evac_hist = []
            st1b = {}
            st2b = {}
            for st in range(3):
                d_xb[st] = P.op("S", lambda g, st=st: g.dma_start(
                    xst[st % 3][:, :], xhd[:, XOF(st + 1):XOF(st + 1) + 2048]),
                    wait=[(sV, v_ln1)], inc=(dXp[st % 3], 16))
            def b_rows(st):
                war_rows = [(sT, st1b[st - 1]["bc"])] if st >= 1 else \
                    [(sG, q0_subs[7]["abs"]), (sS, q0_subs[7]["sign"])]
                ln_rows_bf(st1b[st], 256, ps[4], ps[4], war_rows, sq_col=256)

            def b_bc(st):
                bcp = [ps[2], ps[3]] if st % 2 == 0 else [ps[6], ps[7]]
                war_bc = [(sV, st2b[st - 2]["apply"])] if st >= 2 else \
                    [(sV, v_ln1), (sS, ev1_q0), (sV, v_al2_q0)]
                ln_bc_bf(st1b[st], 256, bcp[0], bcp[1], war_bc)

            def b_apply(st):
                out_ap = bass.AP(y1b, 256 * st, [[T * DC, 128], [T, DC], [1, 256]])
                st2b[st] = ln_stage2b(st1b[st], xst[st % 3], 256, 0, 256, out_ap)
                stripe_apply[st] = st2b[st]["apply"]
                if st + 3 < 8:
                    d_xb[st + 3] = P.op("S", lambda g, st=st: g.dma_start(
                        xst[(st + 3) % 3][:, :],
                        xhd[:, XOF(st + 4):XOF(st + 4) + 2048]),
                        wait=[(sV, st2b[st]["apply"])],
                        inc=(dXp[(st + 3) % 3], 16))

            for st in range(8):
                if st >= 1:
                    b_rows(st - 1)
                for c in range(DC):
                    b3 = (0, 1, 5)[(st * DC + c) % 3]
                    wt = []
                    if c == 0 and st == 0:
                        wt = [(sV, v_hb), (sV, v_ln1), (sS, s_gate_done)]
                    if len(evac_hist) >= 3:
                        wt.append((sS, evac_hist[-3]))
                    P.op("T", lambda t, c=c, st=st, b3=b3: t.matmul(
                        ps[b3][0:128, 0:256], wo_b[0:NST, c * 128:(c + 1) * 128],
                        hbuf[0:NST, KPRE + 256 * st:KPRE + 256 * (st + 1)],
                        start=True, stop=True),
                        wait=wt, inc=(sT, 1))
                    t_m3 = sT.n
                    ew = [(sT, t_m3)]
                    if len(vy_hist) >= 3:
                        ew.append(vy_hist[-3])
                    P.op("S", lambda s, c=c, b3=b3: s.activation(
                        tmps[c % 3][:, 0:256], ps[b3][0:128, 0:256],
                        AF.Identity, bias=bo_sb[:, c:c + 1]),
                        wait=ew, inc=(sS, 1))
                    evac_hist.append(sS.n)
                    P.op("G", lambda g, c=c, st=st: g.tensor_mul(
                        tmps[c % 3][:, 0:256], tmps[c % 3][:, 0:256],
                        bass.AP(gate, c * T + 256 * st, [[T * DC, 128], [1, 256]])),
                        wait=[(sS, evac_hist[-1])], inc=(sG, 1))
                    g_mul = sG.n
                    P.op("V", lambda v, c=c, st=st: v.scalar_tensor_tensor(
                        xst[st % 3][:, c * 256:(c + 1) * 256],
                        xst[st % 3][:, c * 256:(c + 1) * 256], 2.0,
                        tmps[c % 3][:, 0:256], ALU.mult, ALU.add),
                        wait=[(sG, g_mul)] +
                             ([(dXp[st % 3], d_xb[st])] if c == 0 else []),
                        inc=(sV, 1))
                    vy_hist.append((sV, sV.n))
                v_y0 = vy_hist[-1]
                war_cast = [(sT, st1b[st - 1]["mm"])] if st >= 1 else [(sT, st1m[8]["mm"])]
                war_mm = ([(sV, st1b[st - 1]["rows"]), (sS, st1b[st - 1]["rstd"])]
                          if st >= 1 else [])
                st1b[st] = ln_stage1(
                    xst[st % 3], 256, 0, 256, cb, 256, sq, ps[4], ps[4],
                    [v_y0], war_cast, war_cast, war_mm, sq_col=256)
                if st >= 1:
                    b_bc(st - 1)
                if st >= 2:
                    b_apply(st - 2)
            b_rows(7)
            b_bc(7)
            b_apply(6)
            b_apply(7)
            v_y1_done = st2b[7]["apply"]
            if DEBUG:
                allw = [(sT, sT.n), (sV, sV.n), (sS, sS.n), (sG, sG.n)]
                for name, t, cview in [
                    ("dbg_xn", xn, ("(c p) t -> p c t", TW)),
                    ("dbg_h", hbuf, None),
                    ("dbg_gate", gate, ("(c p) t -> p c t", T)),
                    ("dbg_y1b", y1b, ("(c p) t -> p c t", T)),
                    ("dbg_bu", bu_sb, None),
                ]:
                    dst = dbg[name]
                    if cview is not None:
                        dst = dst.rearrange("(c p) t -> p c t", p=128)
                        srcv = t[:, :].rearrange("p (c t) -> p c t", c=DC)
                    else:
                        srcv = t[:, :]
                    P.op("Y", lambda y, d=dst, s=srcv: y.dma_start(d, s),
                         wait=allw, inc=(dDbg, 16))
            joinv = {}
            for eng, sem in ALLSEMS:
                P.op(eng, lambda e: e.nop(),
                     wait=[(sem, sem.n)] if eng in ("T", "Y") else [],
                     inc=(sem, 1))
                joinv[eng] = sem.n
            P.flush(nc)

        with ExitStack() as S2a:
            e2 = S2a.enter_context
            y2acc = e2(nc.sbuf_tensor("y2acc", [128, DC * T], F32))
            with ExitStack() as S2b:
                e2b = S2b.enter_context
                sgn1B = e2b(nc.sbuf_tensor("sgn1B", [128, DC * 1024], BF16))
                sgn2B = e2b(nc.sbuf_tensor("sgn2B", [128, DC * 1024], BF16))
                gbuf = [e2b(nc.sbuf_tensor(f"gbuf{i}", [128, DC * 512], BF16))
                        for i in range(2)]
                slabC = slabB

                for eng, _ in ALLSEMS:
                    P.op(eng, lambda e: e.nop(),
                         wait=[(osem, joinv[oeng]) for oeng, osem in ALLSEMS
                               if oeng != eng] +
                              ([(dDbg, dDbg.n)] if DEBUG else []))

                v_al2_fin = [0]
                s_b2a = [0]
                sgn1 = [sgn1A, sgn1B]
                sgn2 = [sgn2A, sgn2B]
                q_sign_done = [q0_subs[7]["sign"], None, None, None]
                q_alpha_done = [q0_subs[7]["alpha"], None, None, None]
                al1_ev = [al1_ev0, None, None, None]
                gelu_hist = []
                ev5_hist = []
                for q in range(4):
                    s1q = sgn1[q % 2]
                    s2q = sgn2[q % 2]
                    guard = [(sT, sT.n), (sG, sG.n)]
                    prep_state["A"] = prep_state["A"] + guard + \
                        ([(sV, v_y1_done)] if q == 0 else [])
                    prep_state["B"] = prep_state["B"] + guard + [(sS, sS.n)] + \
                        ([(sV, v_y1_done)] if q == 0 else [])
                    subs = []
                    if q < 3:
                        for s_idx in range(4):
                            subs.append((1, s_idx))
                            subs.append((2, s_idx))
                    sub_i = 0
                    last_sign = None
                    pending_alpha = []
                    prepped = []
                    gelu_tile_done = [None] * 4

                    def emit_m4(tile, s1q=s1q, q=q):
                        for hc in range(8):
                            ghc = q * 8 + hc
                            for c in range(DC):
                                wt = []
                                if c == 0:
                                    if len(gelu_hist) >= 2:
                                        wt.append((sS, gelu_hist[-2]))
                                    if hc == 0 and tile == 0:
                                        wt.append((sS, q_sign_done[q]))
                                        wt.append((sV, v_y1_done))
                                P.op("T", lambda t, c=c, hc=hc, tile=tile: t.matmul(
                                    ps[hc % 2][0:128, 0:512],
                                    s1q[:, c * 1024 + hc * 128:c * 1024 + hc * 128 + 128],
                                    bass.AP(y1b, c * T + tile * 512, [[T * DC, 128], [1, 512]]),
                                    start=(c == 0), stop=(c == DC - 1)),
                                    wait=wt, inc=(sT, 1) if c == DC - 1 else None)
                            t_m4 = sT.n
                            gwt = [(sT, t_m4)]
                            if hc == 0 and tile == 0:
                                gwt.append((sS, al1_ev[q]))
                            ghc = q * 8 + hc
                            P.op("S", lambda s, hc=hc, ghc=ghc, tile=tile: s.activation(
                                gbuf[tile % 2][:, hc * 512:(hc + 1) * 512],
                                ps[hc % 2][0:128, 0:512],
                                AF.Gelu, bias=b1_sb[:, ghc:ghc + 1],
                                scale=al1_sb[:, ghc:ghc + 1]),
                                wait=gwt, inc=(sS, 1))
                            gelu_hist.append(sS.n)
                        gelu_tile_done[tile] = sS.n

                    def emit_m5(tile, s2q=s2q, q=q):
                        for c in range(DC):
                            for hc in range(8):
                                wt = []
                                if hc == 0:
                                    wt = [(sS, gelu_tile_done[tile])]
                                    if len(ev5_hist) >= 2:
                                        wt.append((sV, ev5_hist[-2]))
                                P.op("T", lambda t, c=c, hc=hc, tile=tile: t.matmul(
                                    ps[2 + c % 2][0:128, 0:512],
                                    s2q[:, hc * 1024 + c * 128:hc * 1024 + c * 128 + 128],
                                    gbuf[tile % 2][:, hc * 512:(hc + 1) * 512],
                                    start=(hc == 0), stop=(hc == 7)),
                                    wait=wt, inc=(sT, 1) if hc == 7 else None)
                            t_m5 = sT.n
                            if q == 0:
                                P.op("V", lambda v, c=c, tile=tile: v.tensor_copy(
                                    y2acc[:, c * T + tile * 512:c * T + tile * 512 + 512],
                                    ps[2 + c % 2][0:128, 0:512]),
                                    wait=[(sT, t_m5)], inc=(sV, 1))
                            elif q < 3:
                                P.op("V", lambda v, c=c, tile=tile: v.tensor_add(
                                    y2acc[:, c * T + tile * 512:c * T + tile * 512 + 512],
                                    y2acc[:, c * T + tile * 512:c * T + tile * 512 + 512],
                                    ps[2 + c % 2][0:128, 0:512]),
                                    wait=[(sT, t_m5)], inc=(sV, 1))
                            else:
                                bw = [(sV, s_b2a[0])] if c == 0 else []
                                P.op("V", lambda v, c=c, tile=tile: v.scalar_tensor_tensor(
                                    y2acc[:, c * T + tile * 512:c * T + tile * 512 + 512],
                                    ps[2 + c % 2][0:128, 0:512],
                                    b2a_sb[:, c:c + 1],
                                    y2acc[:, c * T + tile * 512:c * T + tile * 512 + 512],
                                    ALU.add, ALU.add),
                                    wait=[(sT, t_m5)] + bw, inc=(sV, 1))
                            ev5_hist.append(sV.n)

                    if q == 3:
                        P.op("S", lambda s: s.activation(
                            al2_sb[:, :], al2acc[:, :], AF.Identity,
                            scale=cst[:, 3:4]),
                            wait=[(sV, v_al2_fin[0])], inc=(sS, 1))
                        P.op("S", lambda s: s.activation(
                            b2a_sb[:, :], al2_sb[:, :], AF.Ln))
                        P.op("S", lambda s: s.activation(
                            b2a_sb[:, :], b2a_sb[:, :], AF.Exp,
                            scale=cst[:, 4:5]), inc=(sS, 1))
                        P.op("V", lambda v: v.tensor_mul(
                            b2a_sb[:, :], b2a_sb[:, :], b2_sb[:, :]),
                            wait=[(sS, sS.n)], inc=(sV, 1))
                        s_b2a[0] = sV.n
                    for tile in range(4):
                        for th in pending_alpha:
                            th()
                        pending_alpha = []
                        for _ in range(2):
                            if sub_i < len(subs):
                                wnum, s_idx = subs[sub_i]
                                par = sub_i % 2
                                msp = emit_prep_subload(
                                    q + 1, wnum, s_idx, sgn1[(q + 1) % 2],
                                    sgn2[(q + 1) % 2],
                                    ps[6] if (q + 1) % 2 else ps[7],
                                    first_al=(sub_i == 0), last_al=(sub_i == 7),
                                    slabT=[slab, slabC][par], wkey="AB"[par],
                                    dsem=dWp[0] if par == 0 else dWb)
                                last_sign = msp["sign"]
                                prepped.append(msp)
                                pending_alpha.append(msp["alpha_thunk"])
                                sub_i += 1
                        emit_m4(tile)
                        if tile >= 1:
                            emit_m5(tile - 1)
                    for th in pending_alpha:
                        th()
                    pending_alpha = []
                    emit_m5(3)
                    if q < 3:
                        q_sign_done[q + 1] = last_sign
                        q_alpha_done[q + 1] = prepped[-1]["alpha"]
                        ev1, v_al2 = emit_alpha_evacs(
                            q + 1, ps[6] if (q + 1) % 2 else ps[7],
                            [(sT, prepped[-1]["alpha"])], first_acc=False)
                        al1_ev[q + 1] = ev1
                        if q == 2:
                            v_al2_fin[0] = v_al2
                if DEBUG:
                    allw = [(sT, sT.n), (sV, sV.n), (sS, sS.n)]
                    P.op("Y", lambda y: y.dma_start(
                        dbg["dbg_y2acc"].rearrange("(c p) t -> p c t", p=128),
                        y2acc[:, :].rearrange("p (c t) -> p c t", c=DC)),
                        wait=allw, inc=(dDbg, 16))
                    P.op("Y", lambda y: y.dma_start(dbg["dbg_al1"], al1_sb[:, :]),
                         wait=allw, inc=(dDbg, 16))
                    P.op("Y", lambda y: y.dma_start(dbg["dbg_al2"], al2_sb[:, :]),
                         wait=allw, inc=(dDbg, 16))

                joinv2 = {}
                for eng, sem in ALLSEMS:
                    P.op(eng, lambda e: e.nop(),
                         wait=[(sem, sem.n)] if eng in ("T", "Y") else [],
                         inc=(sem, 1))
                    joinv2[eng] = sem.n
                P.flush(nc)

            with ExitStack() as S2c:
                e2c = S2c.enter_context
                y2f = [e2c(nc.sbuf_tensor(f"y2f{i}", [128, DC * 256], F32))
                       for i in range(3)]
                outt = [e2c(nc.sbuf_tensor(f"outt{i}", [128, DC * 256], F32))
                        for i in range(2)]
                cb3 = e2c(nc.sbuf_tensor("cb3", [128, DC * 256], BF16))
                sq3 = e2c(nc.sbuf_tensor("sq3", [128, DC * 128], BF16))
                rows3 = e2c(nc.sbuf_tensor("rows3", [1, 4 * 256], F32))
                ones_f = e2c(nc.sbuf_tensor("ones_f", [1, 128], F32))
                P.op("V", lambda v: v.memset(ones_f[:, :], 1.0))

                for eng, _ in ALLSEMS:
                    P.op(eng, lambda e: e.nop(),
                         wait=[(osem, joinv2[oeng]) for oeng, osem in ALLSEMS
                               if oeng != eng])

                st1c = {}
                st2c = {}
                st_out = {}
                R3 = [i * 256 for i in range(4)]

                def c_rows(st):
                    m = st1c[st]
                    war = [(sT, st1c[st - 1]["bc"])] if st >= 1 else []
                    P.op("V", lambda v: v.tensor_scalar(
                        rows3[:, R3[0]:R3[0] + 256], ps[st % 2][0:1, 0:256],
                        1.0 / D, None, ALU.mult), wait=[(sT, m["mm"])] + war)
                    P.op("V", lambda v: v.tensor_scalar(
                        rows3[:, R3[1]:R3[1] + 256], ps[2 + st % 2][0:1, 0:256],
                        1.0 / D, None, ALU.mult))
                    P.op("V", lambda v: v.tensor_mul(
                        rows3[:, R3[2]:R3[2] + 256], rows3[:, R3[0]:R3[0] + 256],
                        rows3[:, R3[0]:R3[0] + 256]))
                    P.op("V", lambda v: v.tensor_sub(
                        rows3[:, R3[1]:R3[1] + 256], rows3[:, R3[1]:R3[1] + 256],
                        rows3[:, R3[2]:R3[2] + 256]))
                    m["rows"] = sV.n
                    P.op("S", lambda s: s.activation(
                        rows3[:, R3[2]:R3[2] + 256], rows3[:, R3[1]:R3[1] + 256],
                        AF.Ln, bias=cst[0:1, 0:1]), wait=[(sV, m["rows"])])
                    P.op("S", lambda s: s.activation(
                        rows3[:, R3[3]:R3[3] + 256], rows3[:, R3[2]:R3[2] + 256],
                        AF.Exp, scale=cst[0:1, 1:2]))
                    m["rstd"] = sS.n

                def c_bc(st):
                    m = st1c[st]
                    bcp = [ps[4], ps[5]] if st % 2 == 0 else [ps[6], ps[7]]
                    war = [(sV, st2c[st - 2]["apply"])] if st >= 2 else []
                    P.op("T", lambda t, st=st: t.matmul(
                        bcp[0][0:128, 0:256], ones_f[:, :], rows3[0:1, R3[0]:R3[0] + 256],
                        start=True, stop=True), wait=[(sS, m["rstd"])] + war)
                    P.op("T", lambda t, st=st: t.matmul(
                        bcp[1][0:128, 0:256], ones_f[:, :], rows3[0:1, R3[3]:R3[3] + 256],
                        start=True, stop=True), inc=(sT, 1))
                    m["bc"] = sT.n
                    m["bcmu"] = bcp[0]
                    m["bcr"] = bcp[1]

                def c_apply(st):
                    war = [st_out[st - 2]] if st >= 2 else []
                    ot = outt[st % 2]
                    out_ap = bass.AP(ot, 0, [[256 * DC, 128], [256, DC], [1, 256]])
                    m = st1c[st]
                    sview = bass.AP(y2f[st % 3], 0, [[256 * DC, 128], [256, DC], [1, 256]])
                    P.op("V", lambda v: v.tensor_sub(
                        sview, sview,
                        bass.AP(m["bcmu"], 0, [[512, 128], [0, DC], [1, 256]])),
                        wait=[(sT, m["bc"])] + war)
                    P.op("V", lambda v, ot=ot: v.tensor_mul(
                        out_ap, sview,
                        bass.AP(m["bcr"], 0, [[512, 128], [0, DC], [1, 256]])))
                    st2c[st] = {"apply": sV.n}
                    osem = dO if st % 2 == 0 else dO1
                    P.op("Y", lambda y, st=st, ot=ot: y.dma_start(
                        outTd[:, st * 2048:(st + 1) * 2048], ot[:, :]),
                        wait=[(sV, st2c[st]["apply"])], inc=(osem, 16))
                    st_out[st] = (osem, osem.n)

                for st in range(8):
                    yf = y2f[st % 3]
                    if st >= 2:
                        c_apply(st - 2)
                    if st >= 1:
                        c_rows(st - 1)
                    for c in range(DC):
                        swt = []
                        if c == 0 and st >= 3:
                            swt.append((sV, st2c[st - 3]["apply"]))
                        P.op("V", lambda v, c=c, st=st, yf=yf: v.scalar_tensor_tensor(
                            yf[:, c * 256:(c + 1) * 256],
                            y2acc[:, c * T + st * 256:c * T + st * 256 + 256],
                            al2_sb[:, c:c + 1],
                            bass.AP(y1b, c * T + st * 256, [[T * DC, 128], [1, 256]]),
                            ALU.mult, ALU.add),
                            wait=swt, inc=(sV, 1) if c == DC - 1 else None)
                    v_y2f = sV.n
                    if st >= 1:
                        c_bc(st - 1)
                    war_cast = [(sT, st1c[st - 1]["mm"])] if st >= 1 else []
                    war_mm = ([(sV, st1c[st - 1]["rows"]), (sS, st1c[st - 1]["rstd"])]
                              if st >= 1 else [])
                    st1c[st] = ln_stage1(
                        yf, 256, 0, 256, cb3, 256, sq3,
                        ps[st % 2], ps[2 + st % 2],
                        [(sV, v_y2f)], war_cast, war_cast, war_mm)
                c_rows(7)
                c_bc(7)
                c_apply(6)
                c_apply(7)
                P.op("Y", lambda y: y.nop(),
                     wait=[(dO, dO.n), (dO1, dO1.n)] +
                          ([(dDbg, dDbg.n)] if DEBUG else []))
                P.flush(nc)

    return nc


def kernel(**inputs):
    x = inputs["L0_attn_x"]
    f = np.float32
    wgT = np.asarray(inputs["L0_attn_Wg"]).T.astype(f)
    wg_host = np.ascontiguousarray(
        wgT.reshape(DC, 128, DC, 128).transpose(1, 2, 0, 3).reshape(128, DC * D))
    w1T = np.asarray(inputs["L0_ffn_w1"]).T.astype(f)
    w1_host = np.ascontiguousarray(
        w1T.reshape(DC, 128, 16, 256).transpose(1, 2, 0, 3).reshape(128, 16 * 2048))
    w2T = np.asarray(inputs["L0_ffn_w2"]).T.astype(f)
    w2_host = np.ascontiguousarray(
        w2T.reshape(32, 128, 1024).transpose(1, 0, 2).reshape(128, 32 * 1024))
    common = {
        "wsT": np.ascontiguousarray(inputs["L0_attn_Ws"].T, dtype=f),
        "wiT": np.ascontiguousarray(inputs["L0_attn_Wi"].T, dtype=f),
        "a_t": np.ascontiguousarray(inputs["L0_attn_A"].T, dtype=f),
        "i16": np.eye(NST, dtype=f),
        "woT": np.ascontiguousarray(inputs["L0_attn_Wo"].T, dtype=f),
        "wgT": wg_host,
        "w1T": w1_host,
        "w2T": w2_host,
        "lnw8": np.ascontiguousarray(np.asarray(inputs["L0_attn_ln_w"]).reshape(DC, 128).T, dtype=f),
        "lnb8": np.ascontiguousarray(np.asarray(inputs["L0_attn_ln_b"]).reshape(DC, 128).T, dtype=f),
        "bs_c": np.ascontiguousarray(np.asarray(inputs["L0_attn_bs"]).reshape(NST, 1), dtype=f),
        "bi_c": np.ascontiguousarray(np.asarray(inputs["L0_attn_bi"]).reshape(NST, 1), dtype=f),
        "bo8": np.ascontiguousarray(np.asarray(inputs["L0_attn_bo"]).reshape(DC, 128).T, dtype=f),
        "bg8": np.ascontiguousarray(np.asarray(inputs["L0_attn_bg"]).reshape(DC, 128).T, dtype=f),
        "b18": np.ascontiguousarray(np.asarray(inputs["L0_ffn_b1"]).reshape(32, 128).T, dtype=f),
        "b28": np.ascontiguousarray(np.asarray(inputs["L0_ffn_b2"]).reshape(DC, 128).T, dtype=f),
    }
    x = np.asarray(x)
    in_maps = []
    for core in range(8):
        b, q = core // 4, core % 4
        start = q * T
        xcore = np.zeros((D, TW), dtype=f)
        xcore[:, KPRE:] = x[b, start:start + T, :].T
        if q > 0:
            xcore[:, :KPRE] = x[b, start - KPRE:start, :].T
        x8 = xcore.reshape(DC, 128, TW)
        blocks = [x8[:, :, 0:KPRE].transpose(1, 0, 2).reshape(128, DC * KPRE)]
        for g in range(8):
            blocks.append(x8[:, :, KPRE + 256 * g:KPRE + 256 * (g + 1)]
                          .transpose(1, 0, 2).reshape(128, DC * 256))
        m = dict(common)
        m["xh"] = np.ascontiguousarray(np.concatenate(blocks, axis=1))
        m["maskh"] = (np.zeros if q == 0 else np.ones)((NST, KPRE), dtype=f)
        in_maps.append(m)

    nc = build_program()
    res = run_bass_kernel_spmd(nc, in_maps, core_ids=list(range(8)))

    out = np.empty((2, 8192, D), dtype=np.float32)
    for core in range(8):
        b, q = core // 4, core % 4
        o = res.results[core]["outT"].reshape(128, 8, DC, 256)
        out[b, q * T:(q + 1) * T, :] = o.transpose(1, 3, 2, 0).reshape(T, D)
    return out
